# version 03
# baseline (speedup 1.0000x reference)
import numpy as np
import ml_dtypes
import concourse.bass as bass
import concourse.mybir as mybir
from concourse.bass_utils import run_bass_kernel_spmd

F32 = mybir.dt.float32
BF16 = mybir.dt.bfloat16
I32 = mybir.dt.int32
AF = mybir.ActivationFunctionType
ALU = mybir.AluOpType
BF = ml_dtypes.bfloat16

D = 4096
SEQ = 4096
BATCH = 2
NC = 8
TOK = 1024
TT = 512
NKC = 32
NEG = -1e30
EPS = 1e-5
ISQ = 1.0 / np.sqrt(128.0)


def _chunks_of_core(c):
    j = c % 4
    return (j, 7 - j)


def _rows_of_core(c):
    s = c // 4
    a, b = _chunks_of_core(c)
    r = []
    for g in (a, b):
        st = s * SEQ + g * TT
        r.append(np.arange(st, st + TT))
    return np.concatenate(r)


def _positions_of_core(c):
    a, b = _chunks_of_core(c)
    return np.concatenate([np.arange(a * TT, a * TT + TT),
                           np.arange(b * TT, b * TT + TT)])


def _rank_of_chunk(s, g):
    j = g if g < 4 else 7 - g
    return 4 * s + j, (0 if g < 4 else 1)


def _rope_tables():
    inv = 1.0 / (10000.0 ** (np.arange(0, 64, 2, dtype=np.float64) / 64.0))
    pos = np.arange(SEQ, dtype=np.float64)
    fr = pos[:, None] * inv[None, :]
    return np.cos(fr).astype(np.float32), np.sin(fr).astype(np.float32)



ENGS = ("sync", "vector", "scalar", "tensor", "gpsimd")


class Prog:

    def __init__(self, nc):
        self.nc = nc
        self.ops = {e: [] for e in ENGS}
        self.sems = {}
        self.cnt = {}

    def sem(self, name):
        if name not in self.sems:
            self.sems[name] = self.nc.alloc_semaphore(name)
            self.cnt[name] = 0
        return self.sems[name]

    def run(self, eng, fn, inc=None, by=1):
        val = None
        sem = None
        if inc is not None:
            sem = self.sem(inc)
            self.cnt[inc] += by
            val = self.cnt[inc]
        def closure(e, fn=fn, sem=sem, by=by):
            ins = fn(e)
            if sem is not None:
                ins.then_inc(sem, by)
        self.ops[eng].append(closure)
        return val

    def wait(self, eng, name, val):
        if val is None or val <= 0:
            return
        sem = self.sem(name)
        self.ops[eng].append(lambda e, s=sem, v=val: e.wait_ge(s, v))

    def now(self, name):
        self.sem(name)
        return self.cnt[name]

    def fence(self, eng, name):
        self.wait(eng, name, self.now(name))

    def barrier(self):
        for name, v in list(self.cnt.items()):
            if v > 0:
                for e in ENGS:
                    self.wait(e, name, v)

    def finish(self):
        with self.nc.Block() as block:
            for e in ENGS:
                fns = self.ops[e]
                def body(engine, fns=fns):
                    for f in fns:
                        f(engine)
                getattr(block, e)(body)


class Gate:

    def __init__(self, n):
        self.g = [[] for _ in range(n)]

    def acquire(self, P, eng, i):
        engs = eng if isinstance(eng, (list, tuple)) else [eng]
        for name, val in self.g[i]:
            for en in engs:
                P.wait(en, name, val)
        self.g[i] = []

    def release(self, i, name, val):
        self.g[i].append((name, val))



class MK:

    def __init__(self, mode_tensors, fused=True):
        nc = bass.Bass("TRN2", target_bir_lowering=False,
                       dynamic_dma_scratch_size=1024)
        self.nc = nc
        self.P = Prog(nc)
        self.fused = fused
        self.dram = {}
        self.t = mode_tensors
        for name, (shape, dt, kind) in mode_tensors.items():
            self.dram[name] = nc.dram_tensor(name, list(shape), dt, kind=kind)
        self._alloc_consts()
        self._alloc_sbuf()
        self._alloc_psum()

    def sd(self, name, L):
        return self.dram[f"{name}{L}"] if self.fused else self.dram[name]

    def _alloc_consts(self):
        nc = self.nc
        self.c_ident = nc.inline_tensor(np.eye(128, dtype=BF), "c_ident")
        self.c_ones_col = nc.inline_tensor(np.ones((128, 1), dtype=BF), "c_ones_col")
        self.c_ones_row = nc.inline_tensor(np.ones((1, 128), dtype=BF), "c_ones_row")
        self.c_ones_row_f = nc.inline_tensor(np.ones((1, 128), dtype=np.float32), "c_ones_row_f")
        self.c_one1_f = nc.inline_tensor(np.ones((1, 1), dtype=np.float32), "c_one1_f")
        mm = np.full((128, 896), NEG, dtype=np.float32)
        ii = np.arange(128)[:, None]
        cc = np.arange(896)[None, :]
        mm[ii <= cc - 384] = 0.0
        self.c_mask = nc.inline_tensor(mm, "c_mask")
        ex = np.zeros((8, 128), dtype=BF)
        for e in range(8):
            ex[e, e * 16:(e + 1) * 16] = 1.0
        self.c_expand = nc.inline_tensor(ex, "c_expand")

    def _alloc_sbuf(self):
        nc = self.nc
        self.arena = nc.alloc_sbuf_tensor("arena", [128, 227840], mybir.dt.uint8)
        a = [nc.lookup_mloc(self.arena).addr]
        self.arena_base = a[0]

        def at(name, shape, dt, align=32):
            a[0] = (a[0] + align - 1) // align * align
            h = nc.alloc_sbuf_tensor_at(name, list(shape), dt, offset=a[0])
            per = int(np.prod(shape[1:])) * mybir.dt.size(dt)
            a[0] += per
            return h

        self.x_res = at("x_res", [128, NKC * TOK], F32)
        self.reg_base = a[0]
        self.xbf = at("xbf", [128, 2 * NKC * TT], BF16)
        rb = self.reg_base
        self.kv_k = nc.alloc_sbuf_tensor_at("kv_k", [128, 36 * 128], BF16, offset=rb)
        self.kv_v = nc.alloc_sbuf_tensor_at("kv_v", [128, 36 * 132], BF16, offset=rb + 9216)
        self.fl_pr = nc.alloc_sbuf_tensor_at("fl_pr", [128, 4 * 512], BF16, offset=rb + 18752)
        self.fl_ct = nc.alloc_sbuf_tensor_at("fl_ct", [128, 512], BF16, offset=rb + 22848)
        self.mhat32 = nc.alloc_sbuf_tensor_at("mhat32", [128, 32], F32, offset=rb + 23872)
        self.fl_q = nc.alloc_sbuf_tensor_at("fl_q", [128, 32 * 512], BF16, offset=rb + 32768)
        self.p2_y2 = nc.alloc_sbuf_tensor_at("p2_y2", [128, 2 * 512], BF16, offset=rb + 32768)
        self.p2_i32 = nc.alloc_sbuf_tensor_at("p2_i32", [128, 512], I32, offset=rb + 34816)
        self.p2_s0 = nc.alloc_sbuf_tensor_at("p2_s0", [128, 512], F32, offset=rb + 36864)
        self.p2_s1 = nc.alloc_sbuf_tensor_at("p2_s1", [128, 512], F32, offset=rb + 38912)
        self.p2_a = nc.alloc_sbuf_tensor_at("p2_a", [128, 512], BF16, offset=rb + 40960)
        self.p2_b = nc.alloc_sbuf_tensor_at("p2_b", [128, 512], BF16, offset=rb + 41984)

        self.w_bf0 = at("w_bf0", [128, 2048], BF16)
        self.w_bf1 = at("w_bf1", [128, 2048], BF16)
        self.w_st = at("w_st", [128, 512], F32)
        self.bc0 = at("bc0", [128, 512], F32)
        self.bc1 = at("bc1", [128, 512], F32)
        self.rot_off = (a[0] + 31) // 32 * 32
        self.rot = at("rot", [128, 4 * 512], BF16)
        self.scr = at("scr", [128, 512], F32)
        self.scr2 = at("scr2", [128, 512], F32)
        self.bc2 = self.scr2
        self.hw_sb = self.rot
        self.rows = at("rows", [128, 512], F32)
        self.rcol = at("rcol", [128, 16], F32)
        self.sb_ident = at("sb_ident", [128, 128], BF16)
        self.sb_ones_col = at("sb_ones_col", [128, 1], BF16)
        self.sb_ones_row = at("sb_ones_row", [1, 128], BF16)
        self.sb_ones_row_f = at("sb_ones_row_f", [1, 128], F32)
        self.sb_one1_f = at("sb_one1_f", [1, 1], F32)
        self.sb_expand = at("sb_expand", [8, 128], BF16)
        self.sb_rope = at("sb_rope", [128, 2 * 256], F32)
        self.vec = at("vec", [128, 7 * 32], F32)
        self.rowbuf = at("rowbuf", [1, 128], F32)
        self.rowbuf_bf = at("rowbuf_bf", [1, 128], BF16)
        self.bb_sb = at("bb_sb", [128, 128], F32)
        self.qn = at("qn", [1, 64], F32)
        self.route_T = nc.alloc_sbuf_tensor_at("route_T", [8, 512], BF16, offset=self.rot_off + 2048)
        self.wg_sb = at("wg_sb", [128, 256], BF16)
        self.lg_sb = at("lg_sb", [128, 8], F32)
        self.small = at("small", [128, 16], F32)
        self.sb_mask = nc.alloc_sbuf_tensor_at("sb_mask", [128, 896], F32, offset=self.reg_base + 24064)
        self.vones = nc.alloc_sbuf_tensor_at("vones", [128, 64], F32, offset=self.reg_base + 27680)
        self.limit = a[0]
        assert a[0] <= self.arena_base + 227840, f"SBUF overflow: {a[0]}"

    def _alloc_psum(self):
        nc = self.nc
        self.ps = [nc.alloc_psum_tensor(f"ps{i}", [128, 512], F32) for i in range(6)]
        self.ps_tp = nc.alloc_psum_tensor("ps_tp", [128, 1024], BF16)
        self.ps_x = nc.alloc_psum_tensor("ps_x", [128, 512], F32)

    def load_consts(self):
        P = self.P
        pairs = [(self.sb_ident, self.c_ident),
                 (self.sb_ones_col, self.c_ones_col), (self.sb_ones_row, self.c_ones_row),
                 (self.sb_ones_row_f, self.c_ones_row_f), (self.sb_one1_f, self.c_one1_f),
                 (self.sb_expand, self.c_expand)]
        for sb, cc in pairs:
            P.run("sync", lambda e, sb=sb, cc=cc: e.dma_start(sb[:, :], cc[:, :]), inc="dconst", by=16)
        if "rope_cos" in self.dram:
            for i, nm in enumerate(("rope_cos", "rope_sin")):
                dr = self.dram[nm]
                P.run("sync", lambda e, i=i, dr=dr: e.dma_start(
                    self.sb_rope[:, i * 256:(i + 1) * 256].rearrange("p (ts f) -> p ts f", ts=8),
                    dr[:, :].rearrange("(ts p) f -> p ts f", p=128)), inc="dconst", by=16)

    def load_x(self):
        P = self.P
        x = self.dram["x_own"]
        for kc in range(NKC):
            P.run("sync", lambda e, kc=kc: e.dma_start(
                self.x_res[:, kc * TOK:(kc + 1) * TOK],
                x[kc * 128:(kc + 1) * 128, :]), inc="dconst", by=16)

    def store_x(self):
        P = self.P
        x = self.dram["x_out"]
        v = P.now("dve")
        P.wait("sync", "dve", v)
        for kc in range(NKC):
            P.run("sync", lambda e, kc=kc: e.dma_start(
                x[kc * 128:(kc + 1) * 128, :],
                self.x_res[:, kc * TOK:(kc + 1) * TOK]), inc="dout", by=16)
        P.wait("sync", "dout", P.now("dout"))

    def preload_vectors(self, phases):
        P = self.P
        if any(p.startswith("ffn0") for p in phases):
            for i, nm in enumerate(("f0_bias", "f0_scale", "f0_gamma", "f0_beta")):
                P.run("sync", lambda e, i=i, nm=nm: e.dma_start(
                    self.vec[:, i * 32:(i + 1) * 32], self.dram[nm][:, :]), inc="dconst", by=16)
            P.run("sync", lambda e: e.dma_start(self.small[0:1, 0:1], self.dram["f0_qs"][:, :]), inc="dconst", by=16)
        if any(p.startswith("ffn1") for p in phases):
            P.run("sync", lambda e: e.dma_start(
                self.vec[:, 6 * 32:7 * 32], self.dram["f1_bbase"][:, :]), inc="dconst", by=16)
        vtot = P.now("dconst")
        for eng in ("vector", "scalar", "tensor", "gpsimd", "sync"):
            P.wait(eng, "dconst", vtot)
        if any(p.startswith("ffn0") for p in phases):
            v = P.run("vector", lambda e: e.reciprocal(self.small[0:1, 1:2], self.small[0:1, 0:1]), inc="dve")
            P.wait("tensor", "dve", v)
            v = P.run("tensor", lambda e: e.matmul(self.ps_x[:, 0:1], self.sb_ones_row_f[:1, :], self.small[0:1, 1:2], start=True, stop=True), inc="pe")
            P.wait("vector", "pe", v)
            v = P.run("vector", lambda e: e.tensor_copy(self.small[:, 2:3], self.ps_x[:, 0:1]), inc="dve")
            P.wait("vector", "dve", v)
            v = P.run("vector", lambda e: e.tensor_scalar(self.vec[:, 4 * 32:5 * 32], self.vec[:, 2 * 32:3 * 32], self.small[:, 2:3], None, ALU.mult), inc="dve")
            v = P.run("vector", lambda e: e.tensor_scalar(self.vec[:, 5 * 32:6 * 32], self.vec[:, 3 * 32:4 * 32], self.small[:, 2:3], None, ALU.mult), inc="dve")
            P.wait("scalar", "dve", v)

    def stats_cast(self, tile, want_rcol=False, kscale=False):
        P = self.P
        if self.resid_ready is not None:
            nm, vv = self.resid_ready
            P.wait("scalar", nm, vv)
        ps_su = self.ps[2 + 2 * tile]
        ps_sq = self.ps[3 + 2 * tile]
        rslot = tile * 32
        for kc in range(NKC):
            xa = self.x_res[:, kc * TOK + tile * TT: kc * TOK + tile * TT + TT]
            r0 = self.rot[:, (2 * kc % 4) * 512:(2 * kc % 4) * 512 + 512]
            r1 = self.rot[:, (2 * kc % 4 + 1) * 512:(2 * kc % 4 + 1) * 512 + 512]
            self.rot_gate.acquire(P, "scalar", 2 * kc % 4)
            self.rot_gate.acquire(P, "scalar", (2 * kc % 4) + 1)
            P.run("scalar", lambda e, xa=xa, r0=r0: e.copy(r0[:, :], xa), inc="act")
            va = P.run("scalar", lambda e, xa=xa, r1=r1: e.activation(r1[:, :], xa, AF.Square), inc="act")
            P.wait("tensor", "act", va)
            P.run("tensor", lambda e, r0=r0, ps_su=ps_su, kc=kc: e.matmul(
                ps_su[0:1, :], self.sb_ones_col[:, :], r0[:, :],
                start=(kc == 0), stop=(kc == NKC - 1)), inc="pe")
            vm = P.run("tensor", lambda e, r1=r1, ps_sq=ps_sq, kc=kc: e.matmul(
                ps_sq[0:1, :], self.sb_ones_col[:, :], r1[:, :],
                start=(kc == 0), stop=(kc == NKC - 1)), inc="pe")
            self.rot_gate.release(2 * kc % 4, "pe", vm)
            self.rot_gate.release(2 * kc % 4 + 1, "pe", vm)
        vpe = P.now("pe")
        P.wait("vector", "pe", vpe)
        rows = self.rows
        srow = self.scr
        P.fence("vector", "dve")
        v1 = P.run("vector", lambda e: e.tensor_scalar(rows[0:1, :], ps_su[0:1, :], 1.0 / D, None, ALU.mult), inc="dve")
        v2 = P.run("vector", lambda e: e.tensor_scalar(srow[0:1, :], ps_sq[0:1, :], 1.0 / D, EPS, ALU.mult, ALU.add), inc="dve")
        P.wait("tensor", "dve", v1)
        v7 = P.run("tensor", lambda e: e.matmul(self.ps_x[:, 0:512], self.sb_ones_row_f[:1, :], rows[0:1, :], start=True, stop=True), inc="pe")
        P.wait("vector", "pe", v7)
        v8 = P.run("vector", lambda e: e.tensor_copy(self.bc2[:, :], self.ps_x[:, 0:512]), inc="dve")
        P.wait("vector", "dve", v8)
        v3 = P.run("vector", lambda e: e.tensor_tensor(rows[0:1, :], rows[0:1, :], rows[0:1, :], ALU.mult), inc="dve")
        P.wait("vector", "dve", v3)
        v4 = P.run("vector", lambda e: e.tensor_tensor(rows[0:1, :], srow[0:1, :], rows[0:1, :], ALU.subtract), inc="dve")
        P.wait("scalar", "dve", v4)
        v5 = P.run("scalar", lambda e: e.activation(srow[0:1, :], rows[0:1, :], AF.Sqrt), inc="act")
        P.wait("vector", "act", v5)
        v6 = P.run("vector", lambda e: e.reciprocal(rows[0:1, :], srow[0:1, :]), inc="dve")
        bcr = (self.bc0, self.bc1)[tile]
        P.wait("tensor", "dve", v6)
        v9 = P.run("tensor", lambda e: e.matmul(self.ps_x[:, 0:512], self.sb_ones_row_f[:1, :], rows[0:1, :], start=True, stop=True), inc="pe")
        P.wait("vector", "pe", v9)
        va = P.run("vector", lambda e, bcr=bcr: e.tensor_copy(bcr[:, :], self.ps_x[:, 0:512]), inc="dve")
        if want_rcol:
            P.wait("tensor", "dve", va)
            for ts_ in range(4):
                vt = P.run("tensor", lambda e, ts_=ts_: e.transpose(
                    self.ps_x[:, 0:1],
                    rows[0:1, ts_ * 128:(ts_ + 1) * 128],
                    self.sb_one1_f[:1, :1]), inc="pe")
                P.wait("vector", "pe", vt)
                col = tile * 4 + ts_
                vc = P.run("vector", lambda e, col=col: e.tensor_copy(self.rcol[:, col:col + 1], self.ps_x[:, 0:1]), inc="dve")
                if kscale:
                    P.wait("vector", "dve", vc)
                    P.run("vector", lambda e, col=col: e.tensor_scalar(
                        self.rcol[:, 8 + col:9 + col], self.rcol[:, col:col + 1], ISQ, None, ALU.mult), inc="dve")
                P.wait("tensor", "dve", P.now("dve"))
        P.wait("vector", "dve", va)
        last = None
        for kc in range(NKC):
            xa = self.x_res[:, kc * TOK + tile * TT: kc * TOK + tile * TT + TT]
            xb = self.xbf[:, (tile * NKC + kc) * TT:(tile * NKC + kc) * TT + TT]
            last = P.run("vector", lambda e, xa=xa, xb=xb: e.tensor_tensor(xb, xa, self.bc2[:, :], ALU.subtract), inc="dve")
        self.cast_done = last

    def stream_w(self, dr, o, half, buf, eng_cast="scalar", cols=128, cast_scale=None):
        P = self.P
        vlast = None
        for q in range(2):
            st = self.w_st[:, (q % 2) * 512:(q % 2) * 512 + 512]
            if self.wst_gate[q] is not None:
                nm, vv = self.wst_gate[q]
                P.wait("sync", nm, vv)
                self.wst_gate[q] = None
            rows0 = half * 2048 + q * 1024
            vd = P.run("sync", lambda e, dr=dr, o=o, rows0=rows0, st=st, cols=cols: e.dma_start(
                st.rearrange("p (kc o) -> p kc o", o=cols) if cols != 512 else st,
                dr[rows0:rows0 + 1024, o * cols:(o + 1) * cols].rearrange("(kc p) o -> p kc o", p=128) if cols != 512 else
                dr[rows0:rows0 + 1024, o * cols:(o + 1) * cols].rearrange("(kc p) o -> p (kc o)", p=128)), inc="dsync", by=16)
            eng = eng_cast
            P.wait(eng, "dsync", vd)
            dst = buf[:, q * 1024:(q + 1) * 1024]
            if cast_scale is None:
                if eng == "scalar":
                    vlast = P.run(eng, lambda e, dst=dst, st=st: e.copy(dst, st), inc="act")
                else:
                    vlast = P.run(eng, lambda e, dst=dst, st=st: e.tensor_copy(dst, st), inc="dve" if eng == "vector" else "pool")
            else:
                vlast = P.run(eng, lambda e, dst=dst, st=st: e.activation(dst, st, AF.Copy, scale=cast_scale), inc="act")
            semn = {"scalar": "act", "vector": "dve", "gpsimd": "pool"}[eng]
            self.wst_gate[q] = (semn, vlast)
        return vlast

    def ffn0(self):
        P = self.P
        nc = self.nc
        self.rot_gate = Gate(4)
        self.wst_gate = [None, None]
        self.stats_cast(0)
        self.stats_cast(1)
        castA_castB_done = self.cast_done
        tern = self.dram["tern"]
        y2d = self.dram["y2"]

        P.wait("tensor", "dve", castA_castB_done)
        wbufs = (self.w_bf0, self.w_bf1)
        self.rot_gate = Gate(4)
        nmm = 0
        for o in range(NKC):
            for half in range(2):
                buf = wbufs[half]
                if self.wbf_gate[half] is not None:
                    nm, vv = self.wbf_gate[half]
                    P.wait("sync", nm, vv)
                    self.wbf_gate[half] = None
                rows0 = half * 2048
                P.run("sync", lambda e, rows0=rows0, o=o, buf=buf: e.dma_start(
                    buf[:, :].rearrange("p (kc o) -> p kc o", o=128),
                    tern[rows0:rows0 + 2048, o * 128:(o + 1) * 128].rearrange("(kc p) o -> p kc o", p=128)), inc="dwt", by=16)
            vw = P.now("dwt")
            for ti in range(2):
                bank = self.ps[nmm % 2]
                P.wait("tensor", "dwt", vw)
                if self.bank_gate[nmm % 2] is not None:
                    nm, vv = self.bank_gate[nmm % 2]
                    P.wait("tensor", nm, vv)
                    self.bank_gate[nmm % 2] = None
                vmm = None
                for kc in range(NKC):
                    w = wbufs[kc // 16][:, (kc % 16) * 128:(kc % 16) * 128 + 128]
                    xb = self.xbf[:, (ti * NKC + kc) * TT:(ti * NKC + kc) * TT + TT]
                    vmm = P.run("tensor", lambda e, bank=bank, w=w, xb=xb, kc=kc: e.matmul(
                        bank[:, :], w, xb, start=(kc == 0), stop=(kc == NKC - 1)),
                        inc="pe" if kc == NKC - 1 else None)
                self.wbf_gate[0] = ("pe", vmm)
                self.wbf_gate[1] = ("pe", vmm)
                P.wait("vector", "pe", vmm)
                bcr = (self.bc0, self.bc1)[ti]
                ri = nmm % 4
                self.rot_gate.acquire(P, "vector", ri)
                rt = self.rot[:, ri * 512:(ri + 1) * 512]
                scr = self.scr
                P.fence("vector", "dve")
                v1 = P.run("vector", lambda e, bank=bank, bcr=bcr, scr=scr: e.tensor_tensor(scr[:, :], bank[:, :], bcr[:, :], ALU.mult), inc="dve")
                self.bank_gate[nmm % 2] = ("dve", v1)
                P.wait("vector", "dve", v1)
                v2 = P.run("vector", lambda e, scr=scr, rt=rt, o=o: e.tensor_scalar(
                    rt, scr[:, :], self.vec[:, 0 * 32 + o:0 * 32 + o + 1], self.vec[:, 1 * 32 + o:1 * 32 + o + 1],
                    ALU.add, ALU.mult), inc="dve")
                P.wait("scalar", "dve", v2)
                ri2 = (ri + 2) % 4
                self.rot_gate.acquire(P, "scalar", ri2)
                rt2 = self.rot[:, ri2 * 512:(ri2 + 1) * 512]
                v3 = P.run("scalar", lambda e, rt=rt, rt2=rt2: e.activation(rt2, rt, AF.Square), inc="act")
                ps_su = self.ps[2 + 2 * ti]
                ps_sq = self.ps[3 + 2 * ti]
                P.wait("tensor", "dve", v2)
                P.run("tensor", lambda e, ps_su=ps_su, rt=rt, o=o: e.matmul(
                    ps_su[0:1, :], self.sb_ones_col[:, :], rt, start=(o == 0), stop=(o == NKC - 1)), inc="pe")
                P.wait("tensor", "act", v3)
                vst = P.run("tensor", lambda e, ps_sq=ps_sq, rt2=rt2, o=o: e.matmul(
                    ps_sq[0:1, :], self.sb_ones_col[:, :], rt2, start=(o == 0), stop=(o == NKC - 1)), inc="pe")
                self.rot_gate.release(ri2, "pe", vst)
                vd = P.run("scalar", lambda e, rt=rt, o=o, ti=ti: e.dma_start(
                    y2d[o * 128:(o + 1) * 128, ti * TT:(ti + 1) * TT], rt), inc=f"dr{ri}", by=16)
                self.rot_gate.release(ri, f"dr{ri}", vd)
                self.rot_gate.release(ri, "pe", vst)
                self.rot_gate.release(ri, "act", v3)
                nmm += 1

        for ti in range(2):
            ps_su = self.ps[2 + 2 * ti]
            ps_sq = self.ps[3 + 2 * ti]
            rows = self.rows
            srow = self.scr
            vpe = P.now("pe")
            P.wait("vector", "pe", vpe)
            P.fence("vector", "dve")
            v1 = P.run("vector", lambda e, ps_su=ps_su: e.tensor_scalar(rows[0:1, :], ps_su[0:1, :], 1.0 / D, None, ALU.mult), inc="dve")
            v2 = P.run("vector", lambda e, ps_sq=ps_sq: e.tensor_scalar(srow[0:1, :], ps_sq[0:1, :], 1.0 / D, EPS, ALU.mult, ALU.add), inc="dve")
            P.wait("tensor", "dve", v1)
            v7 = P.run("tensor", lambda e: e.matmul(self.ps_x[:, 0:512], self.sb_ones_row_f[:1, :], rows[0:1, :], start=True, stop=True), inc="pe")
            P.wait("vector", "pe", v7)
            v8 = P.run("vector", lambda e: e.tensor_copy(self.bc0[:, :], self.ps_x[:, 0:512]), inc="dve")
            P.wait("vector", "dve", v8)
            v3 = P.run("vector", lambda e: e.tensor_tensor(rows[0:1, :], rows[0:1, :], rows[0:1, :], ALU.mult), inc="dve")
            P.wait("vector", "dve", v3)
            v4 = P.run("vector", lambda e: e.tensor_tensor(rows[0:1, :], srow[0:1, :], rows[0:1, :], ALU.subtract), inc="dve")
            P.wait("scalar", "dve", v4)
            v5 = P.run("scalar", lambda e: e.activation(srow[0:1, :], rows[0:1, :], AF.Sqrt), inc="act")
            P.wait("vector", "act", v5)
            v6 = P.run("vector", lambda e: e.reciprocal(rows[0:1, :], srow[0:1, :]), inc="dve")
            P.wait("tensor", "dve", v6)
            v9 = P.run("tensor", lambda e: e.matmul(self.ps_x[:, 0:512], self.sb_ones_row_f[:1, :], rows[0:1, :], start=True, stop=True), inc="pe")
            P.wait("vector", "pe", v9)
            va = P.run("vector", lambda e: e.tensor_copy(self.bc1[:, :], self.ps_x[:, 0:512]), inc="dve")
            P.wait("vector", "dve", va)
            if ti == 0:
                for rr in range(4):
                    P.wait("sync", f"dr{rr}", P.now(f"dr{rr}"))
            for o in range(NKC):
                yb = self.p2_y2[:, (o % 2) * 512:(o % 2) * 512 + 512]
                if self.p2_gate[o % 2] is not None:
                    nm, vv = self.p2_gate[o % 2]
                    P.wait("sync", nm, vv)
                    self.p2_gate[o % 2] = None
                vl = P.run("sync", lambda e, yb=yb, o=o, ti=ti: e.dma_start(
                    yb, y2d[o * 128:(o + 1) * 128, ti * TT:(ti + 1) * TT]), inc=f"dy{o % 2}", by=16)
                P.wait("vector", f"dy{o % 2}", vl)
                P.fence("vector", "dve")
                v1 = P.run("vector", lambda e, yb=yb: e.tensor_tensor(self.p2_s0[:, :], yb, self.bc0[:, :], ALU.subtract), inc="dve")
                P.wait("vector", "dve", v1)
                v2 = P.run("vector", lambda e: e.tensor_tensor(self.p2_s0[:, :], self.p2_s0[:, :], self.bc1[:, :], ALU.mult), inc="dve")
                P.wait("scalar", "dve", v2)
                v3 = P.run("scalar", lambda e, o=o: e.activation(
                    self.p2_s1[:, :], self.p2_s0[:, :], AF.Tanh,
                    bias=self.vec[:, 5 * 32 + o:5 * 32 + o + 1], scale=self.vec[:, 4 * 32 + o:4 * 32 + o + 1]), inc="act")
                P.wait("vector", "act", v3)
                v4 = P.run("vector", lambda e: e.tensor_scalar(self.p2_i32[:, :], self.p2_s1[:, :], 127.0, None, ALU.mult), inc="dve")
                P.wait("vector", "dve", v4)
                v5 = P.run("vector", lambda e: e.tensor_copy(self.p2_s0[:, :], self.p2_i32[:, :]), inc="dve")
                P.wait("vector", "dve", v5)
                v6 = P.run("vector", lambda e: e.scalar_tensor_tensor(self.p2_s1[:, :], self.p2_s1[:, :], 127.0, self.p2_s0[:, :], ALU.mult, ALU.subtract), inc="dve")
                P.wait("vector", "dve", v6)
                v7 = P.run("vector", lambda e: e.tensor_single_scalar(self.p2_a[:, :], self.p2_s1[:, :], 0.5, ALU.is_gt), inc="dve")
                v8 = P.run("vector", lambda e: e.tensor_single_scalar(self.p2_b[:, :], self.p2_s1[:, :], -0.5, ALU.is_lt), inc="dve")
                P.wait("vector", "dve", v8)
                v9 = P.run("vector", lambda e: e.tensor_tensor(self.p2_s1[:, :], self.p2_a[:, :], self.p2_b[:, :], ALU.subtract), inc="dve")
                P.wait("vector", "dve", v9)
                va2 = P.run("vector", lambda e: e.tensor_tensor(self.p2_s0[:, :], self.p2_s0[:, :], self.p2_s1[:, :], ALU.add), inc="dve")
                P.wait("vector", "dve", va2)
                xa = self.x_res[:, o * TOK + ti * TT: o * TOK + ti * TT + TT]
                vb = P.run("vector", lambda e, xa=xa: e.scalar_tensor_tensor(xa, self.p2_s0[:, :], 0.5 / 127.0, xa, ALU.mult, ALU.add), inc="dve")
                self.p2_gate[o % 2] = ("dve", vb)
        self.resid_ready = ("dve", P.now("dve"))


    def load_wslice(self, drname, o, ocols=128):
        P = self.P
        dr = self.dram[drname]
        for half in range(2):
            buf = (self.w_bf0, self.w_bf1)[half]
            if self.wbf_gate[half] is not None:
                nm, vv = self.wbf_gate[half]
                P.wait("sync", nm, vv)
                self.wbf_gate[half] = None
            rows0 = half * 2048
            P.run("sync", lambda e, dr=dr, rows0=rows0, o=o, buf=buf, ocols=ocols: e.dma_start(
                buf[:, :16 * ocols].rearrange("p (kc o) -> p kc o", o=ocols),
                dr[rows0:rows0 + 2048, o * ocols:(o + 1) * ocols].rearrange("(kc p) o -> p kc o", p=128)), inc="dwt", by=16)
        return P.now("dwt")

    def wchunk(self, kc, ocols=128):
        return (self.w_bf0, self.w_bf1)[kc // 16][:, (kc % 16) * ocols:(kc % 16 + 1) * ocols]

    def att_pre(self, L):
        P = self.P
        self.rot_gate = Gate(4)
        self.stats_cast(0, want_rcol=True, kscale=(L == 1))
        self.stats_cast(1, want_rcol=True, kscale=(L == 1))
        castdone = self.cast_done
        P.wait("tensor", "dve", castdone)
        P.fence("vector", "dve")
        P.run("vector", lambda e: e.memset(self.qn[0:1, :], 0.0), inc="dve")
        vz = P.run("vector", lambda e: e.memset(self.small[0:1, 4:8], 0.0), inc="dve")
        P.wait("vector", "dve", vz)
        if L == 0:
            self._pre_l0()
        else:
            self._pre_l1()
        vd = P.now("dve")
        P.wait("scalar", "dve", vd)
        vo = P.run("scalar", lambda e: e.dma_start(self.sd("n_own", L)[:, :], self.small[0:1, 4:6]), inc="dnrm", by=16)
        vo2 = P.run("scalar", lambda e: e.dma_start(self.sd("qn_st", L)[:, :], self.qn[0:1, 0:64]), inc="dnrm", by=16)
        P.wait("scalar", "dnrm", vo2)
        for rr in range(4):
            P.wait("scalar", f"dr{rr}", P.now(f"dr{rr}"))

    def _qnorm(self, rot_fm, width, dest):
        P = self.P
        ri2 = self.rot_gate_acquire_next()
        rt2 = self.rot[:, ri2 * 512:ri2 * 512 + 512]
        v3 = P.run("scalar", lambda e, rot_fm=rot_fm, rt2=rt2, width=width: e.activation(rt2[:, :width], rot_fm, AF.Square), inc="act")
        P.wait("tensor", "act", v3)
        P.wait("tensor", "dve", self.qn_gate)
        vm = P.run("tensor", lambda e, rt2=rt2, width=width: e.matmul(
            self.ps[2][0:1, :width], self.sb_ones_col[:, :], rt2[:, :width], start=True, stop=True), inc="pe")
        self.rot_gate.release(ri2, "pe", vm)
        P.wait("vector", "pe", vm)
        P.fence("vector", "dve")
        v4 = P.run("vector", lambda e, width=width: e.reduce_max(self.small[0:1, 3:4], self.ps[2][0:1, :width], axis=mybir.AxisListType.X), inc="dve")
        P.wait("vector", "dve", v4)
        v5 = P.run("vector", lambda e, dest=dest: e.tensor_tensor(
            dest, dest, self.small[0:1, 3:4], ALU.max), inc="dve")
        self.qn_gate = v4
        return v5

    def rot_gate_acquire_next(self):
        pool = getattr(self, "_rot_pool", (0, 1, 2, 3))
        ri = pool[self._rot_next % len(pool)]
        self._rot_next += 1
        self.rot_gate.acquire(self.P, ("vector", "scalar"), ri)
        return ri

    def _pre_l0(self):
        P = self.P
        qd = self.sd("qd", 0)
        self._rot_next = 0
        nmm = 0
        for o in range(NKC):
            vw = self.load_wslice("l0_wq", o)
            for ti in range(2):
                bank = self.ps[nmm % 2]
                P.wait("tensor", "dwt", vw)
                if self.bank_gate[nmm % 2] is not None:
                    nm, vv = self.bank_gate[nmm % 2]
                    P.wait("tensor", nm, vv)
                    self.bank_gate[nmm % 2] = None
                vmm = None
                for kc in range(NKC):
                    w = self.wchunk(kc)
                    xb = self.xbf[:, (ti * NKC + kc) * TT:(ti * NKC + kc) * TT + TT]
                    vmm = P.run("tensor", lambda e, bank=bank, w=w, xb=xb, kc=kc: e.matmul(
                        bank[:, :], w, xb, start=(kc == 0), stop=(kc == NKC - 1)),
                        inc="pe" if kc == NKC - 1 else None)
                self.wbf_gate[0] = ("pe", vmm)
                self.wbf_gate[1] = ("pe", vmm)
                bcr = (self.bc0, self.bc1)[ti]
                ri = self.rot_gate_acquire_next()
                rt = self.rot[:, ri * 512:ri * 512 + 512]
                P.wait("vector", "pe", vmm)
                P.fence("vector", "dve")
                v2 = P.run("vector", lambda e, bank=bank, bcr=bcr, rt=rt: e.tensor_tensor(rt, bank[:, :], bcr[:, :], ALU.mult), inc="dve")
                self.bank_gate[nmm % 2] = ("dve", v2)
                P.wait("scalar", "dve", v2)
                vq = self._qnorm(rt, 512, self.qn[0:1, ti * 32 + o:ti * 32 + o + 1])
                vd = P.run("scalar", lambda e, rt=rt, o=o, ti=ti, qd=qd: e.dma_start(
                    qd[o * 128:(o + 1) * 128, ti * TT:(ti + 1) * TT], rt), inc=f"dr{ri}", by=16)
                self.rot_gate.release(ri, f"dr{ri}", vd)
                nmm += 1
        vw = self.load_wslice("l0_wk", 0)
        for ti in range(2):
            bank = self.ps[nmm % 2]
            P.wait("tensor", "dwt", vw)
            if self.bank_gate[nmm % 2] is not None:
                nm, vv = self.bank_gate[nmm % 2]
                P.wait("tensor", nm, vv)
                self.bank_gate[nmm % 2] = None
            vmm = None
            for kc in range(NKC):
                w = self.wchunk(kc)
                xb = self.xbf[:, (ti * NKC + kc) * TT:(ti * NKC + kc) * TT + TT]
                vmm = P.run("tensor", lambda e, bank=bank, w=w, xb=xb, kc=kc: e.matmul(
                    bank[:, :], w, xb, start=(kc == 0), stop=(kc == NKC - 1)),
                    inc="pe" if kc == NKC - 1 else None)
            self.wbf_gate[0] = ("pe", vmm)
            self.wbf_gate[1] = ("pe", vmm)
            bcr = (self.bc0, self.bc1)[ti]
            ri = self.rot_gate_acquire_next()
            rt = self.rot[:, ri * 512:ri * 512 + 512]
            P.wait("vector", "pe", vmm)
            P.fence("vector", "dve")
            vs = P.run("vector", lambda e, bcr=bcr: e.tensor_scalar(self.scr2[:, :], bcr[:, :], ISQ, None, ALU.mult), inc="dve")
            P.wait("vector", "dve", vs)
            v2 = P.run("vector", lambda e, bank=bank, rt=rt: e.tensor_tensor(rt, bank[:, :], self.scr2[:, :], ALU.mult), inc="dve")
            self.bank_gate[nmm % 2] = ("dve", v2)
            P.wait("scalar", "dve", v2)
            vq = self._qnorm(rt, 512, self.small[0:1, 4:5])
            vd = P.run("scalar", lambda e, rt=rt, ti=ti: e.dma_start(
                self.sd("k_own", 0)[:, ti * TT:(ti + 1) * TT], rt), inc=f"dr{ri}", by=16)
            self.rot_gate.release(ri, f"dr{ri}", vd)
            nmm += 1
        vw = self.load_wslice("l0_wv", 0)
        for ti in range(2):
            for ts_ in range(4):
                bank = self.ps[nmm % 2]
                P.wait("tensor", "dwt", vw)
                if self.bank_gate[nmm % 2] is not None:
                    nm, vv = self.bank_gate[nmm % 2]
                    P.wait("tensor", nm, vv)
                    self.bank_gate[nmm % 2] = None
                vmm = None
                for kc in range(NKC):
                    xsl = self.xbf[:, (ti * NKC + kc) * TT + ts_ * 128:(ti * NKC + kc) * TT + ts_ * 128 + 128]
                    w = self.wchunk(kc)
                    vmm = P.run("tensor", lambda e, bank=bank, w=w, xsl=xsl, kc=kc: e.matmul(
                        bank[:, 0:128], xsl, w, start=(kc == 0), stop=(kc == NKC - 1)),
                        inc="pe" if kc == NKC - 1 else None)
                self.wbf_gate[0] = ("pe", vmm)
                self.wbf_gate[1] = ("pe", vmm)
                col = ti * 4 + ts_
                ri = self.rot_gate_acquire_next()
                rt = self.rot[:, ri * 512:ri * 512 + 512]
                P.wait("vector", "pe", vmm)
                P.fence("vector", "dve")
                v2 = P.run("vector", lambda e, bank=bank, rt=rt, col=col: e.tensor_scalar(
                    rt[:, 0:128], bank[:, 0:128], self.rcol[:, col:col + 1], None, ALU.mult), inc="dve")
                self.bank_gate[nmm % 2] = ("dve", v2)
                P.wait("scalar", "dve", v2)
                vd = P.run("scalar", lambda e, rt=rt, ti=ti, ts_=ts_: e.dma_start(
                    self.sd("v_own", 0)[ti * TT + ts_ * 128:ti * TT + ts_ * 128 + 128, :], rt[:, 0:128]), inc=f"dr{ri}", by=16)
                self.rot_gate.release(ri, f"dr{ri}", vd)
                nmm += 1

    def _pre_l1(self):
        P = self.P
        qd = self.sd("qd", 1)
        self._rot_next = 0
        nmm = 0
        self.rowbuf_gate = None
        for o in range(36):
            vw = self.load_wslice("l1_qkvw", o)
            if self.rowbuf_gate is not None:
                P.wait("sync", *self.rowbuf_gate)
            vb = P.run("sync", lambda e, o=o: e.dma_start(self.rowbuf[0:1, :], self.dram["l1_qkvb"][0:1, o * 128:(o + 1) * 128]), inc="dbr", by=16)
            P.wait("vector", "dbr", vb)
            P.fence("vector", "dve")
            sc = ISQ if o in (32, 33) else 1.0
            vc = P.run("vector", lambda e, sc=sc: e.tensor_scalar(self.rowbuf_bf[0:1, :], self.rowbuf[0:1, :], sc, None, ALU.mult), inc="dve")
            self.rowbuf_gate = ("dve", vc)
            P.wait("tensor", "dve", vc)
            if self.bbsb_gate:
                P.wait("tensor", "dve", self.bbsb_gate)
            vbb = P.run("tensor", lambda e: e.matmul(self.ps_x[:, 0:128], self.sb_ones_row[:1, :], self.rowbuf_bf[0:1, :], start=True, stop=True), inc="pe")
            P.wait("vector", "pe", vbb)
            P.fence("vector", "dve")
            vbc = P.run("vector", lambda e: e.tensor_copy(self.bb_sb[:, :], self.ps_x[:, 0:128]), inc="dve")
            self.bbsb_gate = vbc
            for ti in range(2):
                for ts_ in range(4):
                    bank = self.ps[nmm % 2]
                    P.wait("tensor", "dwt", vw)
                    P.wait("tensor", "dve", vbc)
                    if self.bank_gate[nmm % 2] is not None:
                        nm, vv = self.bank_gate[nmm % 2]
                        P.wait("tensor", nm, vv)
                        self.bank_gate[nmm % 2] = None
                    vmm = None
                    for kc in range(NKC):
                        xsl = self.xbf[:, (ti * NKC + kc) * TT + ts_ * 128:(ti * NKC + kc) * TT + ts_ * 128 + 128]
                        w = self.wchunk(kc)
                        vmm = P.run("tensor", lambda e, bank=bank, w=w, xsl=xsl, kc=kc: e.matmul(
                            bank[:, 0:128], xsl, w, start=(kc == 0), stop=(kc == NKC - 1)),
                            inc="pe" if kc == NKC - 1 else None)
                    self.wbf_gate[0] = ("pe", vmm)
                    self.wbf_gate[1] = ("pe", vmm)
                    col = (ti * 4 + ts_) + (8 if o in (32, 33) else 0)
                    P.wait("vector", "pe", vmm)
                    P.fence("vector", "dve")
                    v1 = P.run("vector", lambda e, bank=bank, col=col: e.tensor_scalar(
                        self.scr[:, 0:128], bank[:, 0:128], self.rcol[:, col:col + 1], None, ALU.mult), inc="dve")
                    self.bank_gate[nmm % 2] = ("dve", v1)
                    P.wait("vector", "dve", v1)
                    v2 = P.run("vector", lambda e: e.tensor_tensor(self.scr2[:, 0:128], self.scr[:, 0:128], self.bb_sb[:, :], ALU.add), inc="dve")
                    P.wait("vector", "dve", v2)
                    ri = self.rot_gate_acquire_next()
                    rt = self.rot[:, ri * 512:ri * 512 + 512]
                    tsub_abs = ti * 4 + ts_
                    if o < 34:
                        self._rope(ri, tsub_abs)
                        vrp = P.now("dve")
                        P.wait("tensor", "dve", vrp)
                        vtp = P.run("tensor", lambda e, rt=rt: e.transpose(
                            self.ps_tp[:, 0:128], rt[:, 0:128], self.sb_ident[:, :]), inc="pe")
                        ri2 = self.rot_gate_acquire_next()
                        rt2 = self.rot[:, ri2 * 512:ri2 * 512 + 512]
                        P.wait("vector", "pe", vtp)
                        P.fence("vector", "dve")
                        vcp = P.run("vector", lambda e, rt2=rt2: e.tensor_copy(rt2[:, 0:128], self.ps_tp[:, 0:128]), inc="dve")
                        P.wait("scalar", "dve", vcp)
                        if o < 32:
                            vq = self._qnorm(rt2[:, 0:128], 128, self.qn[0:1, ti * 32 + o:ti * 32 + o + 1])
                            vd = P.run("scalar", lambda e, rt2=rt2, o=o, ti=ti, ts_=ts_, qd=qd: e.dma_start(
                                qd[o * 128:(o + 1) * 128, ti * TT + ts_ * 128:ti * TT + ts_ * 128 + 128], rt2[:, 0:128]), inc=f"dr{ri2}", by=16)
                        else:
                            kh = o - 32
                            vq = self._qnorm(rt2[:, 0:128], 128, self.small[0:1, 4 + kh:5 + kh])
                            vd = P.run("scalar", lambda e, rt2=rt2, kh=kh, ti=ti, ts_=ts_: e.dma_start(
                                self.sd("k_own", 1)[kh * 128:(kh + 1) * 128, ti * TT + ts_ * 128:ti * TT + ts_ * 128 + 128], rt2[:, 0:128]), inc=f"dr{ri2}", by=16)
                        self.rot_gate.release(ri2, f"dr{ri2}", vd)
                        self.rot_gate.release(ri, "pe", vtp)
                    else:
                        dh = o - 34
                        vcv = P.run("vector", lambda e, rt=rt: e.tensor_copy(rt[:, 0:128], self.scr2[:, 0:128]), inc="dve")
                        P.wait("scalar", "dve", vcv)
                        vd = P.run("scalar", lambda e, rt=rt, dh=dh, ti=ti, ts_=ts_: e.dma_start(
                            self.sd("v_own", 1)[ti * TT + ts_ * 128:ti * TT + ts_ * 128 + 128, dh * 128:(dh + 1) * 128], rt[:, 0:128]), inc=f"dr{ri}", by=16)
                        self.rot_gate.release(ri, f"dr{ri}", vd)
                    nmm += 1

    def _rope(self, ri, tsub_abs):
        P = self.P
        base = ri * 512
        rt = self.rot[:, base:base + 512]
        cs = self.sb_rope[:, tsub_abs * 32:tsub_abs * 32 + 32]
        sn = self.sb_rope[:, 256 + tsub_abs * 32:256 + tsub_abs * 32 + 32]
        s2 = self.scr2
        scr = self.scr

        def xap(t, off, stride, count, width):
            return bass.AP(t, off, [[width, 128], [stride, count]])

        x1 = xap(s2, 0, 2, 32, 512)
        x2 = xap(s2, 1, 2, 32, 512)
        o1 = xap(self.rot, base + 0, [2, 32][0], 32, 2048)
        o2 = xap(self.rot, base + 1, 2, 32, 2048)
        P.fence("vector", "dve")
        va = P.run("vector", lambda e: e.tensor_tensor(scr[:, 0:32], x1, cs, ALU.mult), inc="dve")
        vb = P.run("vector", lambda e: e.tensor_tensor(scr[:, 32:64], x2, sn, ALU.mult), inc="dve")
        P.wait("vector", "dve", vb)
        vc = P.run("vector", lambda e: e.tensor_tensor(o1, scr[:, 0:32], scr[:, 32:64], ALU.subtract), inc="dve")
        P.wait("vector", "dve", vc)
        vd = P.run("vector", lambda e: e.tensor_tensor(scr[:, 0:32], x2, cs, ALU.mult), inc="dve")
        ve = P.run("vector", lambda e: e.tensor_tensor(scr[:, 32:64], x1, sn, ALU.mult), inc="dve")
        P.wait("vector", "dve", ve)
        vf = P.run("vector", lambda e: e.tensor_tensor(o2, scr[:, 0:32], scr[:, 32:64], ALU.add), inc="dve")
        vg = P.run("vector", lambda e: e.tensor_copy(rt[:, 64:128], s2[:, 64:128]), inc="dve")
        return vg

    def span_tiles(self, qt):
        tiles = []
        nch = 4 if qt == 0 else 8
        for c in range(nch - 1):
            for w in range(4):
                tiles.append((c, w, False))
        for w in range(4):
            tiles.append((qt, w, True))
        return tiles

    def att_flash(self, L):
        P = self.P
        kvh_of = (lambda h: 0) if L == 0 else (lambda h: h // 16)
        nheads_per_kv = 32 if L == 0 else 16
        wname = "l0_wo" if L == 0 else "l1_dense"
        P.wait("sync", "pe", P.now("pe"))
        P.wait("sync", "dve", P.now("dve"))
        if self.mask_gate is not None:
            P.wait("sync", *self.mask_gate)
        P.run("sync", lambda e: e.dma_start(self.sb_mask[:, :], self.c_mask[:, :]), inc="dmsk", by=16)
        vmk = P.run("sync", lambda e: e.dma_start(self.vones[:, :], self.dram["vones"][:, :]), inc="dmsk", by=16)
        vnl = P.run("sync", lambda e: e.dma_start(
            self.small[0:1, 8:16].rearrange("p (r c) -> p r c", r=4),
            self.sd("n_all", L)[:, :].rearrange("(o r) c -> o r c", o=1)[0]), inc="dnrm", by=16)
        vql0 = P.run("sync", lambda e: e.dma_start(self.qn[0:1, 0:64], self.sd("qn_st", L)[:, :]), inc="dnrm", by=16)
        P.wait("vector", "dnrm", vql0)
        P.wait("vector", "dmsk", vmk)
        P.fence("vector", "dve")
        for kh in range(2 if L == 1 else 1):
            ap_in = bass.AP(self.small, 8 + kh, [[16, 1], [2, 4]])
            P.run("vector", lambda e, ap_in=ap_in, kh=kh: e.reduce_max(self.small[0:1, 1 + kh:2 + kh], ap_in, axis=mybir.AxisListType.X), inc="dve")
        vkm = P.now("dve")

        for qt in range(2):
            nkt = 16 if qt == 0 else 32
            vq0 = P.now("dve")
            P.wait("sync", "dve", vq0)
            for h in range(32):
                P.run("sync", lambda e, h=h, qt=qt: e.dma_start(
                    self.fl_q[:, h * 512:(h + 1) * 512],
                    self.sd("qd", L)[h * 128:(h + 1) * 128, qt * TT:(qt + 1) * TT]), inc="dq", by=16)
            vql = P.now("dq")
            nkv = 1 if L == 0 else 2
            P.wait("vector", "dnrm", P.now("dnrm"))
            P.wait("vector", "dve", vkm)
            P.fence("vector", "dve")
            if L == 0:
                v1 = P.run("vector", lambda e, qt=qt: e.tensor_scalar(
                    self.qn[0:1, qt * 32:qt * 32 + 32], self.qn[0:1, qt * 32:qt * 32 + 32], self.small[0:1, 1:2], None, ALU.mult), inc="dve")
            else:
                P.run("vector", lambda e, qt=qt: e.tensor_scalar(
                    self.qn[0:1, qt * 32:qt * 32 + 16], self.qn[0:1, qt * 32:qt * 32 + 16], self.small[0:1, 1:2], None, ALU.mult), inc="dve")
                v1 = P.run("vector", lambda e, qt=qt: e.tensor_scalar(
                    self.qn[0:1, qt * 32 + 16:qt * 32 + 32], self.qn[0:1, qt * 32 + 16:qt * 32 + 32], self.small[0:1, 2:3], None, ALU.mult), inc="dve")
            P.wait("scalar", "dve", v1)
            v2 = P.run("scalar", lambda e, qt=qt: e.activation(self.qn[0:1, qt * 32:qt * 32 + 32], self.qn[0:1, qt * 32:qt * 32 + 32], AF.Sqrt), inc="act")
            P.wait("vector", "act", v2)
            v3 = P.run("vector", lambda e, qt=qt: e.tensor_scalar(
                self.qn[0:1, qt * 32:qt * 32 + 32], self.qn[0:1, qt * 32:qt * 32 + 32], -1.0, 64.0, ALU.mult, ALU.add), inc="dve")
            P.wait("tensor", "dve", v3)
            P.wait("tensor", "dve", self.evac_done)
            vmh = 0
            for h in range(32):
                P.wait("tensor", "dve", vmh)
                vpm = P.run("tensor", lambda e, h=h, qt=qt: e.matmul(
                    self.ps[4][:, 0:1], self.sb_ones_row_f[:1, :], self.qn[0:1, qt * 32 + h:qt * 32 + h + 1], start=True, stop=True), inc="pe")
                P.wait("vector", "pe", vpm)
                P.fence("vector", "dve")
                vmh = P.run("vector", lambda e, h=h: e.tensor_copy(self.mhat32[:, h:h + 1], self.ps[4][:, 0:1]), inc="dve")
            mhat_done = vmh

            for kvg in range(nkv):
                tiles = self.span_tiles(qt)
                if self.kv_gate is not None:
                    P.wait("sync", *self.kv_gate)
                    self.kv_gate = None
                for kt, (c, w, own) in enumerate(tiles):
                    if own:
                        ksrc = self.sd("k_own", L)[kvg * 128 * (L):kvg * 128 * (L) + 128, c * TT + w * 128:c * TT + w * 128 + 128] if L == 1 else                                self.sd("k_own", L)[:, c * TT + w * 128:c * TT + w * 128 + 128]
                        vsrc = self.sd("v_own", L)[c * TT + w * 128:c * TT + w * 128 + 128, kvg * 128:(kvg + 1) * 128] if L == 1 else                                self.sd("v_own", L)[c * TT + w * 128:c * TT + w * 128 + 128, :]
                    else:
                        rk = c if c < 4 else 7 - c
                        sl = 0 if c < 4 else 1
                        rstride = 128 * nkv
                        ksrc = self.sd("k_all", L)[rk * rstride + kvg * 128:rk * rstride + kvg * 128 + 128,
                                                  sl * TT + w * 128:sl * TT + w * 128 + 128]
                        vsrc = self.sd("v_all", L)[rk * TOK + sl * TT + w * 128:rk * TOK + sl * TT + w * 128 + 128,
                                                  kvg * 128:(kvg + 1) * 128] if L == 1 else                                self.sd("v_all", L)[rk * TOK + sl * TT + w * 128:rk * TOK + sl * TT + w * 128 + 128, :]
                    P.run("sync", lambda e, kt=kt, ksrc=ksrc: e.dma_start(
                        self.kv_k[:, kt * 128:(kt + 1) * 128], ksrc), inc="dkv", by=16)
                    P.run("sync", lambda e, kt=kt, vsrc=vsrc: e.dma_start(
                        self.kv_v[:, kt * 132:kt * 132 + 128], vsrc), inc="dkv", by=16)
                vkv = P.now("dkv")
                P.wait("vector", "dkv", vkv)
                P.fence("vector", "dve")
                vo = None
                for kt in range(nkt):
                    P.run("vector", lambda e, kt=kt, qt=qt: e.tensor_scalar(
                        self.kv_v[:, kt * 132:kt * 132 + 128], self.kv_v[:, kt * 132:kt * 132 + 128],
                        self.vones[:, qt * 32 + kt:qt * 32 + kt + 1], None, ALU.mult), inc="dve")
                    vo = P.run("vector", lambda e, kt=kt, qt=qt: e.tensor_copy(
                        self.kv_v[:, kt * 132 + 128:kt * 132 + 129], self.vones[:, qt * 32 + kt:qt * 32 + kt + 1]), inc="dve")
                vones_done = vo

                h0 = kvg * nheads_per_kv
                for hh in range(nheads_per_kv):
                    h = h0 + hh
                    self._flash_head(L, qt, h, nkt, vkv, vql, vones_done, mhat_done)
                self.kv_gate = ("pe", P.now("pe"))

            self._out_proj(L, qt, wname)
        self.mask_gate = ("dve", P.now("dve"))
        self.resid_ready = ("dve", P.now("dve"))

    def _flash_head(self, L, qt, h, nkt, vkv, vql, vones_done, mhat_done):
        P = self.P
        nb = nkt // 2
        ctx_banks = [self.ps[4], self.ps[5], self.ps_x,
                     self.ps_tp[:, :].bitcast(F32)[:, 0:512]]
        qsl = self.fl_q[:, h * 512:(h + 1) * 512]
        P.wait("tensor", "dq", vql)
        P.wait("tensor", "dkv", vkv)
        P.wait("tensor", "dve", max(vones_done, mhat_done, self.evac_done))
        expd = {}
        qkd = {}
        pvd = {}
        maskd = {}
        for b in range(nb):
            s0 = self.ps[(2 * b) % 4]
            s1 = self.ps[(2 * b + 1) % 4]
            if b >= 2:
                P.wait("tensor", "act", expd[b - 2])
            P.run("tensor", lambda e, s0=s0, kt=2 * b, qsl=qsl: e.matmul(
                s0[:, :], self.kv_k[:, kt * 128:(kt + 1) * 128], qsl, start=True, stop=True))
            qkd[b] = P.run("tensor", lambda e, s1=s1, kt=2 * b + 1, qsl=qsl: e.matmul(
                s1[:, :], self.kv_k[:, kt * 128:(kt + 1) * 128], qsl, start=True, stop=True), inc="pe")
            for kk in (2 * b, 2 * b + 1):
                if kk >= nkt - 4:
                    w = kk - (nkt - 4)
                    sb = self.ps[kk % 4]
                    P.wait("vector", "pe", qkd[b])
                    P.fence("vector", "dve")
                    maskd[kk] = P.run("vector", lambda e, sb=sb, w=w: e.tensor_tensor(
                        sb[:, :], sb[:, :], self.sb_mask[:, 384 - 128 * w:896 - 128 * w], ALU.add), inc="dve")
            P.wait("scalar", "pe", qkd[b])
            if (2 * b) >= nkt - 4 or (2 * b + 1) >= nkt - 4:
                P.wait("scalar", "dve", max(v for k, v in maskd.items() if k in (2 * b, 2 * b + 1)))
            if b >= 2:
                P.wait("scalar", "pe", pvd.get(b - 2, 0))
            P.run("scalar", lambda e, s0=s0, b=b, h=h: e.activation(
                self.fl_pr[:, ((2 * b) % 4) * 512:((2 * b) % 4) * 512 + 512], s0[:, :], AF.Exp,
                bias=self.mhat32[:, h:h + 1], scale=1.0))
            expd[b] = P.run("scalar", lambda e, s1=s1, b=b, h=h: e.activation(
                self.fl_pr[:, ((2 * b + 1) % 4) * 512:((2 * b + 1) % 4) * 512 + 512], s1[:, :], AF.Exp,
                bias=self.mhat32[:, h:h + 1], scale=1.0), inc="act")
            P.wait("tensor", "act", expd[b])
            vp = None
            for kk in (2 * b, 2 * b + 1):
                pr = self.fl_pr[:, (kk % 4) * 512:(kk % 4) * 512 + 512]
                for qs in range(4):
                    vp = P.run("tensor", lambda e, pr=pr, qs=qs, kk=kk, b=b: e.matmul(
                        ctx_banks[qs][:, 0:129], pr[:, qs * 128:(qs + 1) * 128],
                        self.kv_v[:, kk * 132:kk * 132 + 129],
                        start=(kk == 0), stop=(kk == nkt - 1)),
                        inc="pe" if (kk == 2 * b + 1 and qs == 3) else None)
            pvd[b] = vp
        P.wait("vector", "pe", pvd[nb - 1])
        P.fence("vector", "dve")
        tpb = self.ps[0].bitcast(BF16) if False else self.ps[0]
        for qs in (3, 2, 1, 0):
            cb = ctx_banks[qs]
            tpd = self.ps_tp[:, 384 + qs * 128:384 + qs * 128 + 128]
            vr = P.run("vector", lambda e, cb=cb: e.reciprocal(self.small[:, 3:4], cb[:, 128:129]), inc="dve")
            P.wait("vector", "dve", vr)
            vs = P.run("vector", lambda e, cb=cb, qs=qs: e.tensor_scalar(
                self.fl_ct[:, qs * 128:(qs + 1) * 128], cb[:, 0:128], self.small[:, 3:4], None, ALU.mult), inc="dve")
            P.wait("tensor", "dve", vs)
            vt = P.run("tensor", lambda e, tpd=tpd, qs=qs: e.transpose(
                tpd, self.fl_ct[:, qs * 128:(qs + 1) * 128], self.sb_ident[:, :]), inc="pe")
            P.wait("vector", "pe", vt)
            ve = P.run("vector", lambda e, tpd=tpd, qs=qs, h=h: e.tensor_copy(
                self.fl_q[:, h * 512 + qs * 128:h * 512 + qs * 128 + 128], tpd), inc="dve")
        self.evac_done = P.now("dve")

    def _out_proj(self, L, qt, wname):
        P = self.P
        P.wait("tensor", "dve", self.evac_done)
        nmm = 0
        for o in range(NKC):
            vw = self.load_wslice(wname, o)
            bank = self.ps[nmm % 2]
            P.wait("tensor", "dwt", vw)
            if self.bank_gate[nmm % 2] is not None:
                nm, vv = self.bank_gate[nmm % 2]
                P.wait("tensor", nm, vv)
                self.bank_gate[nmm % 2] = None
            vmm = None
            for kc in range(NKC):
                w = self.wchunk(kc)
                cx = self.fl_q[:, kc * 512:(kc + 1) * 512]
                vmm = P.run("tensor", lambda e, bank=bank, w=w, cx=cx, kc=kc: e.matmul(
                    bank[:, :], w, cx, start=(kc == 0), stop=(kc == NKC - 1)),
                    inc="pe" if kc == NKC - 1 else None)
            self.wbf_gate[0] = ("pe", vmm)
            self.wbf_gate[1] = ("pe", vmm)
            P.wait("vector", "pe", vmm)
            P.fence("vector", "dve")
            xa = self.x_res[:, o * TOK + qt * TT:o * TOK + qt * TT + TT]
            va = P.run("vector", lambda e, bank=bank, xa=xa: e.tensor_tensor(xa, bank[:, :], xa, ALU.add), inc="dve")
            self.bank_gate[nmm % 2] = ("dve", va)
            nmm += 1

    def ffn1(self):
        P = self.P
        self.rot_gate = Gate(4)
        self._rot_next = 0
        self._rot_pool = (3,)
        self.routeT_gate = 0
        self.stats_cast(0, want_rcol=True)
        self.stats_cast(1, want_rcol=True)
        castdone = self.cast_done
        P.wait("tensor", "dve", castdone)
        if self.wg_gate is not None:
            P.wait("sync", *self.wg_gate)
            self.wg_gate = None
        vwg = P.run("sync", lambda e: e.dma_start(
            self.wg_sb[:, :].rearrange("p (kc o) -> p kc o", o=8),
            self.dram["f1_wg"][:, :].rearrange("(kc p) o -> p kc o", p=128)), inc="dwg", by=16)
        for ti in range(2):
            for ts_ in range(4):
                col = ti * 4 + ts_
                bank = self.ps[0]
                P.wait("tensor", "dwg", vwg)
                if self.bank_gate[0] is not None:
                    P.wait("tensor", *self.bank_gate[0]); self.bank_gate[0] = None
                vmm = None
                for kc in range(NKC):
                    xsl = self.xbf[:, (ti * NKC + kc) * TT + ts_ * 128:(ti * NKC + kc) * TT + ts_ * 128 + 128]
                    wg = self.wg_sb[:, kc * 8:(kc + 1) * 8]
                    vmm = P.run("tensor", lambda e, bank=bank, xsl=xsl, wg=wg, kc=kc: e.matmul(
                        bank[:, 0:8], xsl, wg, start=(kc == 0), stop=(kc == NKC - 1)),
                        inc="pe" if kc == NKC - 1 else None)
                P.wait("vector", "pe", vmm)
                P.fence("vector", "dve")
                v1 = P.run("vector", lambda e, bank=bank, col=col: e.tensor_scalar(
                    self.lg_sb[:, :], bank[:, 0:8], self.rcol[:, col:col + 1], None, ALU.mult), inc="dve")
                self.bank_gate[0] = ("dve", v1)
                P.wait("vector", "dve", v1)
                v2 = P.run("vector", lambda e: e.reduce_max(self.small[:, 3:4], self.lg_sb[:, :], axis=mybir.AxisListType.X, negate=True), inc="dve")
                P.wait("scalar", "dve", v2)
                v3 = P.run("scalar", lambda e: e.activation(self.scr[:, 0:8], self.lg_sb[:, :], AF.Exp, bias=self.small[:, 3:4], scale=1.0), inc="act")
                P.wait("vector", "dve", v2)
                v4 = P.run("vector", lambda e: e.tensor_scalar(self.scr2[:, 0:8], self.lg_sb[:, :], self.small[:, 3:4], None, ALU.add), inc="dve")
                P.wait("vector", "dve", v4)
                v5 = P.run("vector", lambda e: e.tensor_single_scalar(self.scr2[:, 8:16], self.scr2[:, 0:8], -1e-30, ALU.is_ge), inc="dve")
                P.wait("vector", "dve", v5)
                v6 = P.run("vector", lambda e: e.scalar_tensor_tensor(self.scr2[:, 0:8], self.scr2[:, 8:16], NEG, self.lg_sb[:, :], ALU.mult, ALU.add), inc="dve")
                P.wait("vector", "dve", v6)
                v7 = P.run("vector", lambda e: e.reduce_max(self.small[:, 4:5], self.scr2[:, 0:8], axis=mybir.AxisListType.X), inc="dve")
                P.wait("vector", "dve", v7)
                v8 = P.run("vector", lambda e: e.tensor_single_scalar(self.scr2[:, 0:8], self.lg_sb[:, :], self.small[:, 4:5], ALU.is_ge), inc="dve")
                P.wait("vector", "dve", v8)
                P.wait("vector", "act", v3)
                v9 = P.run("vector", lambda e: e.tensor_tensor(self.scr[:, 8:16], self.scr[:, 0:8], self.scr2[:, 0:8], ALU.mult), inc="dve")
                P.wait("vector", "dve", v9)
                va = P.run("vector", lambda e: e.reduce_sum(self.small[:, 5:6], self.scr[:, 8:16], axis=mybir.AxisListType.X), inc="dve")
                P.wait("vector", "dve", va)
                vb = P.run("vector", lambda e: e.reciprocal(self.small[:, 6:7], self.small[:, 5:6]), inc="dve")
                P.wait("vector", "dve", vb)
                ri = self.rot_gate_acquire_next()
                rt = self.rot[:, ri * 512:ri * 512 + 512]
                vc = P.run("vector", lambda e, rt=rt: e.tensor_scalar(rt[:, 0:8], self.scr[:, 8:16], self.small[:, 6:7], None, ALU.mult), inc="dve")
                P.wait("tensor", "dve", vc)
                P.wait("tensor", "pe", self.routeT_gate)
                vt = P.run("tensor", lambda e, rt=rt: e.transpose(self.ps_tp[0:8, 0:128], rt[:, 0:8], self.sb_ident[:, :]), inc="pe")
                P.wait("vector", "pe", vt)
                P.fence("vector", "dve")
                vu = P.run("vector", lambda e, ts_=ts_: e.tensor_copy(self.route_T[0:8, ts_ * 128:(ts_ + 1) * 128], self.ps_tp[0:8, 0:128]), inc="dve")
                self.rot_gate.release(ri, "pe", vt)
            vwa = self.load_wslice("f1_at", 0)
            bank = self.ps[1]
            P.wait("tensor", "dwt", vwa)
            if self.bank_gate[1] is not None:
                P.wait("tensor", *self.bank_gate[1]); self.bank_gate[1] = None
            vmm = None
            for kc in range(NKC):
                xb = self.xbf[:, (ti * NKC + kc) * TT:(ti * NKC + kc) * TT + TT]
                w = self.wchunk(kc)
                vmm = P.run("tensor", lambda e, bank=bank, w=w, xb=xb, kc=kc: e.matmul(
                    bank[:, :], w, xb, start=(kc == 0), stop=(kc == NKC - 1)),
                    inc="pe" if kc == NKC - 1 else None)
            self.wbf_gate[0] = ("pe", vmm)
            self.wbf_gate[1] = ("pe", vmm)
            vre = P.now("dve")
            P.wait("tensor", "dve", vre)
            vx = P.run("tensor", lambda e: e.matmul(self.ps[2][:, :], self.sb_expand[0:8, :], self.route_T[0:8, :], start=True, stop=True), inc="pe")
            self.routeT_gate = vx
            bcr = (self.bc0, self.bc1)[ti]
            P.wait("vector", "pe", vx)
            P.fence("vector", "dve")
            v1 = P.run("vector", lambda e, bcr=bcr: e.tensor_tensor(self.scr[:, :], self.ps[2][:, :], bcr[:, :], ALU.mult), inc="dve")
            P.wait("vector", "pe", vmm)
            P.wait("vector", "dve", v1)
            v2 = P.run("vector", lambda e, bank=bank, ti=ti: e.tensor_tensor(
                self.hw_sb[:, ti * 512:(ti + 1) * 512], bank[:, :], self.scr[:, :], ALU.mult), inc="dve")
            self.bank_gate[1] = ("dve", v2)
            self.hw_done = v2
        self.bsl_gate = [("pe", P.now("pe")), ("pe", P.now("pe"))]
        self.dbank_gate = [("dve", self.hw_done), ("dve", self.hw_done)]
        nmm = 0
        for o in range(NKC):
            vw = self.load_wslice("f1_wb", o)
            bhalf = self.wg_sb[:, (o % 2) * 128:(o % 2) * 128 + 128]
            if self.bsl_gate[o % 2] is not None:
                P.wait("sync", *self.bsl_gate[o % 2])
                self.bsl_gate[o % 2] = None
            vbs = P.run("sync", lambda e, bhalf=bhalf, o=o: e.dma_start(
                bhalf, self.dram["f1_bf"][:, o * 128:(o + 1) * 128]), inc="dbs", by=16)
            for ti in range(2):
                bank = self.ps[nmm % 2]
                dbank = self.ps[2 + nmm % 2]
                P.wait("tensor", "dwt", vw)
                if self.bank_gate[nmm % 2] is not None:
                    nm, vv = self.bank_gate[nmm % 2]
                    P.wait("tensor", nm, vv)
                    self.bank_gate[nmm % 2] = None
                if self.dbank_gate[nmm % 2] is not None:
                    nm, vv = self.dbank_gate[nmm % 2]
                    P.wait("tensor", nm, vv)
                    self.dbank_gate[nmm % 2] = None
                vmm = None
                for kc in range(NKC):
                    w = self.wchunk(kc)
                    xb = self.xbf[:, (ti * NKC + kc) * TT:(ti * NKC + kc) * TT + TT]
                    vmm = P.run("tensor", lambda e, bank=bank, w=w, xb=xb, kc=kc: e.matmul(
                        bank[:, :], w, xb, start=(kc == 0), stop=(kc == NKC - 1)),
                        inc="pe" if kc == NKC - 1 else None)
                self.wbf_gate[0] = ("pe", vmm)
                self.wbf_gate[1] = ("pe", vmm)
                P.wait("tensor", "dbs", vbs)
                vdm = P.run("tensor", lambda e, dbank=dbank, bhalf=bhalf, ti=ti: e.matmul(
                    dbank[:, :], bhalf, self.hw_sb[:, ti * 512:(ti + 1) * 512], start=True, stop=True), inc="pe")
                self.bsl_gate[o % 2] = ("pe", vdm)
                bcr = (self.bc0, self.bc1)[ti]
                P.wait("vector", "pe", vdm)
                P.fence("vector", "dve")
                v1 = P.run("vector", lambda e, bank=bank, bcr=bcr: e.tensor_tensor(self.scr[:, :], bank[:, :], bcr[:, :], ALU.mult), inc="dve")
                self.bank_gate[nmm % 2] = ("dve", v1)
                P.wait("vector", "dve", v1)
                v2 = P.run("vector", lambda e, dbank=dbank, o=o: e.scalar_tensor_tensor(
                    self.scr2[:, :], dbank[:, :], self.vec[:, 6 * 32 + o:6 * 32 + o + 1], self.scr[:, :], ALU.add, ALU.add), inc="dve")
                self.dbank_gate[nmm % 2] = ("dve", v2)
                P.wait("vector", "dve", v2)
                xa = self.x_res[:, o * TOK + ti * TT:o * TOK + ti * TT + TT]
                v3 = P.run("vector", lambda e, xa=xa: e.scalar_tensor_tensor(xa, self.scr2[:, :], 0.5, xa, ALU.mult, ALU.add), inc="dve")
                nmm += 1
        self.resid_ready = ("dve", P.now("dve"))
        self._rot_pool = (0, 1, 2, 3)


    def prep(self):
        P = self.P
        for name, rows, cols, tern in PREP_SPECS:
            win = self.dram[name + "_f"]
            wout = self.dram[name]
            if cols % 512 == 0:
                width = 512
            else:
                width = cols
            ntile = max(1, cols // width)
            for rb in range(rows // 128):
                for tb in range(ntile):
                    if self.pp_gate[0] is not None:
                        P.wait("sync", *self.pp_gate[0])
                        self.pp_gate[0] = None
                    vd = P.run("sync", lambda e, win=win, rb=rb, tb=tb, width=width: e.dma_start(
                        self.w_st[:, :width], win[rb * 128:(rb + 1) * 128, tb * width:(tb + 1) * width]), inc="dpp", by=16)
                    ri = 0
                    rt = self.rot[:, 0:512]
                    if self.pp_gate[1] is not None:
                        P.wait("scalar", *self.pp_gate[1])
                        P.wait("vector", *self.pp_gate[1])
                        self.pp_gate[1] = None
                    if not tern:
                        P.wait("scalar", "dpp", vd)
                        vc = P.run("scalar", lambda e, rt=rt, width=width: e.copy(rt[:, :width], self.w_st[:, :width]), inc="act")
                        self.pp_gate[0] = ("act", vc)
                        gate = ("act", vc)
                    else:
                        P.wait("scalar", "dpp", vd)
                        v1 = P.run("scalar", lambda e, width=width: e.activation(self.scr[:, :width], self.w_st[:, :width], AF.Sign), inc="act")
                        v1b = P.run("scalar", lambda e, width=width: e.activation(self.scr2[:, :width], self.w_st[:, :width], AF.Abs), inc="act")
                        self.pp_gate[0] = ("act", v1b)
                        P.wait("vector", "act", v1b)
                        P.fence("vector", "dve")
                        v2 = P.run("vector", lambda e, width=width: e.tensor_single_scalar(self.scr2[:, :width], self.scr2[:, :width], 0.1, ALU.is_ge), inc="dve")
                        P.wait("vector", "dve", v2)
                        vc = P.run("vector", lambda e, rt=rt, width=width: e.tensor_tensor(rt[:, :width], self.scr[:, :width], self.scr2[:, :width], ALU.mult), inc="dve")
                        gate = ("dve", vc)
                    P.wait("scalar", gate[0], gate[1])
                    vo = P.run("scalar", lambda e, wout=wout, rb=rb, tb=tb, rt=rt, width=width: e.dma_start(
                        wout[rb * 128:(rb + 1) * 128, tb * width:(tb + 1) * width], rt[:, :width]), inc="dpo", by=16)
                    self.pp_gate[1] = ("dpo", vo)
        vtot = P.now("dpo")
        P.wait("sync", "dpo", vtot)

    def collect_kv(self, L):
        P = self.P
        for rr in range(4):
            P.wait("gpsimd", f"dr{rr}", P.now(f"dr{rr}"))
        P.wait("gpsimd", "dnrm", P.now("dnrm"))
        groups = [[0, 1, 2, 3], [4, 5, 6, 7]]
        for src, dst in (("k_own", "k_all"), ("v_own", "v_all"), ("n_own", "n_all")):
            a = self.sd(src, L)
            b = self.sd(dst, L)
            P.run("gpsimd", lambda e, a=a, b=b, groups=groups: e.collective_compute(
                "AllGather", ALU.bypass, replica_groups=groups,
                ins=[a.ap().opt()], outs=[b.ap().opt()]), inc="cc", by=1)
        v = P.now("cc")
        P.wait("sync", "cc", v)
        P.wait("vector", "cc", v)
        P.wait("tensor", "cc", v)
        P.wait("scalar", "cc", v)

    def cc0(self):
        self.collect_kv(0)

    def cc1(self):
        self.collect_kv(1)
    def att_pre0(self):
        self.att_pre(0)

    def att_pre1(self):
        self.att_pre(1)

    def att_flash0(self):
        self.att_flash(0)

    def att_flash1(self):
        self.att_flash(1)

    def build(self, phases):
        self.pp_gate = [None, None]
        self.resid_ready = None
        self.mask_gate = None
        self.kv_gate = None
        self.wg_gate = None
        self.bbsb_gate = None
        self.bsl_gate = [None, None]
        self.evac_done = 0
        self.qn_gate = 0
        self._rot_next = 0
        self.wbf_gate = [None, None]
        self.wst_gate = [None, None]
        self.bank_gate = [None, None]
        self.p2_gate = [None, None]
        self.rot_gate = Gate(4)
        self.load_consts()
        self.load_x()
        self.preload_vectors(phases)
        for ph in phases:
            getattr(self, ph)()
            if self.fused:
                self.P.barrier()
        self.store_x()
        self.P.finish()
        return self.nc



PREP_SPECS = [
    ("l0_wq", D, D, False), ("l0_wk", D, 128, False), ("l0_wv", D, 128, False),
    ("l0_wo", D, D, False), ("tern", D, D, True),
    ("l1_qkvw", D, 4608, False), ("l1_dense", D, D, False),
    ("f1_wb", D, D, False), ("f1_at", D, 128, False), ("f1_wg", D, 8, False),
    ("f1_bf", 128, D, False),
]


def _prep_program():
    nc = bass.Bass("TRN2", target_bir_lowering=False)
    P = Prog(nc)
    st = nc.alloc_sbuf_tensor("st", [128, 1024], F32)
    sg = nc.alloc_sbuf_tensor("sg", [128, 1024], F32)
    ot = nc.alloc_sbuf_tensor("ot", [128, 1024], BF16)
    g_st = [None]
    g_ot = [None]
    for name, rows, cols, tern in PREP_SPECS:
        win = nc.dram_tensor(name + "_f", [rows, cols], F32, kind="ExternalInput")
        wout = nc.dram_tensor(name, [rows, cols], BF16, kind="ExternalOutput")
        if cols % 1024 == 0:
            width = 1024
        elif cols % 512 == 0:
            width = 512
        else:
            width = cols
        ntile = max(1, cols // width)
        for rb in range(rows // 128):
            for tb in range(ntile):
                if g_st[0] is not None:
                    P.wait("sync", *g_st[0])
                vd = P.run("sync", lambda e, win=win, rb=rb, tb=tb, width=width: e.dma_start(
                    st[:, :width], win[rb * 128:(rb + 1) * 128, tb * width:(tb + 1) * width]), inc="d", by=16)
                if g_ot[0] is not None:
                    P.wait("scalar" if not tern else "vector", *g_ot[0])
                if not tern:
                    P.wait("scalar", "d", vd)
                    vc = P.run("scalar", lambda e, width=width: e.copy(ot[:, :width], st[:, :width]), inc="a")
                    g_st[0] = ("a", vc)
                    gate = ("a", vc)
                else:
                    P.wait("scalar", "d", vd)
                    v1 = P.run("scalar", lambda e, width=width: e.activation(sg[:, :width], st[:, :width], AF.Sign), inc="a")
                    v1b = P.run("scalar", lambda e, width=width: e.activation(st[:, :width], st[:, :width], AF.Abs), inc="a")
                    P.wait("vector", "a", v1b)
                    v2 = P.run("vector", lambda e, width=width: e.tensor_single_scalar(st[:, :width], st[:, :width], 0.1, ALU.is_ge), inc="v")
                    P.wait("vector", "v", v2)
                    vc = P.run("vector", lambda e, width=width: e.tensor_tensor(ot[:, :width], sg[:, :width], st[:, :width], ALU.mult), inc="v")
                    g_st[0] = ("v", vc)
                    gate = ("v", vc)
                P.wait("scalar", gate[0], gate[1])
                vo = P.run("scalar", lambda e, wout=wout, rb=rb, tb=tb, width=width: e.dma_start(
                    wout[rb * 128:(rb + 1) * 128, tb * width:(tb + 1) * width], ot[:, :width]), inc="do", by=16)
                g_ot[0] = ("do", vo)
    P.wait("scalar", "do", P.now("do"))
    P.finish()
    return nc


def _mode_for(phases, fused):
    IN, OUT, INT = "ExternalInput", "ExternalOutput", "Internal"
    stg = INT if fused else None
    m = {"x_own": ((D, TOK), F32, IN), "x_out": ((D, TOK), F32, OUT)}
    ph = set(phases)
    if "ffn0" in ph:
        m.update({"tern": ((D, D), BF16, IN), "y2": ((D, TOK), BF16, INT),
                  "f0_bias": ((128, 32), F32, IN), "f0_scale": ((128, 32), F32, IN),
                  "f0_gamma": ((128, 32), F32, IN), "f0_beta": ((128, 32), F32, IN),
                  "f0_qs": ((1, 1), F32, IN)})
    if "ffn1" in ph:
        m.update({"f1_wb": ((D, D), BF16, IN), "f1_at": ((D, 128), BF16, IN),
                  "f1_wg": ((D, 8), BF16, IN), "f1_bf": ((128, D), BF16, IN),
                  "f1_bbase": ((128, 32), F32, IN)})
    if "att_pre0" in ph:
        m.update({"l0_wq": ((D, D), BF16, IN), "l0_wk": ((D, 128), BF16, IN),
                  "l0_wv": ((D, 128), BF16, IN),
                  "qd": ((D, TOK), BF16, stg or OUT),
                  "k_own": ((128, TOK), BF16, stg or OUT),
                  "v_own": ((TOK, 128), BF16, stg or OUT),
                  "n_own": ((1, 2), F32, stg or OUT),
                  "qn_st": ((1, 64), F32, stg or OUT)})
    if "att_flash0" in ph:
        m.update({"l0_wo": ((D, D), BF16, IN), "vones": ((128, 64), F32, IN)})
        if not fused:
            m.update({"qd": ((D, TOK), BF16, IN), "qn_st": ((1, 64), F32, IN),
                      "k_own": ((128, TOK), BF16, IN), "v_own": ((TOK, 128), BF16, IN),
                      "k_all": ((512, TOK), BF16, IN), "v_all": ((4 * TOK, 128), BF16, IN),
                      "n_all": ((4, 2), F32, IN)})
        else:
            m.update({"k_all": ((512, TOK), BF16, INT), "v_all": ((4 * TOK, 128), BF16, INT),
                      "n_all": ((4, 2), F32, INT)})
    if "att_pre1" in ph:
        m.update({"l1_qkvw": ((D, 4608), BF16, IN), "l1_qkvb": ((1, 4608), F32, IN),
                  "rope_cos": ((TOK, 32), F32, IN), "rope_sin": ((TOK, 32), F32, IN),
                  "qd": ((D, TOK), BF16, stg or OUT),
                  "k_own": ((256, TOK), BF16, stg or OUT),
                  "v_own": ((TOK, 256), BF16, stg or OUT),
                  "n_own": ((1, 2), F32, stg or OUT),
                  "qn_st": ((1, 64), F32, stg or OUT)})
    if "att_flash1" in ph:
        m.update({"l1_dense": ((D, D), BF16, IN), "vones": ((128, 64), F32, IN)})
        if not fused:
            m.update({"qd": ((D, TOK), BF16, IN), "qn_st": ((1, 64), F32, IN),
                      "k_own": ((256, TOK), BF16, IN), "v_own": ((TOK, 256), BF16, IN),
                      "k_all": ((1024, TOK), BF16, IN), "v_all": ((4 * TOK, 256), BF16, IN),
                      "n_all": ((4, 2), F32, IN)})
        else:
            m.update({"k_all": ((1024, TOK), BF16, INT), "v_all": ((4 * TOK, 256), BF16, INT),
                      "n_all": ((4, 2), F32, INT)})
    if fused:
        for nm in ("qd", "k_own", "v_own", "n_own", "qn_st", "k_all", "v_all", "n_all"):
            m.pop(nm, None)
        m.update({
            "qd0": ((D, TOK), BF16, INT), "k_own0": ((128, TOK), BF16, INT),
            "v_own0": ((TOK, 128), BF16, INT), "n_own0": ((1, 2), F32, INT),
            "qn_st0": ((1, 64), F32, INT), "k_all0": ((512, TOK), BF16, INT),
            "v_all0": ((4 * TOK, 128), BF16, INT), "n_all0": ((4, 2), F32, INT),
            "qd1": ((D, TOK), BF16, INT), "k_own1": ((256, TOK), BF16, INT),
            "v_own1": ((TOK, 256), BF16, INT), "n_own1": ((1, 2), F32, INT),
            "qn_st1": ((1, 64), F32, INT), "k_all1": ((1024, TOK), BF16, INT),
            "v_all1": ((4 * TOK, 256), BF16, INT), "n_all1": ((4, 2), F32, INT)})
        for name, rows, cols, tern in PREP_SPECS:
            if name in m:
                m[name] = ((rows, cols), BF16, INT)
                m[name + "_f"] = ((rows, cols), F32, IN)
    return m


def _host_shard_x(xfull):
    xf = np.ascontiguousarray(xfull.reshape(BATCH * SEQ, D).T)
    return [np.ascontiguousarray(xf[:, _rows_of_core(c)]) for c in range(NC)]


def _host_unshard_x(shards):
    xf = np.empty((D, BATCH * SEQ), dtype=np.float32)
    for c in range(NC):
        xf[:, _rows_of_core(c)] = shards[c]
    return np.ascontiguousarray(xf.T).reshape(BATCH, SEQ, D)


def _vones_for_core(c):
    v = np.zeros((128, 64), dtype=np.float32)
    j = c % 4
    for qt, g in ((0, j), (1, 7 - j)):
        nch = 4 if qt == 0 else 8
        kt = 0
        for cc in range(nch - 1):
            for w in range(4):
                v[:, qt * 32 + kt] = 1.0 if cc < g else 0.0
                kt += 1
        for w in range(4):
            v[:, qt * 32 + kt] = 1.0
            kt += 1
    return v


def vec128(v):
    return np.ascontiguousarray(np.asarray(v, dtype=np.float32).reshape(32, 128).T)


_PROG_CACHE = {}


def _get_prog(key, phases, fused):
    if key not in _PROG_CACHE:
        mk = MK(_mode_for(phases, fused), fused=fused)
        _PROG_CACHE[key] = (mk, mk.build(list(phases)))
    return _PROG_CACHE[key]


FUSED = True
_FUSED_PHASES = ("prep", "ffn0", "att_pre0", "cc0", "att_flash0", "ffn0",
                 "ffn1", "att_pre1", "cc1", "att_flash1", "ffn1")


def _weights_f(inputs):
    return {
        "l0_wq_f": np.ascontiguousarray(np.asarray(inputs["L0_attn_wq"], np.float32)),
        "l0_wk_f": np.ascontiguousarray(np.asarray(inputs["L0_attn_wk"], np.float32)),
        "l0_wv_f": np.ascontiguousarray(np.asarray(inputs["L0_attn_wv"], np.float32)),
        "l0_wo_f": np.ascontiguousarray(np.asarray(inputs["L0_attn_wo"], np.float32)),
        "tern_f": np.ascontiguousarray(np.asarray(inputs["L0_ffn_weight"], np.float32).T),
        "l1_qkvw_f": np.ascontiguousarray(np.asarray(inputs["L1_attn_qkv_w"], np.float32).T),
        "l1_dense_f": np.ascontiguousarray(np.asarray(inputs["L1_attn_dense_w"], np.float32).T),
        "f1_wb_f": np.ascontiguousarray(np.asarray(inputs["L1_ffn_W_base"], np.float32).T),
        "f1_at_f": np.ascontiguousarray(np.asarray(inputs["L1_ffn_A"], np.float32).reshape(128, D).T),
        "f1_wg_f": np.ascontiguousarray(np.asarray(inputs["L1_ffn_W_gate"], np.float32).T),
        "f1_bf_f": np.ascontiguousarray(np.asarray(inputs["L1_ffn_B"], np.float32).transpose(0, 2, 1).reshape(128, D)),
    }


def _vec_inputs(inputs):
    return {
        "f0_bias": vec128(inputs["L0_ffn_bias"]),
        "f0_scale": vec128(inputs["L0_ffn_scale"]),
        "f0_gamma": vec128(inputs["L0_ffn_ln_gamma"]),
        "f0_beta": vec128(inputs["L0_ffn_ln_beta"]),
        "f0_qs": np.asarray(inputs["L0_ffn_quant_scale"], np.float32).reshape(1, 1),
        "f1_bbase": vec128(inputs["L1_ffn_b_base"]),
        "l1_qkvb": np.ascontiguousarray(np.asarray(inputs["L1_attn_qkv_b"], np.float32).reshape(1, 4608)),
    }


def kernel(**inputs):
    cos, sin = _rope_tables()
    xs = _host_shard_x(np.asarray(inputs["L0_attn_x"], dtype=np.float32))

    if FUSED:
        mk, nc = _get_prog("fused", _FUSED_PHASES, fused=True)
        base = _vec_inputs(inputs)
        base.update(_weights_f(inputs))
        vones = [_vones_for_core(c) for c in range(NC)]
        in_maps = [dict(base, x_own=xs[c], vones=vones[c],
                        rope_cos=np.ascontiguousarray(cos[_positions_of_core(c)]),
                        rope_sin=np.ascontiguousarray(sin[_positions_of_core(c)]))
                   for c in range(NC)]
        res = run_bass_kernel_spmd(nc, in_maps, core_ids=list(range(NC)))
        return _host_unshard_x([res.results[c]["x_out"] for c in range(NC)])

    if "prep" not in _PROG_CACHE:
        _PROG_CACHE["prep"] = _prep_program()
    ncp = _PROG_CACHE["prep"]
    prep_in = {
        "l0_wq_f": np.ascontiguousarray(np.asarray(inputs["L0_attn_wq"], np.float32)),
        "l0_wk_f": np.ascontiguousarray(np.asarray(inputs["L0_attn_wk"], np.float32)),
        "l0_wv_f": np.ascontiguousarray(np.asarray(inputs["L0_attn_wv"], np.float32)),
        "l0_wo_f": np.ascontiguousarray(np.asarray(inputs["L0_attn_wo"], np.float32)),
        "tern_f": np.ascontiguousarray(np.asarray(inputs["L0_ffn_weight"], np.float32).T),
        "l1_qkvw_f": np.ascontiguousarray(np.asarray(inputs["L1_attn_qkv_w"], np.float32).T),
        "l1_dense_f": np.ascontiguousarray(np.asarray(inputs["L1_attn_dense_w"], np.float32).T),
        "f1_wb_f": np.ascontiguousarray(np.asarray(inputs["L1_ffn_W_base"], np.float32).T),
        "f1_at_f": np.ascontiguousarray(np.asarray(inputs["L1_ffn_A"], np.float32).reshape(128, D).T),
        "f1_wg_f": np.ascontiguousarray(np.asarray(inputs["L1_ffn_W_gate"], np.float32).T),
        "f1_bf_f": np.ascontiguousarray(np.asarray(inputs["L1_ffn_B"], np.float32).transpose(0, 2, 1).reshape(128, D)),
    }
    pres = run_bass_kernel_spmd(ncp, [prep_in], core_ids=[0])
    W = pres.results[0]

    base_all = {
        "f0_bias": vec128(inputs["L0_ffn_bias"]),
        "f0_scale": vec128(inputs["L0_ffn_scale"]),
        "f0_gamma": vec128(inputs["L0_ffn_ln_gamma"]),
        "f0_beta": vec128(inputs["L0_ffn_ln_beta"]),
        "f0_qs": np.asarray(inputs["L0_ffn_quant_scale"], np.float32).reshape(1, 1),
        "f1_bbase": vec128(inputs["L1_ffn_b_base"]),
        "l1_qkvb": np.ascontiguousarray(np.asarray(inputs["L1_attn_qkv_b"], np.float32).reshape(1, 4608)),
    }
    base_all.update({k: W[k] for k in W})
    vones = [_vones_for_core(c) for c in range(NC)]
    ropes = [(np.ascontiguousarray(cos[_positions_of_core(c)]),
              np.ascontiguousarray(sin[_positions_of_core(c)])) for c in range(NC)]

    groups = [("g_ffn0", ("ffn0",)), ("g_pre0", ("att_pre0",)), ("g_fl0", ("att_flash0",)),
              ("g_ffn0", ("ffn0",)), ("g_ffn1", ("ffn1",)), ("g_pre1", ("att_pre1",)),
              ("g_fl1", ("att_flash1",)), ("g_ffn1", ("ffn1",))]

    cur = xs
    kv = {}
    for key, phases in groups:
        mk, nc = _get_prog(key, phases, fused=False)
        names = set(mk.t.keys())
        in_maps = []
        for c in range(NC):
            im = {"x_own": cur[c]}
            for nm in names:
                kind = mk.t[nm][2]
                if kind != "ExternalInput" or nm == "x_own":
                    continue
                if nm in ("qd", "k_own", "v_own", "n_own", "qn_st"):
                    im[nm] = kv[nm][c]
                elif nm in ("k_all", "v_all", "n_all"):
                    grp = range(4 * (c // 4), 4 * (c // 4) + 4)
                    src = {"k_all": "k_own", "v_all": "v_own", "n_all": "n_own"}[nm]
                    im[nm] = np.ascontiguousarray(np.concatenate([kv[src][r] for r in grp], axis=0))
                elif nm == "vones":
                    im[nm] = vones[c]
                elif nm == "rope_cos":
                    im[nm] = ropes[c][0]
                elif nm == "rope_sin":
                    im[nm] = ropes[c][1]
                else:
                    im[nm] = base_all[nm]
            in_maps.append(im)
        res = run_bass_kernel_spmd(nc, in_maps, core_ids=list(range(NC)))
        cur = [res.results[c]["x_out"] for c in range(NC)]
        for nm in ("qd", "k_own", "v_own", "n_own", "qn_st"):
            if nm in names and mk.t[nm][2] == "ExternalOutput":
                kv[nm] = [res.results[c][nm] for c in range(NC)]
    return _host_unshard_x(cur)


# version 04
# speedup vs baseline: 1.3438x; 1.3438x over previous
import numpy as np
import ml_dtypes
import concourse.bass as bass
import concourse.mybir as mybir
from concourse.bass_utils import run_bass_kernel_spmd

F32 = mybir.dt.float32
BF16 = mybir.dt.bfloat16
I32 = mybir.dt.int32
AF = mybir.ActivationFunctionType
ALU = mybir.AluOpType
BF = ml_dtypes.bfloat16

D = 4096
SEQ = 4096
BATCH = 2
NC = 8
TOK = 1024
TT = 512
NKC = 32
NEG = -1e30
EPS = 1e-5
ISQ = 1.0 / np.sqrt(128.0)


def _chunks_of_core(c):
    j = c % 4
    return (j, 7 - j)


def _rows_of_core(c):
    s = c // 4
    a, b = _chunks_of_core(c)
    r = []
    for g in (a, b):
        st = s * SEQ + g * TT
        r.append(np.arange(st, st + TT))
    return np.concatenate(r)


def _positions_of_core(c):
    a, b = _chunks_of_core(c)
    return np.concatenate([np.arange(a * TT, a * TT + TT),
                           np.arange(b * TT, b * TT + TT)])


def _rank_of_chunk(s, g):
    j = g if g < 4 else 7 - g
    return 4 * s + j, (0 if g < 4 else 1)


def _rope_tables():
    inv = 1.0 / (10000.0 ** (np.arange(0, 64, 2, dtype=np.float64) / 64.0))
    pos = np.arange(SEQ, dtype=np.float64)
    fr = pos[:, None] * inv[None, :]
    return np.cos(fr).astype(np.float32), np.sin(fr).astype(np.float32)



ENGS = ("sync", "vector", "scalar", "tensor", "gpsimd")


class Prog:

    def __init__(self, nc):
        self.nc = nc
        self.ops = {e: [] for e in ENGS}
        self.sems = {}
        self.cnt = {}

    def sem(self, name):
        if name not in self.sems:
            self.sems[name] = self.nc.alloc_semaphore(name)
            self.cnt[name] = 0
        return self.sems[name]

    def run(self, eng, fn, inc=None, by=1):
        val = None
        sem = None
        if inc is not None:
            sem = self.sem(inc)
            self.cnt[inc] += by
            val = self.cnt[inc]
        def closure(e, fn=fn, sem=sem, by=by):
            ins = fn(e)
            if sem is not None:
                ins.then_inc(sem, by)
        self.ops[eng].append(closure)
        return val

    def wait(self, eng, name, val):
        if val is None or val <= 0:
            return
        sem = self.sem(name)
        self.ops[eng].append(lambda e, s=sem, v=val: e.wait_ge(s, v))

    def now(self, name):
        self.sem(name)
        return self.cnt[name]

    def fence(self, eng, name):
        self.wait(eng, name, self.now(name))

    def barrier(self):
        for name, v in list(self.cnt.items()):
            if v > 0:
                for e in ENGS:
                    self.wait(e, name, v)

    def finish(self):
        with self.nc.Block() as block:
            for e in ENGS:
                fns = self.ops[e]
                def body(engine, fns=fns):
                    for f in fns:
                        f(engine)
                getattr(block, e)(body)


class Gate:

    def __init__(self, n):
        self.g = [[] for _ in range(n)]

    def acquire(self, P, eng, i):
        engs = eng if isinstance(eng, (list, tuple)) else [eng]
        for name, val in self.g[i]:
            for en in engs:
                P.wait(en, name, val)
        self.g[i] = []

    def release(self, i, name, val):
        self.g[i].append((name, val))



class MK:

    def __init__(self, mode_tensors, fused=True):
        nc = bass.Bass("TRN2", target_bir_lowering=False,
                       dynamic_dma_scratch_size=1024)
        self.nc = nc
        self.P = Prog(nc)
        self.fused = fused
        self.dram = {}
        self.t = mode_tensors
        for name, (shape, dt, kind) in mode_tensors.items():
            self.dram[name] = nc.dram_tensor(name, list(shape), dt, kind=kind)
        self._alloc_consts()
        self._alloc_sbuf()
        self._alloc_psum()

    def sd(self, name, L):
        return self.dram[f"{name}{L}"] if self.fused else self.dram[name]

    def _alloc_consts(self):
        nc = self.nc
        self.c_ident = nc.inline_tensor(np.eye(128, dtype=BF), "c_ident")
        self.c_ones_col = nc.inline_tensor(np.ones((128, 1), dtype=BF), "c_ones_col")
        self.c_ones_row = nc.inline_tensor(np.ones((1, 128), dtype=BF), "c_ones_row")
        self.c_ones_row_f = nc.inline_tensor(np.ones((1, 128), dtype=np.float32), "c_ones_row_f")
        self.c_one1_f = nc.inline_tensor(np.ones((1, 1), dtype=np.float32), "c_one1_f")
        mm = np.full((128, 896), NEG, dtype=np.float32)
        ii = np.arange(128)[:, None]
        cc = np.arange(896)[None, :]
        mm[ii <= cc - 384] = 0.0
        self.c_mask = nc.inline_tensor(mm, "c_mask")
        ex = np.zeros((8, 128), dtype=BF)
        for e in range(8):
            ex[e, e * 16:(e + 1) * 16] = 1.0
        self.c_expand = nc.inline_tensor(ex, "c_expand")

    def _alloc_sbuf(self):
        nc = self.nc
        self.arena = nc.alloc_sbuf_tensor("arena", [128, 227840], mybir.dt.uint8)
        a = [nc.lookup_mloc(self.arena).addr]
        self.arena_base = a[0]

        def at(name, shape, dt, align=32):
            a[0] = (a[0] + align - 1) // align * align
            h = nc.alloc_sbuf_tensor_at(name, list(shape), dt, offset=a[0])
            per = int(np.prod(shape[1:])) * mybir.dt.size(dt)
            a[0] += per
            return h

        self.x_res = at("x_res", [128, NKC * TOK], F32)
        self.reg_base = a[0]
        self.xbf = at("xbf", [128, 2 * NKC * TT], BF16)
        rb = self.reg_base
        self.kv_k = nc.alloc_sbuf_tensor_at("kv_k", [128, 36 * 128], BF16, offset=rb)
        self.kv_v = nc.alloc_sbuf_tensor_at("kv_v", [128, 36 * 132], BF16, offset=rb + 9216)
        self.fl_pr = nc.alloc_sbuf_tensor_at("fl_pr", [128, 4 * 512], BF16, offset=rb + 18752)
        self.fl_ct = nc.alloc_sbuf_tensor_at("fl_ct", [128, 512], BF16, offset=rb + 22848)
        self.mhat32 = nc.alloc_sbuf_tensor_at("mhat32", [128, 32], F32, offset=rb + 23872)
        self.fl_q = nc.alloc_sbuf_tensor_at("fl_q", [128, 32 * 512], BF16, offset=rb + 32768)
        self.p2_y2 = nc.alloc_sbuf_tensor_at("p2_y2", [128, 2 * 512], BF16, offset=rb + 32768)
        self.p2_i32 = nc.alloc_sbuf_tensor_at("p2_i32", [128, 512], I32, offset=rb + 34816)
        self.p2_s0 = nc.alloc_sbuf_tensor_at("p2_s0", [128, 512], F32, offset=rb + 36864)
        self.p2_s1 = nc.alloc_sbuf_tensor_at("p2_s1", [128, 512], F32, offset=rb + 38912)
        self.p2_a = nc.alloc_sbuf_tensor_at("p2_a", [128, 512], BF16, offset=rb + 40960)
        self.p2_b = nc.alloc_sbuf_tensor_at("p2_b", [128, 512], BF16, offset=rb + 41984)

        self.w_bf0 = at("w_bf0", [128, 2048], BF16)
        self.w_bf1 = at("w_bf1", [128, 2048], BF16)
        self.w_st = at("w_st", [128, 512], F32)
        self.bc0 = at("bc0", [128, 512], F32)
        self.bc1 = at("bc1", [128, 512], F32)
        self.rot_off = (a[0] + 31) // 32 * 32
        self.rot = at("rot", [128, 4 * 512], BF16)
        self.scr = at("scr", [128, 512], F32)
        self.scr2 = at("scr2", [128, 512], F32)
        self.bc2 = self.scr2
        self.hw_sb = self.rot
        self.rows = at("rows", [128, 512], F32)
        self.rcol = at("rcol", [128, 16], F32)
        self.sb_ident = at("sb_ident", [128, 128], BF16)
        self.sb_ones_col = at("sb_ones_col", [128, 1], BF16)
        self.sb_ones_row = at("sb_ones_row", [1, 128], BF16)
        self.sb_ones_row_f = at("sb_ones_row_f", [1, 128], F32)
        self.sb_one1_f = at("sb_one1_f", [1, 1], F32)
        self.sb_expand = at("sb_expand", [8, 128], BF16)
        self.sb_rope = at("sb_rope", [128, 2 * 256], F32)
        self.vec = at("vec", [128, 7 * 32], F32)
        self.rowbuf = at("rowbuf", [1, 128], F32)
        self.rowbuf_bf = at("rowbuf_bf", [1, 128], BF16)
        self.bb_sb = at("bb_sb", [128, 128], F32)
        self.qn = at("qn", [1, 64], F32)
        self.route_T = nc.alloc_sbuf_tensor_at("route_T", [8, 512], BF16, offset=self.rot_off + 2048)
        self.wg_sb = at("wg_sb", [128, 256], BF16)
        self.lg_sb = at("lg_sb", [128, 8], F32)
        self.small = at("small", [128, 16], F32)
        self.sb_mask = nc.alloc_sbuf_tensor_at("sb_mask", [128, 896], F32, offset=self.reg_base + 24064)
        self.vones = nc.alloc_sbuf_tensor_at("vones", [128, 64], F32, offset=self.reg_base + 27680)
        self.limit = a[0]
        assert a[0] <= self.arena_base + 227840, f"SBUF overflow: {a[0]}"

    def _alloc_psum(self):
        nc = self.nc
        self.ps = [nc.alloc_psum_tensor(f"ps{i}", [128, 512], F32) for i in range(6)]
        self.ps_tp = nc.alloc_psum_tensor("ps_tp", [128, 1024], BF16)
        self.ps_x = nc.alloc_psum_tensor("ps_x", [128, 512], F32)

    def load_consts(self):
        P = self.P
        pairs = [(self.sb_ident, self.c_ident),
                 (self.sb_ones_col, self.c_ones_col), (self.sb_ones_row, self.c_ones_row),
                 (self.sb_ones_row_f, self.c_ones_row_f), (self.sb_one1_f, self.c_one1_f),
                 (self.sb_expand, self.c_expand)]
        for sb, cc in pairs:
            P.run("sync", lambda e, sb=sb, cc=cc: e.dma_start(sb[:, :], cc[:, :]), inc="dconst", by=16)
        if "rope_cos" in self.dram:
            for i, nm in enumerate(("rope_cos", "rope_sin")):
                dr = self.dram[nm]
                P.run("sync", lambda e, i=i, dr=dr: e.dma_start(
                    self.sb_rope[:, i * 256:(i + 1) * 256].rearrange("p (ts f) -> p ts f", ts=8),
                    dr[:, :].rearrange("(ts p) f -> p ts f", p=128)), inc="dconst", by=16)

    def load_x(self):
        P = self.P
        x = self.dram["x_own"]
        for kc in range(NKC):
            P.run("sync", lambda e, kc=kc: e.dma_start(
                self.x_res[:, kc * TOK:(kc + 1) * TOK],
                x[kc * 128:(kc + 1) * 128, :]), inc="dconst", by=16)

    def store_x(self):
        P = self.P
        x = self.dram["x_out"]
        v = P.now("dve")
        P.wait("sync", "dve", v)
        for kc in range(NKC):
            P.run("sync", lambda e, kc=kc: e.dma_start(
                x[kc * 128:(kc + 1) * 128, :],
                self.x_res[:, kc * TOK:(kc + 1) * TOK]), inc="dout", by=16)
        P.wait("sync", "dout", P.now("dout"))

    def preload_vectors(self, phases):
        P = self.P
        if any(p.startswith("ffn0") for p in phases):
            for i, nm in enumerate(("f0_bias", "f0_scale", "f0_gamma", "f0_beta")):
                P.run("sync", lambda e, i=i, nm=nm: e.dma_start(
                    self.vec[:, i * 32:(i + 1) * 32], self.dram[nm][:, :]), inc="dconst", by=16)
            P.run("sync", lambda e: e.dma_start(self.small[0:1, 0:1], self.dram["f0_qs"][:, :]), inc="dconst", by=16)
        if any(p.startswith("ffn1") for p in phases):
            P.run("sync", lambda e: e.dma_start(
                self.vec[:, 6 * 32:7 * 32], self.dram["f1_bbase"][:, :]), inc="dconst", by=16)
        vtot = P.now("dconst")
        for eng in ("vector", "scalar", "tensor", "gpsimd", "sync"):
            P.wait(eng, "dconst", vtot)
        if any(p.startswith("ffn0") for p in phases):
            v = P.run("vector", lambda e: e.reciprocal(self.small[0:1, 1:2], self.small[0:1, 0:1]), inc="dve")
            P.wait("tensor", "dve", v)
            v = P.run("tensor", lambda e: e.matmul(self.ps_x[:, 0:1], self.sb_ones_row_f[:1, :], self.small[0:1, 1:2], start=True, stop=True), inc="pe")
            P.wait("vector", "pe", v)
            v = P.run("vector", lambda e: e.tensor_copy(self.small[:, 2:3], self.ps_x[:, 0:1]), inc="dve")
            P.wait("vector", "dve", v)
            v = P.run("vector", lambda e: e.tensor_scalar(self.vec[:, 4 * 32:5 * 32], self.vec[:, 2 * 32:3 * 32], self.small[:, 2:3], None, ALU.mult), inc="dve")
            v = P.run("vector", lambda e: e.tensor_scalar(self.vec[:, 5 * 32:6 * 32], self.vec[:, 3 * 32:4 * 32], self.small[:, 2:3], None, ALU.mult), inc="dve")
            P.wait("scalar", "dve", v)

    def stats_cast(self, tile, want_rcol=False, kscale=False):
        P = self.P
        if self.resid_ready is not None:
            nm, vv = self.resid_ready
            P.wait("scalar", nm, vv)
        ps_su = self.ps[2 + 2 * tile]
        ps_sq = self.ps[3 + 2 * tile]
        rslot = tile * 32
        for kc in range(NKC):
            xa = self.x_res[:, kc * TOK + tile * TT: kc * TOK + tile * TT + TT]
            r0 = self.rot[:, (2 * kc % 4) * 512:(2 * kc % 4) * 512 + 512]
            r1 = self.rot[:, (2 * kc % 4 + 1) * 512:(2 * kc % 4 + 1) * 512 + 512]
            self.rot_gate.acquire(P, "scalar", 2 * kc % 4)
            self.rot_gate.acquire(P, "scalar", (2 * kc % 4) + 1)
            P.run("scalar", lambda e, xa=xa, r0=r0: e.copy(r0[:, :], xa), inc="act")
            va = P.run("scalar", lambda e, xa=xa, r1=r1: e.activation(r1[:, :], xa, AF.Square), inc="act")
            P.wait("tensor", "act", va)
            P.run("tensor", lambda e, r0=r0, ps_su=ps_su, kc=kc: e.matmul(
                ps_su[0:1, :], self.sb_ones_col[:, :], r0[:, :],
                start=(kc == 0), stop=(kc == NKC - 1)), inc="pe")
            vm = P.run("tensor", lambda e, r1=r1, ps_sq=ps_sq, kc=kc: e.matmul(
                ps_sq[0:1, :], self.sb_ones_col[:, :], r1[:, :],
                start=(kc == 0), stop=(kc == NKC - 1)), inc="pe")
            self.rot_gate.release(2 * kc % 4, "pe", vm)
            self.rot_gate.release(2 * kc % 4 + 1, "pe", vm)
        vpe = P.now("pe")
        P.wait("vector", "pe", vpe)
        rows = self.rows
        srow = self.scr
        P.fence("vector", "dve")
        v1 = P.run("vector", lambda e: e.tensor_scalar(rows[0:1, :], ps_su[0:1, :], 1.0 / D, None, ALU.mult), inc="dve")
        v2 = P.run("vector", lambda e: e.tensor_scalar(srow[0:1, :], ps_sq[0:1, :], 1.0 / D, EPS, ALU.mult, ALU.add), inc="dve")
        P.wait("tensor", "dve", v1)
        v7 = P.run("tensor", lambda e: e.matmul(self.ps_x[:, 0:512], self.sb_ones_row_f[:1, :], rows[0:1, :], start=True, stop=True), inc="pe")
        P.wait("vector", "pe", v7)
        v8 = P.run("vector", lambda e: e.tensor_copy(self.bc2[:, :], self.ps_x[:, 0:512]), inc="dve")
        P.wait("vector", "dve", v8)
        v3 = P.run("vector", lambda e: e.tensor_tensor(rows[0:1, :], rows[0:1, :], rows[0:1, :], ALU.mult), inc="dve")
        P.wait("vector", "dve", v3)
        v4 = P.run("vector", lambda e: e.tensor_tensor(rows[0:1, :], srow[0:1, :], rows[0:1, :], ALU.subtract), inc="dve")
        P.wait("scalar", "dve", v4)
        v5 = P.run("scalar", lambda e: e.activation(srow[0:1, :], rows[0:1, :], AF.Sqrt), inc="act")
        P.wait("vector", "act", v5)
        v6 = P.run("vector", lambda e: e.reciprocal(rows[0:1, :], srow[0:1, :]), inc="dve")
        bcr = (self.bc0, self.bc1)[tile]
        P.wait("tensor", "dve", v6)
        v9 = P.run("tensor", lambda e: e.matmul(self.ps_x[:, 0:512], self.sb_ones_row_f[:1, :], rows[0:1, :], start=True, stop=True), inc="pe")
        P.wait("vector", "pe", v9)
        va = P.run("vector", lambda e, bcr=bcr: e.tensor_copy(bcr[:, :], self.ps_x[:, 0:512]), inc="dve")
        if want_rcol:
            P.wait("tensor", "dve", va)
            for ts_ in range(4):
                vt = P.run("tensor", lambda e, ts_=ts_: e.transpose(
                    self.ps_x[:, 0:1],
                    rows[0:1, ts_ * 128:(ts_ + 1) * 128],
                    self.sb_one1_f[:1, :1]), inc="pe")
                P.wait("vector", "pe", vt)
                col = tile * 4 + ts_
                vc = P.run("vector", lambda e, col=col: e.tensor_copy(self.rcol[:, col:col + 1], self.ps_x[:, 0:1]), inc="dve")
                if kscale:
                    P.wait("vector", "dve", vc)
                    P.run("vector", lambda e, col=col: e.tensor_scalar(
                        self.rcol[:, 8 + col:9 + col], self.rcol[:, col:col + 1], ISQ, None, ALU.mult), inc="dve")
                P.wait("tensor", "dve", P.now("dve"))
        P.wait("vector", "dve", va)
        last = None
        for kc in range(NKC):
            xa = self.x_res[:, kc * TOK + tile * TT: kc * TOK + tile * TT + TT]
            xb = self.xbf[:, (tile * NKC + kc) * TT:(tile * NKC + kc) * TT + TT]
            last = P.run("vector", lambda e, xa=xa, xb=xb: e.tensor_tensor(xb, xa, self.bc2[:, :], ALU.subtract), inc="dve")
        self.cast_done = last

    def stream_w(self, dr, o, half, buf, eng_cast="scalar", cols=128, cast_scale=None):
        P = self.P
        vlast = None
        for q in range(2):
            st = self.w_st[:, (q % 2) * 512:(q % 2) * 512 + 512]
            if self.wst_gate[q] is not None:
                nm, vv = self.wst_gate[q]
                P.wait("sync", nm, vv)
                self.wst_gate[q] = None
            rows0 = half * 2048 + q * 1024
            vd = P.run("sync", lambda e, dr=dr, o=o, rows0=rows0, st=st, cols=cols: e.dma_start(
                st.rearrange("p (kc o) -> p kc o", o=cols) if cols != 512 else st,
                dr[rows0:rows0 + 1024, o * cols:(o + 1) * cols].rearrange("(kc p) o -> p kc o", p=128) if cols != 512 else
                dr[rows0:rows0 + 1024, o * cols:(o + 1) * cols].rearrange("(kc p) o -> p (kc o)", p=128)), inc="dsync", by=16)
            eng = eng_cast
            P.wait(eng, "dsync", vd)
            dst = buf[:, q * 1024:(q + 1) * 1024]
            if cast_scale is None:
                if eng == "scalar":
                    vlast = P.run(eng, lambda e, dst=dst, st=st: e.copy(dst, st), inc="act")
                else:
                    vlast = P.run(eng, lambda e, dst=dst, st=st: e.tensor_copy(dst, st), inc="dve" if eng == "vector" else "pool")
            else:
                vlast = P.run(eng, lambda e, dst=dst, st=st: e.activation(dst, st, AF.Copy, scale=cast_scale), inc="act")
            semn = {"scalar": "act", "vector": "dve", "gpsimd": "pool"}[eng]
            self.wst_gate[q] = (semn, vlast)
        return vlast

    def ffn0(self):
        P = self.P
        nc = self.nc
        self.rot_gate = Gate(4)
        self.wst_gate = [None, None]
        self.stats_cast(0)
        self.stats_cast(1)
        castA_castB_done = self.cast_done
        tern = self.dram["tern"]
        y2d = self.dram["y2"]

        P.wait("tensor", "dve", castA_castB_done)
        wbufs = (self.w_bf0, self.w_bf1)
        self.rot_gate = Gate(4)
        nmm = 0
        for o in range(NKC):
            for half in range(2):
                buf = wbufs[half]
                if self.wbf_gate[half] is not None:
                    nm, vv = self.wbf_gate[half]
                    P.wait("sync", nm, vv)
                    self.wbf_gate[half] = None
                rows0 = half * 2048
                P.run("sync", lambda e, rows0=rows0, o=o, buf=buf: e.dma_start(
                    buf[:, :].rearrange("p (kc o) -> p kc o", o=128),
                    tern[rows0:rows0 + 2048, o * 128:(o + 1) * 128].rearrange("(kc p) o -> p kc o", p=128)), inc="dwt", by=16)
            vw = P.now("dwt")
            for ti in range(2):
                bank = self.ps[nmm % 2]
                P.wait("tensor", "dwt", vw)
                if self.bank_gate[nmm % 2] is not None:
                    nm, vv = self.bank_gate[nmm % 2]
                    P.wait("tensor", nm, vv)
                    self.bank_gate[nmm % 2] = None
                vmm = None
                for kc in range(NKC):
                    w = wbufs[kc // 16][:, (kc % 16) * 128:(kc % 16) * 128 + 128]
                    xb = self.xbf[:, (ti * NKC + kc) * TT:(ti * NKC + kc) * TT + TT]
                    vmm = P.run("tensor", lambda e, bank=bank, w=w, xb=xb, kc=kc: e.matmul(
                        bank[:, :], w, xb, start=(kc == 0), stop=(kc == NKC - 1)),
                        inc="pe" if kc == NKC - 1 else None)
                self.wbf_gate[0] = ("pe", vmm)
                self.wbf_gate[1] = ("pe", vmm)
                P.wait("vector", "pe", vmm)
                bcr = (self.bc0, self.bc1)[ti]
                ri = nmm % 4
                self.rot_gate.acquire(P, "vector", ri)
                rt = self.rot[:, ri * 512:(ri + 1) * 512]
                scr = self.scr
                P.fence("vector", "dve")
                v1 = P.run("vector", lambda e, bank=bank, bcr=bcr, scr=scr: e.tensor_tensor(scr[:, :], bank[:, :], bcr[:, :], ALU.mult), inc="dve")
                self.bank_gate[nmm % 2] = ("dve", v1)
                P.wait("vector", "dve", v1)
                v2 = P.run("vector", lambda e, scr=scr, rt=rt, o=o: e.tensor_scalar(
                    rt, scr[:, :], self.vec[:, 0 * 32 + o:0 * 32 + o + 1], self.vec[:, 1 * 32 + o:1 * 32 + o + 1],
                    ALU.add, ALU.mult), inc="dve")
                P.wait("scalar", "dve", v2)
                ri2 = (ri + 2) % 4
                self.rot_gate.acquire(P, "scalar", ri2)
                rt2 = self.rot[:, ri2 * 512:(ri2 + 1) * 512]
                v3 = P.run("scalar", lambda e, rt=rt, rt2=rt2: e.activation(rt2, rt, AF.Square), inc="act")
                ps_su = self.ps[2 + 2 * ti]
                ps_sq = self.ps[3 + 2 * ti]
                P.wait("tensor", "dve", v2)
                P.run("tensor", lambda e, ps_su=ps_su, rt=rt, o=o: e.matmul(
                    ps_su[0:1, :], self.sb_ones_col[:, :], rt, start=(o == 0), stop=(o == NKC - 1)), inc="pe")
                P.wait("tensor", "act", v3)
                vst = P.run("tensor", lambda e, ps_sq=ps_sq, rt2=rt2, o=o: e.matmul(
                    ps_sq[0:1, :], self.sb_ones_col[:, :], rt2, start=(o == 0), stop=(o == NKC - 1)), inc="pe")
                self.rot_gate.release(ri2, "pe", vst)
                vd = P.run("scalar", lambda e, rt=rt, o=o, ti=ti: e.dma_start(
                    y2d[o * 128:(o + 1) * 128, ti * TT:(ti + 1) * TT], rt), inc=f"dr{ri}", by=16)
                self.rot_gate.release(ri, f"dr{ri}", vd)
                self.rot_gate.release(ri, "pe", vst)
                self.rot_gate.release(ri, "act", v3)
                nmm += 1

        for ti in range(2):
            ps_su = self.ps[2 + 2 * ti]
            ps_sq = self.ps[3 + 2 * ti]
            rows = self.rows
            srow = self.scr
            vpe = P.now("pe")
            P.wait("vector", "pe", vpe)
            P.fence("vector", "dve")
            v1 = P.run("vector", lambda e, ps_su=ps_su: e.tensor_scalar(rows[0:1, :], ps_su[0:1, :], 1.0 / D, None, ALU.mult), inc="dve")
            v2 = P.run("vector", lambda e, ps_sq=ps_sq: e.tensor_scalar(srow[0:1, :], ps_sq[0:1, :], 1.0 / D, EPS, ALU.mult, ALU.add), inc="dve")
            P.wait("tensor", "dve", v1)
            v7 = P.run("tensor", lambda e: e.matmul(self.ps_x[:, 0:512], self.sb_ones_row_f[:1, :], rows[0:1, :], start=True, stop=True), inc="pe")
            P.wait("vector", "pe", v7)
            v8 = P.run("vector", lambda e: e.tensor_copy(self.bc0[:, :], self.ps_x[:, 0:512]), inc="dve")
            P.wait("vector", "dve", v8)
            v3 = P.run("vector", lambda e: e.tensor_tensor(rows[0:1, :], rows[0:1, :], rows[0:1, :], ALU.mult), inc="dve")
            P.wait("vector", "dve", v3)
            v4 = P.run("vector", lambda e: e.tensor_tensor(rows[0:1, :], srow[0:1, :], rows[0:1, :], ALU.subtract), inc="dve")
            P.wait("scalar", "dve", v4)
            v5 = P.run("scalar", lambda e: e.activation(srow[0:1, :], rows[0:1, :], AF.Sqrt), inc="act")
            P.wait("vector", "act", v5)
            v6 = P.run("vector", lambda e: e.reciprocal(rows[0:1, :], srow[0:1, :]), inc="dve")
            P.wait("tensor", "dve", v6)
            v9 = P.run("tensor", lambda e: e.matmul(self.ps_x[:, 0:512], self.sb_ones_row_f[:1, :], rows[0:1, :], start=True, stop=True), inc="pe")
            P.wait("vector", "pe", v9)
            va = P.run("vector", lambda e: e.tensor_copy(self.bc1[:, :], self.ps_x[:, 0:512]), inc="dve")
            P.wait("vector", "dve", va)
            if ti == 0:
                for rr in range(4):
                    P.wait("sync", f"dr{rr}", P.now(f"dr{rr}"))
            for o in range(NKC):
                yb = self.p2_y2[:, (o % 2) * 512:(o % 2) * 512 + 512]
                if self.p2_gate[o % 2] is not None:
                    nm, vv = self.p2_gate[o % 2]
                    P.wait("sync", nm, vv)
                    self.p2_gate[o % 2] = None
                vl = P.run("sync", lambda e, yb=yb, o=o, ti=ti: e.dma_start(
                    yb, y2d[o * 128:(o + 1) * 128, ti * TT:(ti + 1) * TT]), inc=f"dy{o % 2}", by=16)
                P.wait("vector", f"dy{o % 2}", vl)
                P.fence("vector", "dve")
                v1 = P.run("vector", lambda e, yb=yb: e.tensor_tensor(self.p2_s0[:, :], yb, self.bc0[:, :], ALU.subtract), inc="dve")
                P.wait("vector", "dve", v1)
                v2 = P.run("vector", lambda e: e.tensor_tensor(self.p2_s0[:, :], self.p2_s0[:, :], self.bc1[:, :], ALU.mult), inc="dve")
                P.wait("scalar", "dve", v2)
                v3 = P.run("scalar", lambda e, o=o: e.activation(
                    self.p2_s1[:, :], self.p2_s0[:, :], AF.Tanh,
                    bias=self.vec[:, 5 * 32 + o:5 * 32 + o + 1], scale=self.vec[:, 4 * 32 + o:4 * 32 + o + 1]), inc="act")
                P.wait("vector", "act", v3)
                v4 = P.run("vector", lambda e: e.tensor_scalar(self.p2_i32[:, :], self.p2_s1[:, :], 127.0, None, ALU.mult), inc="dve")
                P.wait("vector", "dve", v4)
                v5 = P.run("vector", lambda e: e.tensor_copy(self.p2_s0[:, :], self.p2_i32[:, :]), inc="dve")
                P.wait("vector", "dve", v5)
                v6 = P.run("vector", lambda e: e.scalar_tensor_tensor(self.p2_s1[:, :], self.p2_s1[:, :], 127.0, self.p2_s0[:, :], ALU.mult, ALU.subtract), inc="dve")
                P.wait("vector", "dve", v6)
                v7 = P.run("vector", lambda e: e.tensor_single_scalar(self.p2_a[:, :], self.p2_s1[:, :], 0.5, ALU.is_gt), inc="dve")
                v8 = P.run("vector", lambda e: e.tensor_single_scalar(self.p2_b[:, :], self.p2_s1[:, :], -0.5, ALU.is_lt), inc="dve")
                P.wait("vector", "dve", v8)
                v9 = P.run("vector", lambda e: e.tensor_tensor(self.p2_s1[:, :], self.p2_a[:, :], self.p2_b[:, :], ALU.subtract), inc="dve")
                P.wait("vector", "dve", v9)
                va2 = P.run("vector", lambda e: e.tensor_tensor(self.p2_s0[:, :], self.p2_s0[:, :], self.p2_s1[:, :], ALU.add), inc="dve")
                P.wait("vector", "dve", va2)
                xa = self.x_res[:, o * TOK + ti * TT: o * TOK + ti * TT + TT]
                vb = P.run("vector", lambda e, xa=xa: e.scalar_tensor_tensor(xa, self.p2_s0[:, :], 0.5 / 127.0, xa, ALU.mult, ALU.add), inc="dve")
                self.p2_gate[o % 2] = ("dve", vb)
        self.resid_ready = ("dve", P.now("dve"))


    def load_wslice(self, drname, o, ocols=128):
        P = self.P
        dr = self.dram[drname]
        for half in range(2):
            buf = (self.w_bf0, self.w_bf1)[half]
            if self.wbf_gate[half] is not None:
                nm, vv = self.wbf_gate[half]
                P.wait("sync", nm, vv)
                self.wbf_gate[half] = None
            rows0 = half * 2048
            P.run("sync", lambda e, dr=dr, rows0=rows0, o=o, buf=buf, ocols=ocols: e.dma_start(
                buf[:, :16 * ocols].rearrange("p (kc o) -> p kc o", o=ocols),
                dr[rows0:rows0 + 2048, o * ocols:(o + 1) * ocols].rearrange("(kc p) o -> p kc o", p=128)), inc="dwt", by=16)
        return P.now("dwt")

    def wchunk(self, kc, ocols=128):
        return (self.w_bf0, self.w_bf1)[kc // 16][:, (kc % 16) * ocols:(kc % 16 + 1) * ocols]

    def att_pre(self, L):
        P = self.P
        self.rot_gate = Gate(4)
        self.stats_cast(0, want_rcol=True, kscale=(L == 1))
        self.stats_cast(1, want_rcol=True, kscale=(L == 1))
        castdone = self.cast_done
        P.wait("tensor", "dve", castdone)
        P.fence("vector", "dve")
        P.run("vector", lambda e: e.memset(self.qn[0:1, :], 0.0), inc="dve")
        vz = P.run("vector", lambda e: e.memset(self.small[0:1, 4:8], 0.0), inc="dve")
        P.wait("vector", "dve", vz)
        if L == 0:
            self._pre_l0()
        else:
            self._pre_l1()
        vd = P.now("dve")
        P.wait("scalar", "dve", vd)
        vo = P.run("scalar", lambda e: e.dma_start(self.sd("n_own", L)[:, :], self.small[0:1, 4:6]), inc="dnrm", by=16)
        vo2 = P.run("scalar", lambda e: e.dma_start(self.sd("qn_st", L)[:, :], self.qn[0:1, 0:64]), inc="dnrm", by=16)
        P.wait("scalar", "dnrm", vo2)
        for rr in range(4):
            P.wait("scalar", f"dr{rr}", P.now(f"dr{rr}"))

    def _qnorm(self, rot_fm, width, dest):
        P = self.P
        ri2 = self.rot_gate_acquire_next()
        rt2 = self.rot[:, ri2 * 512:ri2 * 512 + 512]
        v3 = P.run("scalar", lambda e, rot_fm=rot_fm, rt2=rt2, width=width: e.activation(rt2[:, :width], rot_fm, AF.Square), inc="act")
        P.wait("tensor", "act", v3)
        P.wait("tensor", "dve", self.qn_gate)
        vm = P.run("tensor", lambda e, rt2=rt2, width=width: e.matmul(
            self.ps[2][0:1, :width], self.sb_ones_col[:, :], rt2[:, :width], start=True, stop=True), inc="pe")
        self.rot_gate.release(ri2, "pe", vm)
        P.wait("vector", "pe", vm)
        P.fence("vector", "dve")
        v4 = P.run("vector", lambda e, width=width: e.reduce_max(self.small[0:1, 3:4], self.ps[2][0:1, :width], axis=mybir.AxisListType.X), inc="dve")
        P.wait("vector", "dve", v4)
        v5 = P.run("vector", lambda e, dest=dest: e.tensor_tensor(
            dest, dest, self.small[0:1, 3:4], ALU.max), inc="dve")
        self.qn_gate = v4
        return v5

    def rot_gate_acquire_next(self):
        pool = getattr(self, "_rot_pool", (0, 1, 2, 3))
        ri = pool[self._rot_next % len(pool)]
        self._rot_next += 1
        self.rot_gate.acquire(self.P, ("vector", "scalar"), ri)
        return ri

    def _pre_l0(self):
        P = self.P
        qd = self.sd("qd", 0)
        self._rot_next = 0
        nmm = 0
        for o in range(NKC):
            vw = self.load_wslice("l0_wq", o)
            for ti in range(2):
                bank = self.ps[nmm % 2]
                P.wait("tensor", "dwt", vw)
                if self.bank_gate[nmm % 2] is not None:
                    nm, vv = self.bank_gate[nmm % 2]
                    P.wait("tensor", nm, vv)
                    self.bank_gate[nmm % 2] = None
                vmm = None
                for kc in range(NKC):
                    w = self.wchunk(kc)
                    xb = self.xbf[:, (ti * NKC + kc) * TT:(ti * NKC + kc) * TT + TT]
                    vmm = P.run("tensor", lambda e, bank=bank, w=w, xb=xb, kc=kc: e.matmul(
                        bank[:, :], w, xb, start=(kc == 0), stop=(kc == NKC - 1)),
                        inc="pe" if kc == NKC - 1 else None)
                self.wbf_gate[0] = ("pe", vmm)
                self.wbf_gate[1] = ("pe", vmm)
                bcr = (self.bc0, self.bc1)[ti]
                ri = self.rot_gate_acquire_next()
                rt = self.rot[:, ri * 512:ri * 512 + 512]
                P.wait("vector", "pe", vmm)
                P.fence("vector", "dve")
                v2 = P.run("vector", lambda e, bank=bank, bcr=bcr, rt=rt: e.tensor_tensor(rt, bank[:, :], bcr[:, :], ALU.mult), inc="dve")
                self.bank_gate[nmm % 2] = ("dve", v2)
                P.wait("scalar", "dve", v2)
                vq = self._qnorm(rt, 512, self.qn[0:1, ti * 32 + o:ti * 32 + o + 1])
                vd = P.run("scalar", lambda e, rt=rt, o=o, ti=ti, qd=qd: e.dma_start(
                    qd[o * 128:(o + 1) * 128, ti * TT:(ti + 1) * TT], rt), inc=f"dr{ri}", by=16)
                self.rot_gate.release(ri, f"dr{ri}", vd)
                nmm += 1
        vw = self.load_wslice("l0_wk", 0)
        for ti in range(2):
            bank = self.ps[nmm % 2]
            P.wait("tensor", "dwt", vw)
            if self.bank_gate[nmm % 2] is not None:
                nm, vv = self.bank_gate[nmm % 2]
                P.wait("tensor", nm, vv)
                self.bank_gate[nmm % 2] = None
            vmm = None
            for kc in range(NKC):
                w = self.wchunk(kc)
                xb = self.xbf[:, (ti * NKC + kc) * TT:(ti * NKC + kc) * TT + TT]
                vmm = P.run("tensor", lambda e, bank=bank, w=w, xb=xb, kc=kc: e.matmul(
                    bank[:, :], w, xb, start=(kc == 0), stop=(kc == NKC - 1)),
                    inc="pe" if kc == NKC - 1 else None)
            self.wbf_gate[0] = ("pe", vmm)
            self.wbf_gate[1] = ("pe", vmm)
            bcr = (self.bc0, self.bc1)[ti]
            ri = self.rot_gate_acquire_next()
            rt = self.rot[:, ri * 512:ri * 512 + 512]
            P.wait("vector", "pe", vmm)
            P.fence("vector", "dve")
            vs = P.run("vector", lambda e, bcr=bcr: e.tensor_scalar(self.scr2[:, :], bcr[:, :], ISQ, None, ALU.mult), inc="dve")
            P.wait("vector", "dve", vs)
            v2 = P.run("vector", lambda e, bank=bank, rt=rt: e.tensor_tensor(rt, bank[:, :], self.scr2[:, :], ALU.mult), inc="dve")
            self.bank_gate[nmm % 2] = ("dve", v2)
            P.wait("scalar", "dve", v2)
            vq = self._qnorm(rt, 512, self.small[0:1, 4:5])
            vd = P.run("scalar", lambda e, rt=rt, ti=ti: e.dma_start(
                self.sd("k_own", 0)[:, ti * TT:(ti + 1) * TT], rt), inc=f"dr{ri}", by=16)
            self.rot_gate.release(ri, f"dr{ri}", vd)
            nmm += 1
        vw = self.load_wslice("l0_wv", 0)
        for ti in range(2):
            for ts_ in range(4):
                bank = self.ps[nmm % 2]
                P.wait("tensor", "dwt", vw)
                if self.bank_gate[nmm % 2] is not None:
                    nm, vv = self.bank_gate[nmm % 2]
                    P.wait("tensor", nm, vv)
                    self.bank_gate[nmm % 2] = None
                vmm = None
                for kc in range(NKC):
                    xsl = self.xbf[:, (ti * NKC + kc) * TT + ts_ * 128:(ti * NKC + kc) * TT + ts_ * 128 + 128]
                    w = self.wchunk(kc)
                    vmm = P.run("tensor", lambda e, bank=bank, w=w, xsl=xsl, kc=kc: e.matmul(
                        bank[:, 0:128], xsl, w, start=(kc == 0), stop=(kc == NKC - 1)),
                        inc="pe" if kc == NKC - 1 else None)
                self.wbf_gate[0] = ("pe", vmm)
                self.wbf_gate[1] = ("pe", vmm)
                col = ti * 4 + ts_
                ri = self.rot_gate_acquire_next()
                rt = self.rot[:, ri * 512:ri * 512 + 512]
                P.wait("vector", "pe", vmm)
                P.fence("vector", "dve")
                v2 = P.run("vector", lambda e, bank=bank, rt=rt, col=col: e.tensor_scalar(
                    rt[:, 0:128], bank[:, 0:128], self.rcol[:, col:col + 1], None, ALU.mult), inc="dve")
                self.bank_gate[nmm % 2] = ("dve", v2)
                P.wait("scalar", "dve", v2)
                vd = P.run("scalar", lambda e, rt=rt, ti=ti, ts_=ts_: e.dma_start(
                    self.sd("v_own", 0)[ti * TT + ts_ * 128:ti * TT + ts_ * 128 + 128, :], rt[:, 0:128]), inc=f"dr{ri}", by=16)
                self.rot_gate.release(ri, f"dr{ri}", vd)
                nmm += 1

    def _pre_l1(self):
        P = self.P
        qd = self.sd("qd", 1)
        self._rot_next = 0
        nmm = 0
        self.rowbuf_gate = None
        for o in range(36):
            vw = self.load_wslice("l1_qkvw", o)
            if self.rowbuf_gate is not None:
                P.wait("sync", *self.rowbuf_gate)
            vb = P.run("sync", lambda e, o=o: e.dma_start(self.rowbuf[0:1, :], self.dram["l1_qkvb"][0:1, o * 128:(o + 1) * 128]), inc="dbr", by=16)
            P.wait("vector", "dbr", vb)
            P.fence("vector", "dve")
            sc = ISQ if o in (32, 33) else 1.0
            vc = P.run("vector", lambda e, sc=sc: e.tensor_scalar(self.rowbuf_bf[0:1, :], self.rowbuf[0:1, :], sc, None, ALU.mult), inc="dve")
            self.rowbuf_gate = ("dve", vc)
            P.wait("tensor", "dve", vc)
            if self.bbsb_gate:
                P.wait("tensor", "dve", self.bbsb_gate)
            vbb = P.run("tensor", lambda e: e.matmul(self.ps_x[:, 0:128], self.sb_ones_row[:1, :], self.rowbuf_bf[0:1, :], start=True, stop=True), inc="pe")
            P.wait("vector", "pe", vbb)
            P.fence("vector", "dve")
            vbc = P.run("vector", lambda e: e.tensor_copy(self.bb_sb[:, :], self.ps_x[:, 0:128]), inc="dve")
            self.bbsb_gate = vbc
            for ti in range(2):
                for ts_ in range(4):
                    bank = self.ps[nmm % 2]
                    P.wait("tensor", "dwt", vw)
                    P.wait("tensor", "dve", vbc)
                    if self.bank_gate[nmm % 2] is not None:
                        nm, vv = self.bank_gate[nmm % 2]
                        P.wait("tensor", nm, vv)
                        self.bank_gate[nmm % 2] = None
                    vmm = None
                    for kc in range(NKC):
                        xsl = self.xbf[:, (ti * NKC + kc) * TT + ts_ * 128:(ti * NKC + kc) * TT + ts_ * 128 + 128]
                        w = self.wchunk(kc)
                        vmm = P.run("tensor", lambda e, bank=bank, w=w, xsl=xsl, kc=kc: e.matmul(
                            bank[:, 0:128], xsl, w, start=(kc == 0), stop=(kc == NKC - 1)),
                            inc="pe" if kc == NKC - 1 else None)
                    self.wbf_gate[0] = ("pe", vmm)
                    self.wbf_gate[1] = ("pe", vmm)
                    col = (ti * 4 + ts_) + (8 if o in (32, 33) else 0)
                    P.wait("vector", "pe", vmm)
                    P.fence("vector", "dve")
                    v1 = P.run("vector", lambda e, bank=bank, col=col: e.tensor_scalar(
                        self.scr[:, 0:128], bank[:, 0:128], self.rcol[:, col:col + 1], None, ALU.mult), inc="dve")
                    self.bank_gate[nmm % 2] = ("dve", v1)
                    P.wait("vector", "dve", v1)
                    v2 = P.run("vector", lambda e: e.tensor_tensor(self.scr2[:, 0:128], self.scr[:, 0:128], self.bb_sb[:, :], ALU.add), inc="dve")
                    P.wait("vector", "dve", v2)
                    ri = self.rot_gate_acquire_next()
                    rt = self.rot[:, ri * 512:ri * 512 + 512]
                    tsub_abs = ti * 4 + ts_
                    if o < 34:
                        self._rope(ri, tsub_abs)
                        vrp = P.now("dve")
                        P.wait("tensor", "dve", vrp)
                        vtp = P.run("tensor", lambda e, rt=rt: e.transpose(
                            self.ps_tp[:, 0:128], rt[:, 0:128], self.sb_ident[:, :]), inc="pe")
                        ri2 = self.rot_gate_acquire_next()
                        rt2 = self.rot[:, ri2 * 512:ri2 * 512 + 512]
                        P.wait("vector", "pe", vtp)
                        P.fence("vector", "dve")
                        vcp = P.run("vector", lambda e, rt2=rt2: e.tensor_copy(rt2[:, 0:128], self.ps_tp[:, 0:128]), inc="dve")
                        P.wait("scalar", "dve", vcp)
                        if o < 32:
                            vq = self._qnorm(rt2[:, 0:128], 128, self.qn[0:1, ti * 32 + o:ti * 32 + o + 1])
                            vd = P.run("scalar", lambda e, rt2=rt2, o=o, ti=ti, ts_=ts_, qd=qd: e.dma_start(
                                qd[o * 128:(o + 1) * 128, ti * TT + ts_ * 128:ti * TT + ts_ * 128 + 128], rt2[:, 0:128]), inc=f"dr{ri2}", by=16)
                        else:
                            kh = o - 32
                            vq = self._qnorm(rt2[:, 0:128], 128, self.small[0:1, 4 + kh:5 + kh])
                            vd = P.run("scalar", lambda e, rt2=rt2, kh=kh, ti=ti, ts_=ts_: e.dma_start(
                                self.sd("k_own", 1)[kh * 128:(kh + 1) * 128, ti * TT + ts_ * 128:ti * TT + ts_ * 128 + 128], rt2[:, 0:128]), inc=f"dr{ri2}", by=16)
                        self.rot_gate.release(ri2, f"dr{ri2}", vd)
                        self.rot_gate.release(ri, "pe", vtp)
                    else:
                        dh = o - 34
                        vcv = P.run("vector", lambda e, rt=rt: e.tensor_copy(rt[:, 0:128], self.scr2[:, 0:128]), inc="dve")
                        P.wait("scalar", "dve", vcv)
                        vd = P.run("scalar", lambda e, rt=rt, dh=dh, ti=ti, ts_=ts_: e.dma_start(
                            self.sd("v_own", 1)[ti * TT + ts_ * 128:ti * TT + ts_ * 128 + 128, dh * 128:(dh + 1) * 128], rt[:, 0:128]), inc=f"dr{ri}", by=16)
                        self.rot_gate.release(ri, f"dr{ri}", vd)
                    nmm += 1

    def _rope(self, ri, tsub_abs):
        P = self.P
        base = ri * 512
        rt = self.rot[:, base:base + 512]
        cs = self.sb_rope[:, tsub_abs * 32:tsub_abs * 32 + 32]
        sn = self.sb_rope[:, 256 + tsub_abs * 32:256 + tsub_abs * 32 + 32]
        s2 = self.scr2
        scr = self.scr

        def xap(t, off, stride, count, width):
            return bass.AP(t, off, [[width, 128], [stride, count]])

        x1 = xap(s2, 0, 2, 32, 512)
        x2 = xap(s2, 1, 2, 32, 512)
        o1 = xap(self.rot, base + 0, [2, 32][0], 32, 2048)
        o2 = xap(self.rot, base + 1, 2, 32, 2048)
        P.fence("vector", "dve")
        va = P.run("vector", lambda e: e.tensor_tensor(scr[:, 0:32], x1, cs, ALU.mult), inc="dve")
        vb = P.run("vector", lambda e: e.tensor_tensor(scr[:, 32:64], x2, sn, ALU.mult), inc="dve")
        P.wait("vector", "dve", vb)
        vc = P.run("vector", lambda e: e.tensor_tensor(o1, scr[:, 0:32], scr[:, 32:64], ALU.subtract), inc="dve")
        P.wait("vector", "dve", vc)
        vd = P.run("vector", lambda e: e.tensor_tensor(scr[:, 0:32], x2, cs, ALU.mult), inc="dve")
        ve = P.run("vector", lambda e: e.tensor_tensor(scr[:, 32:64], x1, sn, ALU.mult), inc="dve")
        P.wait("vector", "dve", ve)
        vf = P.run("vector", lambda e: e.tensor_tensor(o2, scr[:, 0:32], scr[:, 32:64], ALU.add), inc="dve")
        vg = P.run("vector", lambda e: e.tensor_copy(rt[:, 64:128], s2[:, 64:128]), inc="dve")
        return vg

    def span_tiles(self, qt):
        tiles = []
        nch = 4 if qt == 0 else 8
        for c in range(nch - 1):
            for w in range(4):
                tiles.append((c, w, False))
        for w in range(4):
            tiles.append((qt, w, True))
        return tiles

    def att_flash(self, L):
        P = self.P
        kvh_of = (lambda h: 0) if L == 0 else (lambda h: h // 16)
        nheads_per_kv = 32 if L == 0 else 16
        wname = "l0_wo" if L == 0 else "l1_dense"
        P.wait("sync", "pe", P.now("pe"))
        P.wait("sync", "dve", P.now("dve"))
        if self.mask_gate is not None:
            P.wait("sync", *self.mask_gate)
        P.run("sync", lambda e: e.dma_start(self.sb_mask[:, :], self.c_mask[:, :]), inc="dmsk", by=16)
        vmk = P.run("sync", lambda e: e.dma_start(self.vones[:, :], self.dram["vones"][:, :]), inc="dmsk", by=16)
        vnl = P.run("sync", lambda e: e.dma_start(
            self.small[0:1, 8:16].rearrange("p (r c) -> p r c", r=4),
            self.sd("n_all", L)[:, :].rearrange("(o r) c -> o r c", o=1)[0]), inc="dnrm", by=16)
        vql0 = P.run("sync", lambda e: e.dma_start(self.qn[0:1, 0:64], self.sd("qn_st", L)[:, :]), inc="dnrm", by=16)
        P.wait("vector", "dnrm", vql0)
        P.wait("vector", "dmsk", vmk)
        P.fence("vector", "dve")
        for kh in range(2 if L == 1 else 1):
            ap_in = bass.AP(self.small, 8 + kh, [[16, 1], [2, 4]])
            P.run("vector", lambda e, ap_in=ap_in, kh=kh: e.reduce_max(self.small[0:1, 1 + kh:2 + kh], ap_in, axis=mybir.AxisListType.X), inc="dve")
        vkm = P.now("dve")

        for qt in range(2):
            nkt = 16 if qt == 0 else 32
            vq0 = P.now("dve")
            P.wait("sync", "dve", vq0)
            for h in range(32):
                P.run("sync", lambda e, h=h, qt=qt: e.dma_start(
                    self.fl_q[:, h * 512:(h + 1) * 512],
                    self.sd("qd", L)[h * 128:(h + 1) * 128, qt * TT:(qt + 1) * TT]), inc="dq", by=16)
            vql = P.now("dq")
            nkv = 1 if L == 0 else 2
            P.wait("vector", "dnrm", P.now("dnrm"))
            P.wait("vector", "dve", vkm)
            P.fence("vector", "dve")
            if L == 0:
                v1 = P.run("vector", lambda e, qt=qt: e.tensor_scalar(
                    self.qn[0:1, qt * 32:qt * 32 + 32], self.qn[0:1, qt * 32:qt * 32 + 32], self.small[0:1, 1:2], None, ALU.mult), inc="dve")
            else:
                P.run("vector", lambda e, qt=qt: e.tensor_scalar(
                    self.qn[0:1, qt * 32:qt * 32 + 16], self.qn[0:1, qt * 32:qt * 32 + 16], self.small[0:1, 1:2], None, ALU.mult), inc="dve")
                v1 = P.run("vector", lambda e, qt=qt: e.tensor_scalar(
                    self.qn[0:1, qt * 32 + 16:qt * 32 + 32], self.qn[0:1, qt * 32 + 16:qt * 32 + 32], self.small[0:1, 2:3], None, ALU.mult), inc="dve")
            P.wait("scalar", "dve", v1)
            v2 = P.run("scalar", lambda e, qt=qt: e.activation(self.qn[0:1, qt * 32:qt * 32 + 32], self.qn[0:1, qt * 32:qt * 32 + 32], AF.Sqrt), inc="act")
            P.wait("vector", "act", v2)
            v3 = P.run("vector", lambda e, qt=qt: e.tensor_scalar(
                self.qn[0:1, qt * 32:qt * 32 + 32], self.qn[0:1, qt * 32:qt * 32 + 32], -1.0, 64.0, ALU.mult, ALU.add), inc="dve")
            P.wait("tensor", "dve", v3)
            P.wait("tensor", "dve", self.evac_done)
            vmh = 0
            for h in range(32):
                P.wait("tensor", "dve", vmh)
                vpm = P.run("tensor", lambda e, h=h, qt=qt: e.matmul(
                    self.ps[4][:, 0:1], self.sb_ones_row_f[:1, :], self.qn[0:1, qt * 32 + h:qt * 32 + h + 1], start=True, stop=True), inc="pe")
                P.wait("vector", "pe", vpm)
                P.fence("vector", "dve")
                vmh = P.run("vector", lambda e, h=h: e.tensor_copy(self.mhat32[:, h:h + 1], self.ps[4][:, 0:1]), inc="dve")
            mhat_done = vmh

            for kvg in range(nkv):
                tiles = self.span_tiles(qt)
                if self.kv_gate is not None:
                    P.wait("sync", *self.kv_gate)
                    self.kv_gate = None
                for kt, (c, w, own) in enumerate(tiles):
                    if own:
                        ksrc = self.sd("k_own", L)[kvg * 128 * (L):kvg * 128 * (L) + 128, c * TT + w * 128:c * TT + w * 128 + 128] if L == 1 else                                self.sd("k_own", L)[:, c * TT + w * 128:c * TT + w * 128 + 128]
                        vsrc = self.sd("v_own", L)[c * TT + w * 128:c * TT + w * 128 + 128, kvg * 128:(kvg + 1) * 128] if L == 1 else                                self.sd("v_own", L)[c * TT + w * 128:c * TT + w * 128 + 128, :]
                    else:
                        rk = c if c < 4 else 7 - c
                        sl = 0 if c < 4 else 1
                        rstride = 128 * nkv
                        ksrc = self.sd("k_all", L)[rk * rstride + kvg * 128:rk * rstride + kvg * 128 + 128,
                                                  sl * TT + w * 128:sl * TT + w * 128 + 128]
                        vsrc = self.sd("v_all", L)[rk * TOK + sl * TT + w * 128:rk * TOK + sl * TT + w * 128 + 128,
                                                  kvg * 128:(kvg + 1) * 128] if L == 1 else                                self.sd("v_all", L)[rk * TOK + sl * TT + w * 128:rk * TOK + sl * TT + w * 128 + 128, :]
                    P.run("sync", lambda e, kt=kt, ksrc=ksrc: e.dma_start(
                        self.kv_k[:, kt * 128:(kt + 1) * 128], ksrc), inc="dkv", by=16)
                    P.run("sync", lambda e, kt=kt, vsrc=vsrc: e.dma_start(
                        self.kv_v[:, kt * 132:kt * 132 + 128], vsrc), inc="dkv", by=16)
                vkv = P.now("dkv")
                P.wait("vector", "dkv", vkv)
                P.fence("vector", "dve")
                vo = None
                for kt in range(nkt):
                    P.run("vector", lambda e, kt=kt, qt=qt: e.tensor_scalar(
                        self.kv_v[:, kt * 132:kt * 132 + 128], self.kv_v[:, kt * 132:kt * 132 + 128],
                        self.vones[:, qt * 32 + kt:qt * 32 + kt + 1], None, ALU.mult), inc="dve")
                    vo = P.run("vector", lambda e, kt=kt, qt=qt: e.tensor_copy(
                        self.kv_v[:, kt * 132 + 128:kt * 132 + 129], self.vones[:, qt * 32 + kt:qt * 32 + kt + 1]), inc="dve")
                vones_done = vo

                h0 = kvg * nheads_per_kv
                for hh in range(nheads_per_kv):
                    h = h0 + hh
                    self._flash_head(L, qt, h, nkt, vkv, vql, vones_done, mhat_done)
                self.kv_gate = ("pe", P.now("pe"))

            self._out_proj(L, qt, wname)
        self.mask_gate = ("dve", P.now("dve"))
        self.resid_ready = ("dve", P.now("dve"))

    def _flash_head(self, L, qt, h, nkt, vkv, vql, vones_done, mhat_done):
        P = self.P
        nb = nkt // 2
        ctx_banks = [self.ps[4], self.ps[5], self.ps_x,
                     self.ps_tp[:, :].bitcast(F32)[:, 0:512]]
        qsl = self.fl_q[:, h * 512:(h + 1) * 512]
        P.wait("tensor", "dq", vql)
        P.wait("tensor", "dkv", vkv)
        P.wait("tensor", "dve", max(vones_done, mhat_done, self.evac_done))
        expd = {}
        qkd = {}
        pvd = {}
        maskd = {}
        for b in range(nb):
            s0 = self.ps[(2 * b) % 4]
            s1 = self.ps[(2 * b + 1) % 4]
            if b >= 2:
                P.wait("tensor", "act", expd[b - 2])
            P.run("tensor", lambda e, s0=s0, kt=2 * b, qsl=qsl: e.matmul(
                s0[:, :], self.kv_k[:, kt * 128:(kt + 1) * 128], qsl, start=True, stop=True))
            qkd[b] = P.run("tensor", lambda e, s1=s1, kt=2 * b + 1, qsl=qsl: e.matmul(
                s1[:, :], self.kv_k[:, kt * 128:(kt + 1) * 128], qsl, start=True, stop=True), inc="pe")
            for kk in (2 * b, 2 * b + 1):
                if kk >= nkt - 4:
                    w = kk - (nkt - 4)
                    sb = self.ps[kk % 4]
                    P.wait("vector", "pe", qkd[b])
                    P.fence("vector", "dve")
                    maskd[kk] = P.run("vector", lambda e, sb=sb, w=w: e.tensor_tensor(
                        sb[:, :], sb[:, :], self.sb_mask[:, 384 - 128 * w:896 - 128 * w], ALU.add), inc="dve")
            P.wait("scalar", "pe", qkd[b])
            if (2 * b) >= nkt - 4 or (2 * b + 1) >= nkt - 4:
                P.wait("scalar", "dve", max(v for k, v in maskd.items() if k in (2 * b, 2 * b + 1)))
            if b >= 2:
                P.wait("scalar", "pe", pvd.get(b - 2, 0))
            P.run("scalar", lambda e, s0=s0, b=b, h=h: e.activation(
                self.fl_pr[:, ((2 * b) % 4) * 512:((2 * b) % 4) * 512 + 512], s0[:, :], AF.Exp,
                bias=self.mhat32[:, h:h + 1], scale=1.0))
            expd[b] = P.run("scalar", lambda e, s1=s1, b=b, h=h: e.activation(
                self.fl_pr[:, ((2 * b + 1) % 4) * 512:((2 * b + 1) % 4) * 512 + 512], s1[:, :], AF.Exp,
                bias=self.mhat32[:, h:h + 1], scale=1.0), inc="act")
            P.wait("tensor", "act", expd[b])
            vp = None
            for kk in (2 * b, 2 * b + 1):
                pr = self.fl_pr[:, (kk % 4) * 512:(kk % 4) * 512 + 512]
                for qs in range(4):
                    vp = P.run("tensor", lambda e, pr=pr, qs=qs, kk=kk, b=b: e.matmul(
                        ctx_banks[qs][:, 0:129], pr[:, qs * 128:(qs + 1) * 128],
                        self.kv_v[:, kk * 132:kk * 132 + 129],
                        start=(kk == 0), stop=(kk == nkt - 1)),
                        inc="pe" if (kk == 2 * b + 1 and qs == 3) else None)
            pvd[b] = vp
        P.wait("vector", "pe", pvd[nb - 1])
        P.fence("vector", "dve")
        tpb = self.ps[0].bitcast(BF16) if False else self.ps[0]
        for qs in (3, 2, 1, 0):
            cb = ctx_banks[qs]
            tpd = self.ps_tp[:, 384 + qs * 128:384 + qs * 128 + 128]
            vr = P.run("vector", lambda e, cb=cb: e.reciprocal(self.small[:, 3:4], cb[:, 128:129]), inc="dve")
            P.wait("vector", "dve", vr)
            vs = P.run("vector", lambda e, cb=cb, qs=qs: e.tensor_scalar(
                self.fl_ct[:, qs * 128:(qs + 1) * 128], cb[:, 0:128], self.small[:, 3:4], None, ALU.mult), inc="dve")
            P.wait("tensor", "dve", vs)
            vt = P.run("tensor", lambda e, tpd=tpd, qs=qs: e.transpose(
                tpd, self.fl_ct[:, qs * 128:(qs + 1) * 128], self.sb_ident[:, :]), inc="pe")
            P.wait("vector", "pe", vt)
            ve = P.run("vector", lambda e, tpd=tpd, qs=qs, h=h: e.tensor_copy(
                self.fl_q[:, h * 512 + qs * 128:h * 512 + qs * 128 + 128], tpd), inc="dve")
        self.evac_done = P.now("dve")

    def _out_proj(self, L, qt, wname):
        P = self.P
        P.wait("tensor", "dve", self.evac_done)
        nmm = 0
        for o in range(NKC):
            vw = self.load_wslice(wname, o)
            bank = self.ps[nmm % 2]
            P.wait("tensor", "dwt", vw)
            if self.bank_gate[nmm % 2] is not None:
                nm, vv = self.bank_gate[nmm % 2]
                P.wait("tensor", nm, vv)
                self.bank_gate[nmm % 2] = None
            vmm = None
            for kc in range(NKC):
                w = self.wchunk(kc)
                cx = self.fl_q[:, kc * 512:(kc + 1) * 512]
                vmm = P.run("tensor", lambda e, bank=bank, w=w, cx=cx, kc=kc: e.matmul(
                    bank[:, :], w, cx, start=(kc == 0), stop=(kc == NKC - 1)),
                    inc="pe" if kc == NKC - 1 else None)
            self.wbf_gate[0] = ("pe", vmm)
            self.wbf_gate[1] = ("pe", vmm)
            P.wait("vector", "pe", vmm)
            P.fence("vector", "dve")
            xa = self.x_res[:, o * TOK + qt * TT:o * TOK + qt * TT + TT]
            va = P.run("vector", lambda e, bank=bank, xa=xa: e.tensor_tensor(xa, bank[:, :], xa, ALU.add), inc="dve")
            self.bank_gate[nmm % 2] = ("dve", va)
            nmm += 1

    def ffn1(self):
        P = self.P
        self.rot_gate = Gate(4)
        self._rot_next = 0
        self._rot_pool = (3,)
        self.routeT_gate = 0
        self.stats_cast(0, want_rcol=True)
        self.stats_cast(1, want_rcol=True)
        castdone = self.cast_done
        P.wait("tensor", "dve", castdone)
        if self.wg_gate is not None:
            P.wait("sync", *self.wg_gate)
            self.wg_gate = None
        vwg = P.run("sync", lambda e: e.dma_start(
            self.wg_sb[:, :].rearrange("p (kc o) -> p kc o", o=8),
            self.dram["f1_wg"][:, :].rearrange("(kc p) o -> p kc o", p=128)), inc="dwg", by=16)
        for ti in range(2):
            for ts_ in range(4):
                col = ti * 4 + ts_
                bank = self.ps[0]
                P.wait("tensor", "dwg", vwg)
                if self.bank_gate[0] is not None:
                    P.wait("tensor", *self.bank_gate[0]); self.bank_gate[0] = None
                vmm = None
                for kc in range(NKC):
                    xsl = self.xbf[:, (ti * NKC + kc) * TT + ts_ * 128:(ti * NKC + kc) * TT + ts_ * 128 + 128]
                    wg = self.wg_sb[:, kc * 8:(kc + 1) * 8]
                    vmm = P.run("tensor", lambda e, bank=bank, xsl=xsl, wg=wg, kc=kc: e.matmul(
                        bank[:, 0:8], xsl, wg, start=(kc == 0), stop=(kc == NKC - 1)),
                        inc="pe" if kc == NKC - 1 else None)
                P.wait("vector", "pe", vmm)
                P.fence("vector", "dve")
                v1 = P.run("vector", lambda e, bank=bank, col=col: e.tensor_scalar(
                    self.lg_sb[:, :], bank[:, 0:8], self.rcol[:, col:col + 1], None, ALU.mult), inc="dve")
                self.bank_gate[0] = ("dve", v1)
                P.wait("vector", "dve", v1)
                v2 = P.run("vector", lambda e: e.reduce_max(self.small[:, 3:4], self.lg_sb[:, :], axis=mybir.AxisListType.X, negate=True), inc="dve")
                P.wait("scalar", "dve", v2)
                v3 = P.run("scalar", lambda e: e.activation(self.scr[:, 0:8], self.lg_sb[:, :], AF.Exp, bias=self.small[:, 3:4], scale=1.0), inc="act")
                P.wait("vector", "dve", v2)
                v4 = P.run("vector", lambda e: e.tensor_scalar(self.scr2[:, 0:8], self.lg_sb[:, :], self.small[:, 3:4], None, ALU.add), inc="dve")
                P.wait("vector", "dve", v4)
                v5 = P.run("vector", lambda e: e.tensor_single_scalar(self.scr2[:, 8:16], self.scr2[:, 0:8], -1e-30, ALU.is_ge), inc="dve")
                P.wait("vector", "dve", v5)
                v6 = P.run("vector", lambda e: e.scalar_tensor_tensor(self.scr2[:, 0:8], self.scr2[:, 8:16], NEG, self.lg_sb[:, :], ALU.mult, ALU.add), inc="dve")
                P.wait("vector", "dve", v6)
                v7 = P.run("vector", lambda e: e.reduce_max(self.small[:, 4:5], self.scr2[:, 0:8], axis=mybir.AxisListType.X), inc="dve")
                P.wait("vector", "dve", v7)
                v8 = P.run("vector", lambda e: e.tensor_single_scalar(self.scr2[:, 0:8], self.lg_sb[:, :], self.small[:, 4:5], ALU.is_ge), inc="dve")
                P.wait("vector", "dve", v8)
                P.wait("vector", "act", v3)
                v9 = P.run("vector", lambda e: e.tensor_tensor(self.scr[:, 8:16], self.scr[:, 0:8], self.scr2[:, 0:8], ALU.mult), inc="dve")
                P.wait("vector", "dve", v9)
                va = P.run("vector", lambda e: e.reduce_sum(self.small[:, 5:6], self.scr[:, 8:16], axis=mybir.AxisListType.X), inc="dve")
                P.wait("vector", "dve", va)
                vb = P.run("vector", lambda e: e.reciprocal(self.small[:, 6:7], self.small[:, 5:6]), inc="dve")
                P.wait("vector", "dve", vb)
                ri = self.rot_gate_acquire_next()
                rt = self.rot[:, ri * 512:ri * 512 + 512]
                vc = P.run("vector", lambda e, rt=rt: e.tensor_scalar(rt[:, 0:8], self.scr[:, 8:16], self.small[:, 6:7], None, ALU.mult), inc="dve")
                P.wait("tensor", "dve", vc)
                P.wait("tensor", "pe", self.routeT_gate)
                vt = P.run("tensor", lambda e, rt=rt: e.transpose(self.ps_tp[0:8, 0:128], rt[:, 0:8], self.sb_ident[:, :]), inc="pe")
                P.wait("vector", "pe", vt)
                P.fence("vector", "dve")
                vu = P.run("vector", lambda e, ts_=ts_: e.tensor_copy(self.route_T[0:8, ts_ * 128:(ts_ + 1) * 128], self.ps_tp[0:8, 0:128]), inc="dve")
                self.rot_gate.release(ri, "pe", vt)
            vwa = self.load_wslice("f1_at", 0)
            bank = self.ps[1]
            P.wait("tensor", "dwt", vwa)
            if self.bank_gate[1] is not None:
                P.wait("tensor", *self.bank_gate[1]); self.bank_gate[1] = None
            vmm = None
            for kc in range(NKC):
                xb = self.xbf[:, (ti * NKC + kc) * TT:(ti * NKC + kc) * TT + TT]
                w = self.wchunk(kc)
                vmm = P.run("tensor", lambda e, bank=bank, w=w, xb=xb, kc=kc: e.matmul(
                    bank[:, :], w, xb, start=(kc == 0), stop=(kc == NKC - 1)),
                    inc="pe" if kc == NKC - 1 else None)
            self.wbf_gate[0] = ("pe", vmm)
            self.wbf_gate[1] = ("pe", vmm)
            vre = P.now("dve")
            P.wait("tensor", "dve", vre)
            vx = P.run("tensor", lambda e: e.matmul(self.ps[2][:, :], self.sb_expand[0:8, :], self.route_T[0:8, :], start=True, stop=True), inc="pe")
            self.routeT_gate = vx
            bcr = (self.bc0, self.bc1)[ti]
            P.wait("vector", "pe", vx)
            P.fence("vector", "dve")
            v1 = P.run("vector", lambda e, bcr=bcr: e.tensor_tensor(self.scr[:, :], self.ps[2][:, :], bcr[:, :], ALU.mult), inc="dve")
            P.wait("vector", "pe", vmm)
            P.wait("vector", "dve", v1)
            v2 = P.run("vector", lambda e, bank=bank, ti=ti: e.tensor_tensor(
                self.hw_sb[:, ti * 512:(ti + 1) * 512], bank[:, :], self.scr[:, :], ALU.mult), inc="dve")
            self.bank_gate[1] = ("dve", v2)
            self.hw_done = v2
        self.bsl_gate = [("pe", P.now("pe")), ("pe", P.now("pe"))]
        self.dbank_gate = [("dve", self.hw_done), ("dve", self.hw_done)]
        nmm = 0
        for o in range(NKC):
            vw = self.load_wslice("f1_wb", o)
            bhalf = self.wg_sb[:, (o % 2) * 128:(o % 2) * 128 + 128]
            if self.bsl_gate[o % 2] is not None:
                P.wait("sync", *self.bsl_gate[o % 2])
                self.bsl_gate[o % 2] = None
            vbs = P.run("sync", lambda e, bhalf=bhalf, o=o: e.dma_start(
                bhalf, self.dram["f1_bf"][:, o * 128:(o + 1) * 128]), inc="dbs", by=16)
            for ti in range(2):
                bank = self.ps[nmm % 2]
                dbank = self.ps[2 + nmm % 2]
                P.wait("tensor", "dwt", vw)
                if self.bank_gate[nmm % 2] is not None:
                    nm, vv = self.bank_gate[nmm % 2]
                    P.wait("tensor", nm, vv)
                    self.bank_gate[nmm % 2] = None
                if self.dbank_gate[nmm % 2] is not None:
                    nm, vv = self.dbank_gate[nmm % 2]
                    P.wait("tensor", nm, vv)
                    self.dbank_gate[nmm % 2] = None
                vmm = None
                for kc in range(NKC):
                    w = self.wchunk(kc)
                    xb = self.xbf[:, (ti * NKC + kc) * TT:(ti * NKC + kc) * TT + TT]
                    vmm = P.run("tensor", lambda e, bank=bank, w=w, xb=xb, kc=kc: e.matmul(
                        bank[:, :], w, xb, start=(kc == 0), stop=(kc == NKC - 1)),
                        inc="pe" if kc == NKC - 1 else None)
                self.wbf_gate[0] = ("pe", vmm)
                self.wbf_gate[1] = ("pe", vmm)
                P.wait("tensor", "dbs", vbs)
                vdm = P.run("tensor", lambda e, dbank=dbank, bhalf=bhalf, ti=ti: e.matmul(
                    dbank[:, :], bhalf, self.hw_sb[:, ti * 512:(ti + 1) * 512], start=True, stop=True), inc="pe")
                self.bsl_gate[o % 2] = ("pe", vdm)
                bcr = (self.bc0, self.bc1)[ti]
                P.wait("vector", "pe", vdm)
                P.fence("vector", "dve")
                v1 = P.run("vector", lambda e, bank=bank, bcr=bcr: e.tensor_tensor(self.scr[:, :], bank[:, :], bcr[:, :], ALU.mult), inc="dve")
                self.bank_gate[nmm % 2] = ("dve", v1)
                P.wait("vector", "dve", v1)
                v2 = P.run("vector", lambda e, dbank=dbank, o=o: e.scalar_tensor_tensor(
                    self.scr2[:, :], dbank[:, :], self.vec[:, 6 * 32 + o:6 * 32 + o + 1], self.scr[:, :], ALU.add, ALU.add), inc="dve")
                self.dbank_gate[nmm % 2] = ("dve", v2)
                P.wait("vector", "dve", v2)
                xa = self.x_res[:, o * TOK + ti * TT:o * TOK + ti * TT + TT]
                v3 = P.run("vector", lambda e, xa=xa: e.scalar_tensor_tensor(xa, self.scr2[:, :], 0.5, xa, ALU.mult, ALU.add), inc="dve")
                nmm += 1
        self.resid_ready = ("dve", P.now("dve"))
        self._rot_pool = (0, 1, 2, 3)


    def prep(self):
        P = self.P
        groups = [[0, 1, 2, 3], [4, 5, 6, 7]]
        order = [sp for sp in PREP_SPECS if sp[0] in PREP_SHARDED] + \
                [sp for sp in PREP_SPECS if sp[0] not in PREP_SHARDED]
        for name, rows, cols, tern in order:
            shard = name in PREP_SHARDED
            win = self.dram[name + "_f"]
            wout = self.dram[name + "_part"] if shard else self.dram[name]
            lrows = rows // 4 if shard else rows
            if cols % 512 == 0:
                width = 512
            else:
                width = cols
            ntile = max(1, cols // width)
            for rb in range(max(1, lrows // 128)):
                pr = min(128, lrows)
                for tb in range(ntile):
                    if self.pp_gate[0] is not None:
                        P.wait("sync", *self.pp_gate[0])
                        self.pp_gate[0] = None
                    vd = P.run("sync", lambda e, win=win, rb=rb, tb=tb, width=width, pr=pr: e.dma_start(
                        self.w_st[0:pr, :width], win[rb * 128:rb * 128 + pr, tb * width:(tb + 1) * width]), inc="dpp", by=16)
                    rt = self.rot[:, 0:512]
                    if self.pp_gate[1] is not None:
                        P.wait("scalar", *self.pp_gate[1])
                        P.wait("vector", *self.pp_gate[1])
                        self.pp_gate[1] = None
                    if not tern:
                        P.wait("scalar", "dpp", vd)
                        vc = P.run("scalar", lambda e, rt=rt, width=width, pr=pr: e.copy(rt[0:pr, :width], self.w_st[0:pr, :width]), inc="act")
                        self.pp_gate[0] = ("act", vc)
                        gate = ("act", vc)
                    else:
                        P.wait("scalar", "dpp", vd)
                        v1 = P.run("scalar", lambda e, width=width, pr=pr: e.activation(self.scr[0:pr, :width], self.w_st[0:pr, :width], AF.Sign), inc="act")
                        v1b = P.run("scalar", lambda e, width=width, pr=pr: e.activation(self.scr2[0:pr, :width], self.w_st[0:pr, :width], AF.Abs), inc="act")
                        self.pp_gate[0] = ("act", v1b)
                        P.wait("vector", "act", v1b)
                        P.fence("vector", "dve")
                        v2 = P.run("vector", lambda e, width=width, pr=pr: e.tensor_single_scalar(self.scr2[0:pr, :width], self.scr2[0:pr, :width], 0.1, ALU.is_ge), inc="dve")
                        P.wait("vector", "dve", v2)
                        vc = P.run("vector", lambda e, rt=rt, width=width, pr=pr: e.tensor_tensor(rt[0:pr, :width], self.scr[0:pr, :width], self.scr2[0:pr, :width], ALU.mult), inc="dve")
                        gate = ("dve", vc)
                    P.wait("scalar", gate[0], gate[1])
                    vo = P.run("scalar", lambda e, wout=wout, rb=rb, tb=tb, rt=rt, width=width, pr=pr: e.dma_start(
                        wout[rb * 128:rb * 128 + pr, tb * width:(tb + 1) * width], rt[0:pr, :width]), inc="dpo", by=16)
                    self.pp_gate[1] = ("dpo", vo)
            if shard:
                a = self.dram[name + "_part"]
                b = self.dram[name]
                P.wait("gpsimd", "dpo", P.now("dpo"))
                P.run("gpsimd", lambda e, a=a, b=b, groups=groups: e.collective_compute(
                    "AllGather", ALU.bypass, replica_groups=groups,
                    ins=[a.ap().opt()], outs=[b.ap().opt()]), inc="cc", by=1)
        vtot = P.now("dpo")
        P.wait("sync", "dpo", vtot)

    def collect_kv(self, L):
        P = self.P
        for rr in range(4):
            P.wait("gpsimd", f"dr{rr}", P.now(f"dr{rr}"))
        P.wait("gpsimd", "dnrm", P.now("dnrm"))
        groups = [[0, 1, 2, 3], [4, 5, 6, 7]]
        for src, dst in (("k_own", "k_all"), ("v_own", "v_all"), ("n_own", "n_all")):
            a = self.sd(src, L)
            b = self.sd(dst, L)
            P.run("gpsimd", lambda e, a=a, b=b, groups=groups: e.collective_compute(
                "AllGather", ALU.bypass, replica_groups=groups,
                ins=[a.ap().opt()], outs=[b.ap().opt()]), inc="cc", by=1)
        v = P.now("cc")
        P.wait("sync", "cc", v)
        P.wait("vector", "cc", v)
        P.wait("tensor", "cc", v)
        P.wait("scalar", "cc", v)

    def cc0(self):
        self.collect_kv(0)

    def cc1(self):
        self.collect_kv(1)
    def att_pre0(self):
        self.att_pre(0)

    def att_pre1(self):
        self.att_pre(1)

    def att_flash0(self):
        self.att_flash(0)

    def att_flash1(self):
        self.att_flash(1)

    def build(self, phases):
        self.pp_gate = [None, None]
        self.resid_ready = None
        self.mask_gate = None
        self.kv_gate = None
        self.wg_gate = None
        self.bbsb_gate = None
        self.bsl_gate = [None, None]
        self.evac_done = 0
        self.qn_gate = 0
        self._rot_next = 0
        self.wbf_gate = [None, None]
        self.wst_gate = [None, None]
        self.bank_gate = [None, None]
        self.p2_gate = [None, None]
        self.rot_gate = Gate(4)
        self.load_consts()
        self.load_x()
        self.preload_vectors(phases)
        for ph in phases:
            getattr(self, ph)()
            if self.fused:
                self.P.barrier()
        self.store_x()
        self.P.finish()
        return self.nc



PREP_SPECS = [
    ("l0_wq", D, D, False), ("l0_wk", D, 128, False), ("l0_wv", D, 128, False),
    ("l0_wo", D, D, False), ("tern", D, D, True),
    ("l1_qkvw", D, 4608, False), ("l1_dense", D, D, False),
    ("f1_wb", D, D, False), ("f1_at", D, 128, False), ("f1_wg", D, 8, False),
    ("f1_bf", 128, D, False),
]

PREP_SHARDED = {"l0_wq", "l0_wo", "tern", "l1_qkvw", "l1_dense", "f1_wb"}


def _prep_program():
    nc = bass.Bass("TRN2", target_bir_lowering=False)
    P = Prog(nc)
    st = nc.alloc_sbuf_tensor("st", [128, 1024], F32)
    sg = nc.alloc_sbuf_tensor("sg", [128, 1024], F32)
    ot = nc.alloc_sbuf_tensor("ot", [128, 1024], BF16)
    g_st = [None]
    g_ot = [None]
    for name, rows, cols, tern in PREP_SPECS:
        win = nc.dram_tensor(name + "_f", [rows, cols], F32, kind="ExternalInput")
        wout = nc.dram_tensor(name, [rows, cols], BF16, kind="ExternalOutput")
        if cols % 1024 == 0:
            width = 1024
        elif cols % 512 == 0:
            width = 512
        else:
            width = cols
        ntile = max(1, cols // width)
        for rb in range(rows // 128):
            for tb in range(ntile):
                if g_st[0] is not None:
                    P.wait("sync", *g_st[0])
                vd = P.run("sync", lambda e, win=win, rb=rb, tb=tb, width=width: e.dma_start(
                    st[:, :width], win[rb * 128:(rb + 1) * 128, tb * width:(tb + 1) * width]), inc="d", by=16)
                if g_ot[0] is not None:
                    P.wait("scalar" if not tern else "vector", *g_ot[0])
                if not tern:
                    P.wait("scalar", "d", vd)
                    vc = P.run("scalar", lambda e, width=width: e.copy(ot[:, :width], st[:, :width]), inc="a")
                    g_st[0] = ("a", vc)
                    gate = ("a", vc)
                else:
                    P.wait("scalar", "d", vd)
                    v1 = P.run("scalar", lambda e, width=width: e.activation(sg[:, :width], st[:, :width], AF.Sign), inc="a")
                    v1b = P.run("scalar", lambda e, width=width: e.activation(st[:, :width], st[:, :width], AF.Abs), inc="a")
                    P.wait("vector", "a", v1b)
                    v2 = P.run("vector", lambda e, width=width: e.tensor_single_scalar(st[:, :width], st[:, :width], 0.1, ALU.is_ge), inc="v")
                    P.wait("vector", "v", v2)
                    vc = P.run("vector", lambda e, width=width: e.tensor_tensor(ot[:, :width], sg[:, :width], st[:, :width], ALU.mult), inc="v")
                    g_st[0] = ("v", vc)
                    gate = ("v", vc)
                P.wait("scalar", gate[0], gate[1])
                vo = P.run("scalar", lambda e, wout=wout, rb=rb, tb=tb, width=width: e.dma_start(
                    wout[rb * 128:(rb + 1) * 128, tb * width:(tb + 1) * width], ot[:, :width]), inc="do", by=16)
                g_ot[0] = ("do", vo)
    P.wait("scalar", "do", P.now("do"))
    P.finish()
    return nc


def _mode_for(phases, fused):
    IN, OUT, INT = "ExternalInput", "ExternalOutput", "Internal"
    stg = INT if fused else None
    m = {"x_own": ((D, TOK), F32, IN), "x_out": ((D, TOK), F32, OUT)}
    ph = set(phases)
    if "ffn0" in ph:
        m.update({"tern": ((D, D), BF16, IN), "y2": ((D, TOK), BF16, INT),
                  "f0_bias": ((128, 32), F32, IN), "f0_scale": ((128, 32), F32, IN),
                  "f0_gamma": ((128, 32), F32, IN), "f0_beta": ((128, 32), F32, IN),
                  "f0_qs": ((1, 1), F32, IN)})
    if "ffn1" in ph:
        m.update({"f1_wb": ((D, D), BF16, IN), "f1_at": ((D, 128), BF16, IN),
                  "f1_wg": ((D, 8), BF16, IN), "f1_bf": ((128, D), BF16, IN),
                  "f1_bbase": ((128, 32), F32, IN)})
    if "att_pre0" in ph:
        m.update({"l0_wq": ((D, D), BF16, IN), "l0_wk": ((D, 128), BF16, IN),
                  "l0_wv": ((D, 128), BF16, IN),
                  "qd": ((D, TOK), BF16, stg or OUT),
                  "k_own": ((128, TOK), BF16, stg or OUT),
                  "v_own": ((TOK, 128), BF16, stg or OUT),
                  "n_own": ((1, 2), F32, stg or OUT),
                  "qn_st": ((1, 64), F32, stg or OUT)})
    if "att_flash0" in ph:
        m.update({"l0_wo": ((D, D), BF16, IN), "vones": ((128, 64), F32, IN)})
        if not fused:
            m.update({"qd": ((D, TOK), BF16, IN), "qn_st": ((1, 64), F32, IN),
                      "k_own": ((128, TOK), BF16, IN), "v_own": ((TOK, 128), BF16, IN),
                      "k_all": ((512, TOK), BF16, IN), "v_all": ((4 * TOK, 128), BF16, IN),
                      "n_all": ((4, 2), F32, IN)})
        else:
            m.update({"k_all": ((512, TOK), BF16, INT), "v_all": ((4 * TOK, 128), BF16, INT),
                      "n_all": ((4, 2), F32, INT)})
    if "att_pre1" in ph:
        m.update({"l1_qkvw": ((D, 4608), BF16, IN), "l1_qkvb": ((1, 4608), F32, IN),
                  "rope_cos": ((TOK, 32), F32, IN), "rope_sin": ((TOK, 32), F32, IN),
                  "qd": ((D, TOK), BF16, stg or OUT),
                  "k_own": ((256, TOK), BF16, stg or OUT),
                  "v_own": ((TOK, 256), BF16, stg or OUT),
                  "n_own": ((1, 2), F32, stg or OUT),
                  "qn_st": ((1, 64), F32, stg or OUT)})
    if "att_flash1" in ph:
        m.update({"l1_dense": ((D, D), BF16, IN), "vones": ((128, 64), F32, IN)})
        if not fused:
            m.update({"qd": ((D, TOK), BF16, IN), "qn_st": ((1, 64), F32, IN),
                      "k_own": ((256, TOK), BF16, IN), "v_own": ((TOK, 256), BF16, IN),
                      "k_all": ((1024, TOK), BF16, IN), "v_all": ((4 * TOK, 256), BF16, IN),
                      "n_all": ((4, 2), F32, IN)})
        else:
            m.update({"k_all": ((1024, TOK), BF16, INT), "v_all": ((4 * TOK, 256), BF16, INT),
                      "n_all": ((4, 2), F32, INT)})
    if fused:
        for nm in ("qd", "k_own", "v_own", "n_own", "qn_st", "k_all", "v_all", "n_all"):
            m.pop(nm, None)
        m.update({
            "qd0": ((D, TOK), BF16, INT), "k_own0": ((128, TOK), BF16, INT),
            "v_own0": ((TOK, 128), BF16, INT), "n_own0": ((1, 2), F32, INT),
            "qn_st0": ((1, 64), F32, INT), "k_all0": ((512, TOK), BF16, INT),
            "v_all0": ((4 * TOK, 128), BF16, INT), "n_all0": ((4, 2), F32, INT),
            "qd1": ((D, TOK), BF16, INT), "k_own1": ((256, TOK), BF16, INT),
            "v_own1": ((TOK, 256), BF16, INT), "n_own1": ((1, 2), F32, INT),
            "qn_st1": ((1, 64), F32, INT), "k_all1": ((1024, TOK), BF16, INT),
            "v_all1": ((4 * TOK, 256), BF16, INT), "n_all1": ((4, 2), F32, INT)})
        for name, rows, cols, tern in PREP_SPECS:
            if name in m:
                m[name] = ((rows, cols), BF16, INT)
                if name in PREP_SHARDED:
                    m[name + "_f"] = ((rows // 4, cols), F32, IN)
                    m[name + "_part"] = ((rows // 4, cols), BF16, INT)
                else:
                    m[name + "_f"] = ((rows, cols), F32, IN)
    return m


def _host_shard_x(xfull):
    xf = np.ascontiguousarray(xfull.reshape(BATCH * SEQ, D).T)
    return [np.ascontiguousarray(xf[:, _rows_of_core(c)]) for c in range(NC)]


def _host_unshard_x(shards):
    xf = np.empty((D, BATCH * SEQ), dtype=np.float32)
    for c in range(NC):
        xf[:, _rows_of_core(c)] = shards[c]
    return np.ascontiguousarray(xf.T).reshape(BATCH, SEQ, D)


def _vones_for_core(c):
    v = np.zeros((128, 64), dtype=np.float32)
    j = c % 4
    for qt, g in ((0, j), (1, 7 - j)):
        nch = 4 if qt == 0 else 8
        kt = 0
        for cc in range(nch - 1):
            for w in range(4):
                v[:, qt * 32 + kt] = 1.0 if cc < g else 0.0
                kt += 1
        for w in range(4):
            v[:, qt * 32 + kt] = 1.0
            kt += 1
    return v


def vec128(v):
    return np.ascontiguousarray(np.asarray(v, dtype=np.float32).reshape(32, 128).T)


_PROG_CACHE = {}


def _get_prog(key, phases, fused):
    if key not in _PROG_CACHE:
        mk = MK(_mode_for(phases, fused), fused=fused)
        _PROG_CACHE[key] = (mk, mk.build(list(phases)))
    return _PROG_CACHE[key]


FUSED = True
_FUSED_PHASES = ("prep", "ffn0", "att_pre0", "cc0", "att_flash0", "ffn0",
                 "ffn1", "att_pre1", "cc1", "att_flash1", "ffn1")


def _weights_f(inputs):
    return {
        "l0_wq_f": np.ascontiguousarray(np.asarray(inputs["L0_attn_wq"], np.float32)),
        "l0_wk_f": np.ascontiguousarray(np.asarray(inputs["L0_attn_wk"], np.float32)),
        "l0_wv_f": np.ascontiguousarray(np.asarray(inputs["L0_attn_wv"], np.float32)),
        "l0_wo_f": np.ascontiguousarray(np.asarray(inputs["L0_attn_wo"], np.float32)),
        "tern_f": np.ascontiguousarray(np.asarray(inputs["L0_ffn_weight"], np.float32).T),
        "l1_qkvw_f": np.ascontiguousarray(np.asarray(inputs["L1_attn_qkv_w"], np.float32).T),
        "l1_dense_f": np.ascontiguousarray(np.asarray(inputs["L1_attn_dense_w"], np.float32).T),
        "f1_wb_f": np.ascontiguousarray(np.asarray(inputs["L1_ffn_W_base"], np.float32).T),
        "f1_at_f": np.ascontiguousarray(np.asarray(inputs["L1_ffn_A"], np.float32).reshape(128, D).T),
        "f1_wg_f": np.ascontiguousarray(np.asarray(inputs["L1_ffn_W_gate"], np.float32).T),
        "f1_bf_f": np.ascontiguousarray(np.asarray(inputs["L1_ffn_B"], np.float32).transpose(0, 2, 1).reshape(128, D)),
    }


def _vec_inputs(inputs):
    return {
        "f0_bias": vec128(inputs["L0_ffn_bias"]),
        "f0_scale": vec128(inputs["L0_ffn_scale"]),
        "f0_gamma": vec128(inputs["L0_ffn_ln_gamma"]),
        "f0_beta": vec128(inputs["L0_ffn_ln_beta"]),
        "f0_qs": np.asarray(inputs["L0_ffn_quant_scale"], np.float32).reshape(1, 1),
        "f1_bbase": vec128(inputs["L1_ffn_b_base"]),
        "l1_qkvb": np.ascontiguousarray(np.asarray(inputs["L1_attn_qkv_b"], np.float32).reshape(1, 4608)),
    }


def kernel(**inputs):
    cos, sin = _rope_tables()
    xs = _host_shard_x(np.asarray(inputs["L0_attn_x"], dtype=np.float32))

    if FUSED:
        mk, nc = _get_prog("fused", _FUSED_PHASES, fused=True)
        wf = _weights_f(inputs)
        base = _vec_inputs(inputs)
        for nm, arr in wf.items():
            if nm[:-2] not in PREP_SHARDED:
                base[nm] = arr
        vones = [_vones_for_core(c) for c in range(NC)]
        in_maps = []
        for c in range(NC):
            gr = c % 4
            im = dict(base, x_own=xs[c], vones=vones[c],
                      rope_cos=np.ascontiguousarray(cos[_positions_of_core(c)]),
                      rope_sin=np.ascontiguousarray(sin[_positions_of_core(c)]))
            for nm in PREP_SHARDED:
                arr = wf[nm + "_f"]
                q = arr.shape[0] // 4
                im[nm + "_f"] = arr[gr * q:(gr + 1) * q]
            in_maps.append(im)
        res = run_bass_kernel_spmd(nc, in_maps, core_ids=list(range(NC)))
        return _host_unshard_x([res.results[c]["x_out"] for c in range(NC)])

    if "prep" not in _PROG_CACHE:
        _PROG_CACHE["prep"] = _prep_program()
    ncp = _PROG_CACHE["prep"]
    prep_in = {
        "l0_wq_f": np.ascontiguousarray(np.asarray(inputs["L0_attn_wq"], np.float32)),
        "l0_wk_f": np.ascontiguousarray(np.asarray(inputs["L0_attn_wk"], np.float32)),
        "l0_wv_f": np.ascontiguousarray(np.asarray(inputs["L0_attn_wv"], np.float32)),
        "l0_wo_f": np.ascontiguousarray(np.asarray(inputs["L0_attn_wo"], np.float32)),
        "tern_f": np.ascontiguousarray(np.asarray(inputs["L0_ffn_weight"], np.float32).T),
        "l1_qkvw_f": np.ascontiguousarray(np.asarray(inputs["L1_attn_qkv_w"], np.float32).T),
        "l1_dense_f": np.ascontiguousarray(np.asarray(inputs["L1_attn_dense_w"], np.float32).T),
        "f1_wb_f": np.ascontiguousarray(np.asarray(inputs["L1_ffn_W_base"], np.float32).T),
        "f1_at_f": np.ascontiguousarray(np.asarray(inputs["L1_ffn_A"], np.float32).reshape(128, D).T),
        "f1_wg_f": np.ascontiguousarray(np.asarray(inputs["L1_ffn_W_gate"], np.float32).T),
        "f1_bf_f": np.ascontiguousarray(np.asarray(inputs["L1_ffn_B"], np.float32).transpose(0, 2, 1).reshape(128, D)),
    }
    pres = run_bass_kernel_spmd(ncp, [prep_in], core_ids=[0])
    W = pres.results[0]

    base_all = {
        "f0_bias": vec128(inputs["L0_ffn_bias"]),
        "f0_scale": vec128(inputs["L0_ffn_scale"]),
        "f0_gamma": vec128(inputs["L0_ffn_ln_gamma"]),
        "f0_beta": vec128(inputs["L0_ffn_ln_beta"]),
        "f0_qs": np.asarray(inputs["L0_ffn_quant_scale"], np.float32).reshape(1, 1),
        "f1_bbase": vec128(inputs["L1_ffn_b_base"]),
        "l1_qkvb": np.ascontiguousarray(np.asarray(inputs["L1_attn_qkv_b"], np.float32).reshape(1, 4608)),
    }
    base_all.update({k: W[k] for k in W})
    vones = [_vones_for_core(c) for c in range(NC)]
    ropes = [(np.ascontiguousarray(cos[_positions_of_core(c)]),
              np.ascontiguousarray(sin[_positions_of_core(c)])) for c in range(NC)]

    groups = [("g_ffn0", ("ffn0",)), ("g_pre0", ("att_pre0",)), ("g_fl0", ("att_flash0",)),
              ("g_ffn0", ("ffn0",)), ("g_ffn1", ("ffn1",)), ("g_pre1", ("att_pre1",)),
              ("g_fl1", ("att_flash1",)), ("g_ffn1", ("ffn1",))]

    cur = xs
    kv = {}
    for key, phases in groups:
        mk, nc = _get_prog(key, phases, fused=False)
        names = set(mk.t.keys())
        in_maps = []
        for c in range(NC):
            im = {"x_own": cur[c]}
            for nm in names:
                kind = mk.t[nm][2]
                if kind != "ExternalInput" or nm == "x_own":
                    continue
                if nm in ("qd", "k_own", "v_own", "n_own", "qn_st"):
                    im[nm] = kv[nm][c]
                elif nm in ("k_all", "v_all", "n_all"):
                    grp = range(4 * (c // 4), 4 * (c // 4) + 4)
                    src = {"k_all": "k_own", "v_all": "v_own", "n_all": "n_own"}[nm]
                    im[nm] = np.ascontiguousarray(np.concatenate([kv[src][r] for r in grp], axis=0))
                elif nm == "vones":
                    im[nm] = vones[c]
                elif nm == "rope_cos":
                    im[nm] = ropes[c][0]
                elif nm == "rope_sin":
                    im[nm] = ropes[c][1]
                else:
                    im[nm] = base_all[nm]
            in_maps.append(im)
        res = run_bass_kernel_spmd(nc, in_maps, core_ids=list(range(NC)))
        cur = [res.results[c]["x_out"] for c in range(NC)]
        for nm in ("qd", "k_own", "v_own", "n_own", "qn_st"):
            if nm in names and mk.t[nm][2] == "ExternalOutput":
                kv[nm] = [res.results[c][nm] for c in range(NC)]
    return _host_unshard_x(cur)


# version 10 (final)
# speedup vs baseline: 1.3740x; 1.0225x over previous
import numpy as np
import ml_dtypes
import concourse.bass as bass
import concourse.mybir as mybir
from concourse.bass_utils import run_bass_kernel_spmd

F32 = mybir.dt.float32
BF16 = mybir.dt.bfloat16
I32 = mybir.dt.int32
AF = mybir.ActivationFunctionType
ALU = mybir.AluOpType
BF = ml_dtypes.bfloat16

D = 4096
SEQ = 4096
BATCH = 2
NC = 8
TOK = 1024
TT = 512
NKC = 32
NEG = -1e30
EPS = 1e-5
ISQ = 1.0 / np.sqrt(128.0)


def _chunks_of_core(c):
    j = c % 4
    return (j, 7 - j)


def _rows_of_core(c):
    s = c // 4
    a, b = _chunks_of_core(c)
    r = []
    for g in (a, b):
        st = s * SEQ + g * TT
        r.append(np.arange(st, st + TT))
    return np.concatenate(r)


def _positions_of_core(c):
    a, b = _chunks_of_core(c)
    return np.concatenate([np.arange(a * TT, a * TT + TT),
                           np.arange(b * TT, b * TT + TT)])


def _rank_of_chunk(s, g):
    j = g if g < 4 else 7 - g
    return 4 * s + j, (0 if g < 4 else 1)


def _rope_tables():
    inv = 1.0 / (10000.0 ** (np.arange(0, 64, 2, dtype=np.float64) / 64.0))
    pos = np.arange(SEQ, dtype=np.float64)
    fr = pos[:, None] * inv[None, :]
    return np.cos(fr).astype(np.float32), np.sin(fr).astype(np.float32)



ENGS = ("sync", "vector", "scalar", "tensor", "gpsimd")


class Prog:

    def __init__(self, nc):
        self.nc = nc
        self.ops = {e: [] for e in ENGS}
        self.sems = {}
        self.cnt = {}

    def sem(self, name):
        if name not in self.sems:
            self.sems[name] = self.nc.alloc_semaphore(name)
            self.cnt[name] = 0
        return self.sems[name]

    def run(self, eng, fn, inc=None, by=1):
        val = None
        sem = None
        if inc is not None:
            sem = self.sem(inc)
            self.cnt[inc] += by
            val = self.cnt[inc]
        def closure(e, fn=fn, sem=sem, by=by):
            ins = fn(e)
            if sem is not None:
                ins.then_inc(sem, by)
        self.ops[eng].append(closure)
        return val

    def wait(self, eng, name, val):
        if val is None or val <= 0:
            return
        sem = self.sem(name)
        self.ops[eng].append(lambda e, s=sem, v=val: e.wait_ge(s, v))

    def now(self, name):
        self.sem(name)
        return self.cnt[name]

    def fence(self, eng, name):
        self.wait(eng, name, self.now(name))

    def barrier(self):
        for name, v in list(self.cnt.items()):
            if v > 0:
                for e in ENGS:
                    self.wait(e, name, v)

    def finish(self):
        with self.nc.Block() as block:
            for e in ENGS:
                fns = self.ops[e]
                def body(engine, fns=fns):
                    for f in fns:
                        f(engine)
                getattr(block, e)(body)


class Gate:

    def __init__(self, n):
        self.g = [[] for _ in range(n)]

    def acquire(self, P, eng, i):
        engs = eng if isinstance(eng, (list, tuple)) else [eng]
        for name, val in self.g[i]:
            for en in engs:
                P.wait(en, name, val)
        self.g[i] = []

    def release(self, i, name, val):
        self.g[i].append((name, val))



class MK:

    def __init__(self, mode_tensors, fused=True):
        nc = bass.Bass("TRN2", target_bir_lowering=False,
                       dynamic_dma_scratch_size=1024)
        self.nc = nc
        self.P = Prog(nc)
        self.fused = fused
        self.dram = {}
        self.t = mode_tensors
        for name, (shape, dt, kind) in mode_tensors.items():
            self.dram[name] = nc.dram_tensor(name, list(shape), dt, kind=kind)
        self._alloc_consts()
        self._alloc_sbuf()
        self._alloc_psum()

    def sd(self, name, L):
        return self.dram[f"{name}{L}"] if self.fused else self.dram[name]

    def _alloc_consts(self):
        nc = self.nc
        self.c_ident = nc.inline_tensor(np.eye(128, dtype=BF), "c_ident")
        self.c_ones_col = nc.inline_tensor(np.ones((128, 1), dtype=BF), "c_ones_col")
        self.c_ones_row = nc.inline_tensor(np.ones((1, 128), dtype=BF), "c_ones_row")
        self.c_ones_row_f = nc.inline_tensor(np.ones((1, 128), dtype=np.float32), "c_ones_row_f")
        self.c_one1_f = nc.inline_tensor(np.ones((1, 1), dtype=np.float32), "c_one1_f")
        mm = np.full((128, 896), NEG, dtype=np.float32)
        ii = np.arange(128)[:, None]
        cc = np.arange(896)[None, :]
        mm[ii <= cc - 384] = 0.0
        self.c_mask = nc.inline_tensor(mm, "c_mask")
        ex = np.zeros((8, 128), dtype=BF)
        for e in range(8):
            ex[e, e * 16:(e + 1) * 16] = 1.0
        self.c_expand = nc.inline_tensor(ex, "c_expand")

    def _alloc_sbuf(self):
        nc = self.nc
        self.arena = nc.alloc_sbuf_tensor("arena", [128, 227840], mybir.dt.uint8)
        a = [nc.lookup_mloc(self.arena).addr]
        self.arena_base = a[0]

        def at(name, shape, dt, align=32):
            a[0] = (a[0] + align - 1) // align * align
            h = nc.alloc_sbuf_tensor_at(name, list(shape), dt, offset=a[0])
            per = int(np.prod(shape[1:])) * mybir.dt.size(dt)
            a[0] += per
            return h

        self.x_res = at("x_res", [128, NKC * TOK], F32)
        self.reg_base = a[0]
        self.xbf = at("xbf", [128, 2 * NKC * TT], BF16)
        rb = self.reg_base
        self.kv_k = nc.alloc_sbuf_tensor_at("kv_k", [128, 36 * 128], BF16, offset=rb)
        self.kv_v = nc.alloc_sbuf_tensor_at("kv_v", [128, 36 * 132], BF16, offset=rb + 9216)
        self.fl_pr = nc.alloc_sbuf_tensor_at("fl_pr", [128, 4 * 512], BF16, offset=rb + 18752)
        self.fl_ct = nc.alloc_sbuf_tensor_at("fl_ct", [128, 512], BF16, offset=rb + 22848)
        self.mhat32 = nc.alloc_sbuf_tensor_at("mhat32", [128, 32], F32, offset=rb + 23872)
        self.fl_q = nc.alloc_sbuf_tensor_at("fl_q", [128, 32 * 512], BF16, offset=rb + 32768)
        self.p2_y2 = nc.alloc_sbuf_tensor_at("p2_y2", [128, 2 * 512], BF16, offset=rb + 32768)
        self.p2_i32 = nc.alloc_sbuf_tensor_at("p2_i32", [128, 512], I32, offset=rb + 34816)
        self.p2_s0 = nc.alloc_sbuf_tensor_at("p2_s0", [128, 512], F32, offset=rb + 36864)
        self.p2_s1 = nc.alloc_sbuf_tensor_at("p2_s1", [128, 512], F32, offset=rb + 38912)
        self.p2_a = nc.alloc_sbuf_tensor_at("p2_a", [128, 512], BF16, offset=rb + 40960)
        self.p2_b = nc.alloc_sbuf_tensor_at("p2_b", [128, 512], BF16, offset=rb + 41984)

        self.w_bf0 = at("w_bf0", [128, 2048], BF16)
        self.w_bf1 = at("w_bf1", [128, 2048], BF16)
        self.w_st = at("w_st", [128, 512], F32)
        self.bc0 = at("bc0", [128, 512], F32)
        self.bc1 = at("bc1", [128, 512], F32)
        self.rot_off = (a[0] + 31) // 32 * 32
        self.rot = at("rot", [128, 4 * 512], BF16)
        self.scr = at("scr", [128, 512], F32)
        self.scr2 = at("scr2", [128, 512], F32)
        self.bc2 = self.scr2
        self.hw_sb = self.rot
        self.rows = at("rows", [128, 512], F32)
        self.rcol = at("rcol", [128, 16], F32)
        self.sb_ident = at("sb_ident", [128, 128], BF16)
        self.sb_ones_col = at("sb_ones_col", [128, 1], BF16)
        self.sb_ones_row = at("sb_ones_row", [1, 128], BF16)
        self.sb_ones_row_f = at("sb_ones_row_f", [1, 128], F32)
        self.sb_one1_f = at("sb_one1_f", [1, 1], F32)
        self.sb_expand = at("sb_expand", [8, 128], BF16)
        self.sb_rope = at("sb_rope", [128, 2 * 256], F32)
        self.vec = at("vec", [128, 7 * 32], F32)
        self.rowbuf = at("rowbuf", [1, 128], F32)
        self.rowbuf_bf = at("rowbuf_bf", [1, 128], BF16)
        self.bb_sb = at("bb_sb", [128, 128], F32)
        self.qn = at("qn", [1, 64], F32)
        self.route_T = nc.alloc_sbuf_tensor_at("route_T", [8, 512], BF16, offset=self.rot_off + 2048)
        self.wg_sb = at("wg_sb", [128, 256], BF16)
        self.lg_sb = at("lg_sb", [128, 8], F32)
        self.small = at("small", [128, 16], F32)
        self.sb_mask = nc.alloc_sbuf_tensor_at("sb_mask", [128, 896], F32, offset=self.reg_base + 24064)
        self.vones = nc.alloc_sbuf_tensor_at("vones", [128, 64], F32, offset=self.reg_base + 27680)
        self.limit = a[0]
        assert a[0] <= self.arena_base + 227840, f"SBUF overflow: {a[0]}"

    def _alloc_psum(self):
        nc = self.nc
        self.ps = [nc.alloc_psum_tensor(f"ps{i}", [128, 512], F32) for i in range(6)]
        self.ps_tp = nc.alloc_psum_tensor("ps_tp", [128, 1024], BF16)
        self.ps_x = nc.alloc_psum_tensor("ps_x", [128, 512], F32)

    def load_consts(self):
        P = self.P
        pairs = [(self.sb_ident, self.c_ident),
                 (self.sb_ones_col, self.c_ones_col), (self.sb_ones_row, self.c_ones_row),
                 (self.sb_ones_row_f, self.c_ones_row_f), (self.sb_one1_f, self.c_one1_f),
                 (self.sb_expand, self.c_expand)]
        for sb, cc in pairs:
            P.run("sync", lambda e, sb=sb, cc=cc: e.dma_start(sb[:, :], cc[:, :]), inc="dconst", by=16)
        if "rope_cos" in self.dram:
            for i, nm in enumerate(("rope_cos", "rope_sin")):
                dr = self.dram[nm]
                P.run("sync", lambda e, i=i, dr=dr: e.dma_start(
                    self.sb_rope[:, i * 256:(i + 1) * 256].rearrange("p (ts f) -> p ts f", ts=8),
                    dr[:, :].rearrange("(ts p) f -> p ts f", p=128)), inc="dconst", by=16)

    def load_x(self):
        P = self.P
        x = self.dram["x_own"]
        for kc in range(NKC):
            P.run("sync", lambda e, kc=kc: e.dma_start(
                self.x_res[:, kc * TOK:(kc + 1) * TOK],
                x[kc * 128:(kc + 1) * 128, :]), inc="dconst", by=16)

    def store_x(self):
        P = self.P
        x = self.dram["x_out"]
        v = P.now("dve")
        P.wait("sync", "dve", v)
        for kc in range(NKC):
            P.run("sync", lambda e, kc=kc: e.dma_start(
                x[kc * 128:(kc + 1) * 128, :],
                self.x_res[:, kc * TOK:(kc + 1) * TOK]), inc="dout", by=16)
        P.wait("sync", "dout", P.now("dout"))

    def preload_vectors(self, phases):
        P = self.P
        if any(p.startswith("ffn0") for p in phases):
            for i, nm in enumerate(("f0_bias", "f0_scale", "f0_gamma", "f0_beta")):
                P.run("sync", lambda e, i=i, nm=nm: e.dma_start(
                    self.vec[:, i * 32:(i + 1) * 32], self.dram[nm][:, :]), inc="dconst", by=16)
            P.run("sync", lambda e: e.dma_start(self.small[0:1, 0:1], self.dram["f0_qs"][:, :]), inc="dconst", by=16)
        if any(p.startswith("ffn1") for p in phases):
            P.run("sync", lambda e: e.dma_start(
                self.vec[:, 6 * 32:7 * 32], self.dram["f1_bbase"][:, :]), inc="dconst", by=16)
        vtot = P.now("dconst")
        for eng in ("vector", "scalar", "tensor", "gpsimd", "sync"):
            P.wait(eng, "dconst", vtot)
        if any(p.startswith("ffn0") for p in phases):
            v = P.run("vector", lambda e: e.reciprocal(self.small[0:1, 1:2], self.small[0:1, 0:1]), inc="dve")
            P.wait("tensor", "dve", v)
            v = P.run("tensor", lambda e: e.matmul(self.ps_x[:, 0:1], self.sb_ones_row_f[:1, :], self.small[0:1, 1:2], start=True, stop=True), inc="pe")
            P.wait("vector", "pe", v)
            v = P.run("vector", lambda e: e.tensor_copy(self.small[:, 2:3], self.ps_x[:, 0:1]), inc="dve")
            P.wait("vector", "dve", v)
            v = P.run("vector", lambda e: e.tensor_scalar(self.vec[:, 4 * 32:5 * 32], self.vec[:, 2 * 32:3 * 32], self.small[:, 2:3], None, ALU.mult), inc="dve")
            v = P.run("vector", lambda e: e.tensor_scalar(self.vec[:, 5 * 32:6 * 32], self.vec[:, 3 * 32:4 * 32], self.small[:, 2:3], None, ALU.mult), inc="dve")
            P.wait("scalar", "dve", v)

    def stats_cast(self, tile, want_rcol=False, kscale=False):
        P = self.P
        if self.resid_ready is not None:
            nm, vv = self.resid_ready
            P.wait("scalar", nm, vv)
        ps_su = self.ps[2 + 2 * tile]
        ps_sq = self.ps[3 + 2 * tile]
        rslot = tile * 32
        for kc in range(NKC):
            xa = self.x_res[:, kc * TOK + tile * TT: kc * TOK + tile * TT + TT]
            r0 = self.rot[:, (2 * kc % 4) * 512:(2 * kc % 4) * 512 + 512]
            r1 = self.rot[:, (2 * kc % 4 + 1) * 512:(2 * kc % 4 + 1) * 512 + 512]
            self.rot_gate.acquire(P, "scalar", 2 * kc % 4)
            self.rot_gate.acquire(P, "scalar", (2 * kc % 4) + 1)
            P.run("scalar", lambda e, xa=xa, r0=r0: e.copy(r0[:, :], xa), inc="act")
            va = P.run("scalar", lambda e, xa=xa, r1=r1: e.activation(r1[:, :], xa, AF.Square), inc="act")
            P.wait("tensor", "act", va)
            P.run("tensor", lambda e, r0=r0, ps_su=ps_su, kc=kc: e.matmul(
                ps_su[0:1, :], self.sb_ones_col[:, :], r0[:, :],
                start=(kc == 0), stop=(kc == NKC - 1)), inc="pe")
            vm = P.run("tensor", lambda e, r1=r1, ps_sq=ps_sq, kc=kc: e.matmul(
                ps_sq[0:1, :], self.sb_ones_col[:, :], r1[:, :],
                start=(kc == 0), stop=(kc == NKC - 1)), inc="pe")
            self.rot_gate.release(2 * kc % 4, "pe", vm)
            self.rot_gate.release(2 * kc % 4 + 1, "pe", vm)
        vpe = P.now("pe")
        P.wait("vector", "pe", vpe)
        rows = self.rows
        srow = self.scr
        P.fence("vector", "dve")
        v1 = P.run("vector", lambda e: e.tensor_scalar(rows[0:1, :], ps_su[0:1, :], 1.0 / D, None, ALU.mult), inc="dve")
        v2 = P.run("vector", lambda e: e.tensor_scalar(srow[0:1, :], ps_sq[0:1, :], 1.0 / D, EPS, ALU.mult, ALU.add), inc="dve")
        P.wait("tensor", "dve", v1)
        v7 = P.run("tensor", lambda e: e.matmul(self.ps_x[:, 0:512], self.sb_ones_row_f[:1, :], rows[0:1, :], start=True, stop=True), inc="pe")
        P.wait("vector", "pe", v7)
        v8 = P.run("vector", lambda e: e.tensor_copy(self.bc2[:, :], self.ps_x[:, 0:512]), inc="dve")
        P.wait("vector", "dve", v8)
        v3 = P.run("vector", lambda e: e.tensor_tensor(rows[0:1, :], rows[0:1, :], rows[0:1, :], ALU.mult), inc="dve")
        P.wait("vector", "dve", v3)
        v4 = P.run("vector", lambda e: e.tensor_tensor(rows[0:1, :], srow[0:1, :], rows[0:1, :], ALU.subtract), inc="dve")
        P.wait("scalar", "dve", v4)
        v5 = P.run("scalar", lambda e: e.activation(srow[0:1, :], rows[0:1, :], AF.Sqrt), inc="act")
        P.wait("vector", "act", v5)
        v6 = P.run("vector", lambda e: e.reciprocal(rows[0:1, :], srow[0:1, :]), inc="dve")
        bcr = (self.bc0, self.bc1)[tile]
        P.wait("tensor", "dve", v6)
        v9 = P.run("tensor", lambda e: e.matmul(self.ps_x[:, 0:512], self.sb_ones_row_f[:1, :], rows[0:1, :], start=True, stop=True), inc="pe")
        P.wait("vector", "pe", v9)
        va = P.run("vector", lambda e, bcr=bcr: e.tensor_copy(bcr[:, :], self.ps_x[:, 0:512]), inc="dve")
        if want_rcol:
            P.wait("tensor", "dve", va)
            for ts_ in range(4):
                vt = P.run("tensor", lambda e, ts_=ts_: e.transpose(
                    self.ps_x[:, 0:1],
                    rows[0:1, ts_ * 128:(ts_ + 1) * 128],
                    self.sb_one1_f[:1, :1]), inc="pe")
                P.wait("vector", "pe", vt)
                col = tile * 4 + ts_
                vc = P.run("vector", lambda e, col=col: e.tensor_copy(self.rcol[:, col:col + 1], self.ps_x[:, 0:1]), inc="dve")
                if kscale:
                    P.wait("vector", "dve", vc)
                    P.run("vector", lambda e, col=col: e.tensor_scalar(
                        self.rcol[:, 8 + col:9 + col], self.rcol[:, col:col + 1], ISQ, None, ALU.mult), inc="dve")
                P.wait("tensor", "dve", P.now("dve"))
        P.wait("vector", "dve", va)
        last = None
        for kc in range(NKC):
            xa = self.x_res[:, kc * TOK + tile * TT: kc * TOK + tile * TT + TT]
            xb = self.xbf[:, (tile * NKC + kc) * TT:(tile * NKC + kc) * TT + TT]
            last = P.run("vector", lambda e, xa=xa, xb=xb: e.tensor_tensor(xb, xa, self.bc2[:, :], ALU.subtract), inc="dve")
        self.cast_done = last

    def stream_w(self, dr, o, half, buf, eng_cast="scalar", cols=128, cast_scale=None):
        P = self.P
        vlast = None
        for q in range(2):
            st = self.w_st[:, (q % 2) * 512:(q % 2) * 512 + 512]
            if self.wst_gate[q] is not None:
                nm, vv = self.wst_gate[q]
                P.wait("sync", nm, vv)
                self.wst_gate[q] = None
            rows0 = half * 2048 + q * 1024
            vd = P.run("sync", lambda e, dr=dr, o=o, rows0=rows0, st=st, cols=cols: e.dma_start(
                st.rearrange("p (kc o) -> p kc o", o=cols) if cols != 512 else st,
                dr[rows0:rows0 + 1024, o * cols:(o + 1) * cols].rearrange("(kc p) o -> p kc o", p=128) if cols != 512 else
                dr[rows0:rows0 + 1024, o * cols:(o + 1) * cols].rearrange("(kc p) o -> p (kc o)", p=128)), inc="dsync", by=16)
            eng = eng_cast
            P.wait(eng, "dsync", vd)
            dst = buf[:, q * 1024:(q + 1) * 1024]
            if cast_scale is None:
                if eng == "scalar":
                    vlast = P.run(eng, lambda e, dst=dst, st=st: e.copy(dst, st), inc="act")
                else:
                    vlast = P.run(eng, lambda e, dst=dst, st=st: e.tensor_copy(dst, st), inc="dve" if eng == "vector" else "pool")
            else:
                vlast = P.run(eng, lambda e, dst=dst, st=st: e.activation(dst, st, AF.Copy, scale=cast_scale), inc="act")
            semn = {"scalar": "act", "vector": "dve", "gpsimd": "pool"}[eng]
            self.wst_gate[q] = (semn, vlast)
        return vlast

    def ffn0(self):
        P = self.P
        nc = self.nc
        self.rot_gate = Gate(4)
        self.wst_gate = [None, None]
        self.stats_cast(0)
        self.stats_cast(1)
        castA_castB_done = self.cast_done
        tern = self.dram["tern"]
        y2d = self.dram["y2"]

        P.wait("tensor", "dve", castA_castB_done)
        wbufs = (self.w_bf0, self.w_bf1)
        self.rot_gate = Gate(4)
        nmm = 0
        for o in range(NKC):
            for half in range(2):
                buf = wbufs[half]
                if self.wbf_gate[half] is not None:
                    nm, vv = self.wbf_gate[half]
                    P.wait("sync", nm, vv)
                    self.wbf_gate[half] = None
                rows0 = half * 2048
                P.run("sync", lambda e, rows0=rows0, o=o, buf=buf: e.dma_start(
                    buf[:, :].rearrange("p (kc o) -> p kc o", o=128),
                    tern[rows0:rows0 + 2048, o * 128:(o + 1) * 128].rearrange("(kc p) o -> p kc o", p=128)), inc="dwt", by=16)
            vw = P.now("dwt")
            for ti in range(2):
                bank = self.ps[nmm % 2]
                P.wait("tensor", "dwt", vw)
                if self.bank_gate[nmm % 2] is not None:
                    nm, vv = self.bank_gate[nmm % 2]
                    P.wait("tensor", nm, vv)
                    self.bank_gate[nmm % 2] = None
                vmm = None
                for kc in range(NKC):
                    w = wbufs[kc // 16][:, (kc % 16) * 128:(kc % 16) * 128 + 128]
                    xb = self.xbf[:, (ti * NKC + kc) * TT:(ti * NKC + kc) * TT + TT]
                    vmm = P.run("tensor", lambda e, bank=bank, w=w, xb=xb, kc=kc: e.matmul(
                        bank[:, :], w, xb, start=(kc == 0), stop=(kc == NKC - 1)),
                        inc="pe" if kc == NKC - 1 else None)
                self.wbf_gate[0] = ("pe", vmm)
                self.wbf_gate[1] = ("pe", vmm)
                P.wait("vector", "pe", vmm)
                bcr = (self.bc0, self.bc1)[ti]
                ri = nmm % 4
                self.rot_gate.acquire(P, "vector", ri)
                rt = self.rot[:, ri * 512:(ri + 1) * 512]
                scr = self.scr
                P.fence("vector", "dve")
                v1 = P.run("vector", lambda e, bank=bank, bcr=bcr, scr=scr: e.tensor_tensor(scr[:, :], bank[:, :], bcr[:, :], ALU.mult), inc="dve")
                self.bank_gate[nmm % 2] = ("dve", v1)
                P.wait("vector", "dve", v1)
                v2 = P.run("vector", lambda e, scr=scr, rt=rt, o=o: e.tensor_scalar(
                    rt, scr[:, :], self.vec[:, 0 * 32 + o:0 * 32 + o + 1], self.vec[:, 1 * 32 + o:1 * 32 + o + 1],
                    ALU.add, ALU.mult), inc="dve")
                P.wait("scalar", "dve", v2)
                ri2 = (ri + 2) % 4
                self.rot_gate.acquire(P, "scalar", ri2)
                rt2 = self.rot[:, ri2 * 512:(ri2 + 1) * 512]
                v3 = P.run("scalar", lambda e, rt=rt, rt2=rt2: e.activation(rt2, rt, AF.Square), inc="act")
                ps_su = self.ps[2 + 2 * ti]
                ps_sq = self.ps[3 + 2 * ti]
                P.wait("tensor", "dve", v2)
                P.run("tensor", lambda e, ps_su=ps_su, rt=rt, o=o: e.matmul(
                    ps_su[0:1, :], self.sb_ones_col[:, :], rt, start=(o == 0), stop=(o == NKC - 1)), inc="pe")
                P.wait("tensor", "act", v3)
                vst = P.run("tensor", lambda e, ps_sq=ps_sq, rt2=rt2, o=o: e.matmul(
                    ps_sq[0:1, :], self.sb_ones_col[:, :], rt2, start=(o == 0), stop=(o == NKC - 1)), inc="pe")
                self.rot_gate.release(ri2, "pe", vst)
                vd = P.run("scalar", lambda e, rt=rt, o=o, ti=ti: e.dma_start(
                    y2d[o * 128:(o + 1) * 128, ti * TT:(ti + 1) * TT], rt), inc=f"dr{ri}", by=16)
                self.rot_gate.release(ri, f"dr{ri}", vd)
                self.rot_gate.release(ri, "pe", vst)
                self.rot_gate.release(ri, "act", v3)
                nmm += 1

        for ti in range(2):
            ps_su = self.ps[2 + 2 * ti]
            ps_sq = self.ps[3 + 2 * ti]
            rows = self.rows
            srow = self.scr
            vpe = P.now("pe")
            P.wait("vector", "pe", vpe)
            P.fence("vector", "dve")
            v1 = P.run("vector", lambda e, ps_su=ps_su: e.tensor_scalar(rows[0:1, :], ps_su[0:1, :], 1.0 / D, None, ALU.mult), inc="dve")
            v2 = P.run("vector", lambda e, ps_sq=ps_sq: e.tensor_scalar(srow[0:1, :], ps_sq[0:1, :], 1.0 / D, EPS, ALU.mult, ALU.add), inc="dve")
            P.wait("tensor", "dve", v1)
            v7 = P.run("tensor", lambda e: e.matmul(self.ps_x[:, 0:512], self.sb_ones_row_f[:1, :], rows[0:1, :], start=True, stop=True), inc="pe")
            P.wait("vector", "pe", v7)
            v8 = P.run("vector", lambda e: e.tensor_copy(self.bc0[:, :], self.ps_x[:, 0:512]), inc="dve")
            P.wait("vector", "dve", v8)
            v3 = P.run("vector", lambda e: e.tensor_tensor(rows[0:1, :], rows[0:1, :], rows[0:1, :], ALU.mult), inc="dve")
            P.wait("vector", "dve", v3)
            v4 = P.run("vector", lambda e: e.tensor_tensor(rows[0:1, :], srow[0:1, :], rows[0:1, :], ALU.subtract), inc="dve")
            P.wait("scalar", "dve", v4)
            v5 = P.run("scalar", lambda e: e.activation(srow[0:1, :], rows[0:1, :], AF.Sqrt), inc="act")
            P.wait("vector", "act", v5)
            v6 = P.run("vector", lambda e: e.reciprocal(rows[0:1, :], srow[0:1, :]), inc="dve")
            P.wait("tensor", "dve", v6)
            v9 = P.run("tensor", lambda e: e.matmul(self.ps_x[:, 0:512], self.sb_ones_row_f[:1, :], rows[0:1, :], start=True, stop=True), inc="pe")
            P.wait("vector", "pe", v9)
            va = P.run("vector", lambda e: e.tensor_copy(self.bc1[:, :], self.ps_x[:, 0:512]), inc="dve")
            P.wait("vector", "dve", va)
            if ti == 0:
                for rr in range(4):
                    P.wait("sync", f"dr{rr}", P.now(f"dr{rr}"))
            for o in range(NKC):
                yb = self.p2_y2[:, (o % 2) * 512:(o % 2) * 512 + 512]
                if self.p2_gate[o % 2] is not None:
                    nm, vv = self.p2_gate[o % 2]
                    P.wait("sync", nm, vv)
                    self.p2_gate[o % 2] = None
                vl = P.run("sync", lambda e, yb=yb, o=o, ti=ti: e.dma_start(
                    yb, y2d[o * 128:(o + 1) * 128, ti * TT:(ti + 1) * TT]), inc=f"dy{o % 2}", by=16)
                P.wait("vector", f"dy{o % 2}", vl)
                P.fence("vector", "dve")
                v1 = P.run("vector", lambda e, yb=yb: e.tensor_tensor(self.p2_s0[:, :], yb, self.bc0[:, :], ALU.subtract), inc="dve")
                P.wait("vector", "dve", v1)
                v2 = P.run("vector", lambda e: e.tensor_tensor(self.p2_s0[:, :], self.p2_s0[:, :], self.bc1[:, :], ALU.mult), inc="dve")
                P.wait("scalar", "dve", v2)
                v3 = P.run("scalar", lambda e, o=o: e.activation(
                    self.p2_s1[:, :], self.p2_s0[:, :], AF.Tanh,
                    bias=self.vec[:, 5 * 32 + o:5 * 32 + o + 1], scale=self.vec[:, 4 * 32 + o:4 * 32 + o + 1]), inc="act")
                P.wait("vector", "act", v3)
                v4 = P.run("vector", lambda e: e.tensor_scalar(self.p2_i32[:, :], self.p2_s1[:, :], 127.0, None, ALU.mult), inc="dve")
                P.wait("vector", "dve", v4)
                v5 = P.run("vector", lambda e: e.tensor_copy(self.p2_s0[:, :], self.p2_i32[:, :]), inc="dve")
                P.wait("vector", "dve", v5)
                v6 = P.run("vector", lambda e: e.scalar_tensor_tensor(self.p2_s1[:, :], self.p2_s1[:, :], 127.0, self.p2_s0[:, :], ALU.mult, ALU.subtract), inc="dve")
                P.wait("vector", "dve", v6)
                v7 = P.run("vector", lambda e: e.tensor_single_scalar(self.p2_a[:, :], self.p2_s1[:, :], 0.5, ALU.is_gt), inc="dve")
                v8 = P.run("vector", lambda e: e.tensor_single_scalar(self.p2_b[:, :], self.p2_s1[:, :], -0.5, ALU.is_lt), inc="dve")
                P.wait("vector", "dve", v8)
                v9 = P.run("vector", lambda e: e.tensor_tensor(self.p2_s1[:, :], self.p2_a[:, :], self.p2_b[:, :], ALU.subtract), inc="dve")
                P.wait("vector", "dve", v9)
                va2 = P.run("vector", lambda e: e.tensor_tensor(self.p2_s0[:, :], self.p2_s0[:, :], self.p2_s1[:, :], ALU.add), inc="dve")
                P.wait("vector", "dve", va2)
                xa = self.x_res[:, o * TOK + ti * TT: o * TOK + ti * TT + TT]
                vb = P.run("vector", lambda e, xa=xa: e.scalar_tensor_tensor(xa, self.p2_s0[:, :], 0.5 / 127.0, xa, ALU.mult, ALU.add), inc="dve")
                self.p2_gate[o % 2] = ("dve", vb)
        self.resid_ready = ("dve", P.now("dve"))


    def load_wslice(self, drname, o, ocols=128):
        P = self.P
        dr = self.dram[drname]
        for half in range(2):
            buf = (self.w_bf0, self.w_bf1)[half]
            if self.wbf_gate[half] is not None:
                nm, vv = self.wbf_gate[half]
                P.wait("sync", nm, vv)
                self.wbf_gate[half] = None
            rows0 = half * 2048
            P.run("sync", lambda e, dr=dr, rows0=rows0, o=o, buf=buf, ocols=ocols: e.dma_start(
                buf[:, :16 * ocols].rearrange("p (kc o) -> p kc o", o=ocols),
                dr[rows0:rows0 + 2048, o * ocols:(o + 1) * ocols].rearrange("(kc p) o -> p kc o", p=128)), inc="dwt", by=16)
        return P.now("dwt")

    def wchunk(self, kc, ocols=128):
        return (self.w_bf0, self.w_bf1)[kc // 16][:, (kc % 16) * ocols:(kc % 16 + 1) * ocols]

    def att_pre(self, L):
        P = self.P
        self.rot_gate = Gate(4)
        self.stats_cast(0, want_rcol=True, kscale=(L == 1))
        self.stats_cast(1, want_rcol=True, kscale=(L == 1))
        castdone = self.cast_done
        P.wait("tensor", "dve", castdone)
        P.fence("vector", "dve")
        P.run("vector", lambda e: e.memset(self.qn[0:1, :], 0.0), inc="dve")
        vz = P.run("vector", lambda e: e.memset(self.small[0:1, 4:8], 0.0), inc="dve")
        P.wait("vector", "dve", vz)
        if L == 0:
            self._pre_l0()
        else:
            self._pre_l1()
        vd = P.now("dve")
        P.wait("scalar", "dve", vd)
        vo = P.run("scalar", lambda e: e.dma_start(self.sd("n_own", L)[:, :], self.small[0:1, 4:6]), inc="dnrm", by=16)
        vo2 = P.run("scalar", lambda e: e.dma_start(self.sd("qn_st", L)[:, :], self.qn[0:1, 0:64]), inc="dnrm", by=16)
        P.wait("scalar", "dnrm", vo2)
        for rr in range(4):
            P.wait("scalar", f"dr{rr}", P.now(f"dr{rr}"))

    def _qnorm(self, rot_fm, width, dest):
        P = self.P
        ri2 = self.rot_gate_acquire_next()
        rt2 = self.rot[:, ri2 * 512:ri2 * 512 + 512]
        v3 = P.run("scalar", lambda e, rot_fm=rot_fm, rt2=rt2, width=width: e.activation(rt2[:, :width], rot_fm, AF.Square), inc="act")
        P.wait("tensor", "act", v3)
        P.wait("tensor", "dve", self.qn_gate)
        vm = P.run("tensor", lambda e, rt2=rt2, width=width: e.matmul(
            self.ps[2][0:1, :width], self.sb_ones_col[:, :], rt2[:, :width], start=True, stop=True), inc="pe")
        self.rot_gate.release(ri2, "pe", vm)
        P.wait("vector", "pe", vm)
        P.fence("vector", "dve")
        v4 = P.run("vector", lambda e, width=width: e.reduce_max(self.small[0:1, 3:4], self.ps[2][0:1, :width], axis=mybir.AxisListType.X), inc="dve")
        P.wait("vector", "dve", v4)
        v5 = P.run("vector", lambda e, dest=dest: e.tensor_tensor(
            dest, dest, self.small[0:1, 3:4], ALU.max), inc="dve")
        self.qn_gate = v4
        return v5

    def rot_gate_acquire_next(self):
        pool = getattr(self, "_rot_pool", (0, 1, 2, 3))
        ri = pool[self._rot_next % len(pool)]
        self._rot_next += 1
        self.rot_gate.acquire(self.P, ("vector", "scalar"), ri)
        return ri

    def _pre_l0(self):
        P = self.P
        qd = self.sd("qd", 0)
        self._rot_next = 0
        nmm = 0
        for o in range(NKC):
            vw = self.load_wslice("l0_wq", o)
            for ti in range(2):
                bank = self.ps[nmm % 2]
                P.wait("tensor", "dwt", vw)
                if self.bank_gate[nmm % 2] is not None:
                    nm, vv = self.bank_gate[nmm % 2]
                    P.wait("tensor", nm, vv)
                    self.bank_gate[nmm % 2] = None
                vmm = None
                for kc in range(NKC):
                    w = self.wchunk(kc)
                    xb = self.xbf[:, (ti * NKC + kc) * TT:(ti * NKC + kc) * TT + TT]
                    vmm = P.run("tensor", lambda e, bank=bank, w=w, xb=xb, kc=kc: e.matmul(
                        bank[:, :], w, xb, start=(kc == 0), stop=(kc == NKC - 1)),
                        inc="pe" if kc == NKC - 1 else None)
                self.wbf_gate[0] = ("pe", vmm)
                self.wbf_gate[1] = ("pe", vmm)
                bcr = (self.bc0, self.bc1)[ti]
                ri = self.rot_gate_acquire_next()
                rt = self.rot[:, ri * 512:ri * 512 + 512]
                P.wait("vector", "pe", vmm)
                P.fence("vector", "dve")
                v2 = P.run("vector", lambda e, bank=bank, bcr=bcr, rt=rt: e.tensor_tensor(rt, bank[:, :], bcr[:, :], ALU.mult), inc="dve")
                self.bank_gate[nmm % 2] = ("dve", v2)
                P.wait("scalar", "dve", v2)
                vq = self._qnorm(rt, 512, self.qn[0:1, ti * 32 + o:ti * 32 + o + 1])
                vd = P.run("scalar", lambda e, rt=rt, o=o, ti=ti, qd=qd: e.dma_start(
                    qd[o * 128:(o + 1) * 128, ti * TT:(ti + 1) * TT], rt), inc=f"dr{ri}", by=16)
                self.rot_gate.release(ri, f"dr{ri}", vd)
                nmm += 1
        vw = self.load_wslice("l0_wk", 0)
        for ti in range(2):
            bank = self.ps[nmm % 2]
            P.wait("tensor", "dwt", vw)
            if self.bank_gate[nmm % 2] is not None:
                nm, vv = self.bank_gate[nmm % 2]
                P.wait("tensor", nm, vv)
                self.bank_gate[nmm % 2] = None
            vmm = None
            for kc in range(NKC):
                w = self.wchunk(kc)
                xb = self.xbf[:, (ti * NKC + kc) * TT:(ti * NKC + kc) * TT + TT]
                vmm = P.run("tensor", lambda e, bank=bank, w=w, xb=xb, kc=kc: e.matmul(
                    bank[:, :], w, xb, start=(kc == 0), stop=(kc == NKC - 1)),
                    inc="pe" if kc == NKC - 1 else None)
            self.wbf_gate[0] = ("pe", vmm)
            self.wbf_gate[1] = ("pe", vmm)
            bcr = (self.bc0, self.bc1)[ti]
            ri = self.rot_gate_acquire_next()
            rt = self.rot[:, ri * 512:ri * 512 + 512]
            P.wait("vector", "pe", vmm)
            P.fence("vector", "dve")
            vs = P.run("vector", lambda e, bcr=bcr: e.tensor_scalar(self.scr2[:, :], bcr[:, :], ISQ, None, ALU.mult), inc="dve")
            P.wait("vector", "dve", vs)
            v2 = P.run("vector", lambda e, bank=bank, rt=rt: e.tensor_tensor(rt, bank[:, :], self.scr2[:, :], ALU.mult), inc="dve")
            self.bank_gate[nmm % 2] = ("dve", v2)
            P.wait("scalar", "dve", v2)
            vq = self._qnorm(rt, 512, self.small[0:1, 4:5])
            vd = P.run("scalar", lambda e, rt=rt, ti=ti: e.dma_start(
                self.sd("k_own", 0)[:, ti * TT:(ti + 1) * TT], rt), inc=f"dr{ri}", by=16)
            self.rot_gate.release(ri, f"dr{ri}", vd)
            nmm += 1
        vw = self.load_wslice("l0_wv", 0)
        for ti in range(2):
            for ts_ in range(4):
                bank = self.ps[nmm % 2]
                P.wait("tensor", "dwt", vw)
                if self.bank_gate[nmm % 2] is not None:
                    nm, vv = self.bank_gate[nmm % 2]
                    P.wait("tensor", nm, vv)
                    self.bank_gate[nmm % 2] = None
                vmm = None
                for kc in range(NKC):
                    xsl = self.xbf[:, (ti * NKC + kc) * TT + ts_ * 128:(ti * NKC + kc) * TT + ts_ * 128 + 128]
                    w = self.wchunk(kc)
                    vmm = P.run("tensor", lambda e, bank=bank, w=w, xsl=xsl, kc=kc: e.matmul(
                        bank[:, 0:128], xsl, w, start=(kc == 0), stop=(kc == NKC - 1)),
                        inc="pe" if kc == NKC - 1 else None)
                self.wbf_gate[0] = ("pe", vmm)
                self.wbf_gate[1] = ("pe", vmm)
                col = ti * 4 + ts_
                ri = self.rot_gate_acquire_next()
                rt = self.rot[:, ri * 512:ri * 512 + 512]
                P.wait("vector", "pe", vmm)
                P.fence("vector", "dve")
                v2 = P.run("vector", lambda e, bank=bank, rt=rt, col=col: e.tensor_scalar(
                    rt[:, 0:128], bank[:, 0:128], self.rcol[:, col:col + 1], None, ALU.mult), inc="dve")
                self.bank_gate[nmm % 2] = ("dve", v2)
                P.wait("scalar", "dve", v2)
                vd = P.run("scalar", lambda e, rt=rt, ti=ti, ts_=ts_: e.dma_start(
                    self.sd("v_own", 0)[ti * TT + ts_ * 128:ti * TT + ts_ * 128 + 128, :], rt[:, 0:128]), inc=f"dr{ri}", by=16)
                self.rot_gate.release(ri, f"dr{ri}", vd)
                nmm += 1

    def _pre_l1(self):
        P = self.P
        qd = self.sd("qd", 1)
        self._rot_next = 0
        nmm = 0
        self.rowbuf_gate = None
        for o in range(36):
            vw = self.load_wslice("l1_qkvw", o)
            if self.rowbuf_gate is not None:
                P.wait("sync", *self.rowbuf_gate)
            vb = P.run("sync", lambda e, o=o: e.dma_start(self.rowbuf[0:1, :], self.dram["l1_qkvb"][0:1, o * 128:(o + 1) * 128]), inc="dbr", by=16)
            P.wait("vector", "dbr", vb)
            P.fence("vector", "dve")
            sc = ISQ if o in (32, 33) else 1.0
            vc = P.run("vector", lambda e, sc=sc: e.tensor_scalar(self.rowbuf_bf[0:1, :], self.rowbuf[0:1, :], sc, None, ALU.mult), inc="dve")
            self.rowbuf_gate = ("dve", vc)
            P.wait("tensor", "dve", vc)
            if self.bbsb_gate:
                P.wait("tensor", "dve", self.bbsb_gate)
            vbb = P.run("tensor", lambda e: e.matmul(self.ps_x[:, 0:128], self.sb_ones_row[:1, :], self.rowbuf_bf[0:1, :], start=True, stop=True), inc="pe")
            P.wait("vector", "pe", vbb)
            P.fence("vector", "dve")
            vbc = P.run("vector", lambda e: e.tensor_copy(self.bb_sb[:, :], self.ps_x[:, 0:128]), inc="dve")
            self.bbsb_gate = vbc
            for ti in range(2):
                for ts_ in range(4):
                    bank = self.ps[nmm % 2]
                    P.wait("tensor", "dwt", vw)
                    P.wait("tensor", "dve", vbc)
                    if self.bank_gate[nmm % 2] is not None:
                        nm, vv = self.bank_gate[nmm % 2]
                        P.wait("tensor", nm, vv)
                        self.bank_gate[nmm % 2] = None
                    vmm = None
                    for kc in range(NKC):
                        xsl = self.xbf[:, (ti * NKC + kc) * TT + ts_ * 128:(ti * NKC + kc) * TT + ts_ * 128 + 128]
                        w = self.wchunk(kc)
                        vmm = P.run("tensor", lambda e, bank=bank, w=w, xsl=xsl, kc=kc: e.matmul(
                            bank[:, 0:128], xsl, w, start=(kc == 0), stop=(kc == NKC - 1)),
                            inc="pe" if kc == NKC - 1 else None)
                    self.wbf_gate[0] = ("pe", vmm)
                    self.wbf_gate[1] = ("pe", vmm)
                    col = (ti * 4 + ts_) + (8 if o in (32, 33) else 0)
                    P.wait("vector", "pe", vmm)
                    P.fence("vector", "dve")
                    v1 = P.run("vector", lambda e, bank=bank, col=col: e.tensor_scalar(
                        self.scr[:, 0:128], bank[:, 0:128], self.rcol[:, col:col + 1], None, ALU.mult), inc="dve")
                    self.bank_gate[nmm % 2] = ("dve", v1)
                    P.wait("vector", "dve", v1)
                    v2 = P.run("vector", lambda e: e.tensor_tensor(self.scr2[:, 0:128], self.scr[:, 0:128], self.bb_sb[:, :], ALU.add), inc="dve")
                    P.wait("vector", "dve", v2)
                    ri = self.rot_gate_acquire_next()
                    rt = self.rot[:, ri * 512:ri * 512 + 512]
                    tsub_abs = ti * 4 + ts_
                    if o < 34:
                        self._rope(ri, tsub_abs)
                        vrp = P.now("dve")
                        P.wait("tensor", "dve", vrp)
                        vtp = P.run("tensor", lambda e, rt=rt: e.transpose(
                            self.ps_tp[:, 0:128], rt[:, 0:128], self.sb_ident[:, :]), inc="pe")
                        ri2 = self.rot_gate_acquire_next()
                        rt2 = self.rot[:, ri2 * 512:ri2 * 512 + 512]
                        P.wait("vector", "pe", vtp)
                        P.fence("vector", "dve")
                        vcp = P.run("vector", lambda e, rt2=rt2: e.tensor_copy(rt2[:, 0:128], self.ps_tp[:, 0:128]), inc="dve")
                        P.wait("scalar", "dve", vcp)
                        if o < 32:
                            vq = self._qnorm(rt2[:, 0:128], 128, self.qn[0:1, ti * 32 + o:ti * 32 + o + 1])
                            vd = P.run("scalar", lambda e, rt2=rt2, o=o, ti=ti, ts_=ts_, qd=qd: e.dma_start(
                                qd[o * 128:(o + 1) * 128, ti * TT + ts_ * 128:ti * TT + ts_ * 128 + 128], rt2[:, 0:128]), inc=f"dr{ri2}", by=16)
                        else:
                            kh = o - 32
                            vq = self._qnorm(rt2[:, 0:128], 128, self.small[0:1, 4 + kh:5 + kh])
                            vd = P.run("scalar", lambda e, rt2=rt2, kh=kh, ti=ti, ts_=ts_: e.dma_start(
                                self.sd("k_own", 1)[kh * 128:(kh + 1) * 128, ti * TT + ts_ * 128:ti * TT + ts_ * 128 + 128], rt2[:, 0:128]), inc=f"dr{ri2}", by=16)
                        self.rot_gate.release(ri2, f"dr{ri2}", vd)
                        self.rot_gate.release(ri, "pe", vtp)
                    else:
                        dh = o - 34
                        vcv = P.run("vector", lambda e, rt=rt: e.tensor_copy(rt[:, 0:128], self.scr2[:, 0:128]), inc="dve")
                        P.wait("scalar", "dve", vcv)
                        vd = P.run("scalar", lambda e, rt=rt, dh=dh, ti=ti, ts_=ts_: e.dma_start(
                            self.sd("v_own", 1)[ti * TT + ts_ * 128:ti * TT + ts_ * 128 + 128, dh * 128:(dh + 1) * 128], rt[:, 0:128]), inc=f"dr{ri}", by=16)
                        self.rot_gate.release(ri, f"dr{ri}", vd)
                    nmm += 1

    def _rope(self, ri, tsub_abs):
        P = self.P
        base = ri * 512
        rt = self.rot[:, base:base + 512]
        cs = self.sb_rope[:, tsub_abs * 32:tsub_abs * 32 + 32]
        sn = self.sb_rope[:, 256 + tsub_abs * 32:256 + tsub_abs * 32 + 32]
        s2 = self.scr2
        scr = self.scr

        def xap(t, off, stride, count, width):
            return bass.AP(t, off, [[width, 128], [stride, count]])

        x1 = xap(s2, 0, 2, 32, 512)
        x2 = xap(s2, 1, 2, 32, 512)
        o1 = xap(self.rot, base + 0, [2, 32][0], 32, 2048)
        o2 = xap(self.rot, base + 1, 2, 32, 2048)
        P.fence("vector", "dve")
        va = P.run("vector", lambda e: e.tensor_tensor(scr[:, 0:32], x1, cs, ALU.mult), inc="dve")
        vb = P.run("vector", lambda e: e.tensor_tensor(scr[:, 32:64], x2, sn, ALU.mult), inc="dve")
        P.wait("vector", "dve", vb)
        vc = P.run("vector", lambda e: e.tensor_tensor(o1, scr[:, 0:32], scr[:, 32:64], ALU.subtract), inc="dve")
        P.wait("vector", "dve", vc)
        vd = P.run("vector", lambda e: e.tensor_tensor(scr[:, 0:32], x2, cs, ALU.mult), inc="dve")
        ve = P.run("vector", lambda e: e.tensor_tensor(scr[:, 32:64], x1, sn, ALU.mult), inc="dve")
        P.wait("vector", "dve", ve)
        vf = P.run("vector", lambda e: e.tensor_tensor(o2, scr[:, 0:32], scr[:, 32:64], ALU.add), inc="dve")
        vg = P.run("vector", lambda e: e.tensor_copy(rt[:, 64:128], s2[:, 64:128]), inc="dve")
        return vg

    def span_tiles(self, qt):
        tiles = []
        nch = 4 if qt == 0 else 8
        for c in range(nch - 1):
            for w in range(4):
                tiles.append((c, w, False))
        for w in range(4):
            tiles.append((qt, w, True))
        return tiles

    def att_flash(self, L):
        P = self.P
        kvh_of = (lambda h: 0) if L == 0 else (lambda h: h // 16)
        nheads_per_kv = 32 if L == 0 else 16
        wname = "l0_wo" if L == 0 else "l1_dense"
        P.wait("sync", "pe", P.now("pe"))
        P.wait("sync", "dve", P.now("dve"))
        if self.mask_gate is not None:
            P.wait("sync", *self.mask_gate)
        P.run("sync", lambda e: e.dma_start(self.sb_mask[:, :], self.c_mask[:, :]), inc="dmsk", by=16)
        vmk = P.run("sync", lambda e: e.dma_start(self.vones[:, :], self.dram["vones"][:, :]), inc="dmsk", by=16)
        vnl = P.run("sync", lambda e: e.dma_start(
            self.small[0:1, 8:16].rearrange("p (r c) -> p r c", r=4),
            self.sd("n_all", L)[:, :].rearrange("(o r) c -> o r c", o=1)[0]), inc="dnrm", by=16)
        vql0 = P.run("sync", lambda e: e.dma_start(self.qn[0:1, 0:64], self.sd("qn_st", L)[:, :]), inc="dnrm", by=16)
        P.wait("vector", "dnrm", vql0)
        P.wait("vector", "dmsk", vmk)
        P.fence("vector", "dve")
        for kh in range(2 if L == 1 else 1):
            ap_in = bass.AP(self.small, 8 + kh, [[16, 1], [2, 4]])
            P.run("vector", lambda e, ap_in=ap_in, kh=kh: e.reduce_max(self.small[0:1, 1 + kh:2 + kh], ap_in, axis=mybir.AxisListType.X), inc="dve")
        vkm = P.now("dve")

        for qt in range(2):
            nkt = 16 if qt == 0 else 32
            vq0 = P.now("dve")
            P.wait("sync", "dve", vq0)
            for h in range(32):
                P.run("sync", lambda e, h=h, qt=qt: e.dma_start(
                    self.fl_q[:, h * 512:(h + 1) * 512],
                    self.sd("qd", L)[h * 128:(h + 1) * 128, qt * TT:(qt + 1) * TT]), inc="dq", by=16)
            vql = P.now("dq")
            nkv = 1 if L == 0 else 2
            P.wait("vector", "dnrm", P.now("dnrm"))
            P.wait("vector", "dve", vkm)
            P.fence("vector", "dve")
            if L == 0:
                v1 = P.run("vector", lambda e, qt=qt: e.tensor_scalar(
                    self.qn[0:1, qt * 32:qt * 32 + 32], self.qn[0:1, qt * 32:qt * 32 + 32], self.small[0:1, 1:2], None, ALU.mult), inc="dve")
            else:
                P.run("vector", lambda e, qt=qt: e.tensor_scalar(
                    self.qn[0:1, qt * 32:qt * 32 + 16], self.qn[0:1, qt * 32:qt * 32 + 16], self.small[0:1, 1:2], None, ALU.mult), inc="dve")
                v1 = P.run("vector", lambda e, qt=qt: e.tensor_scalar(
                    self.qn[0:1, qt * 32 + 16:qt * 32 + 32], self.qn[0:1, qt * 32 + 16:qt * 32 + 32], self.small[0:1, 2:3], None, ALU.mult), inc="dve")
            P.wait("scalar", "dve", v1)
            v2 = P.run("scalar", lambda e, qt=qt: e.activation(self.qn[0:1, qt * 32:qt * 32 + 32], self.qn[0:1, qt * 32:qt * 32 + 32], AF.Sqrt), inc="act")
            P.wait("vector", "act", v2)
            v3 = P.run("vector", lambda e, qt=qt: e.tensor_scalar(
                self.qn[0:1, qt * 32:qt * 32 + 32], self.qn[0:1, qt * 32:qt * 32 + 32], -1.0, 64.0, ALU.mult, ALU.add), inc="dve")
            P.wait("tensor", "dve", v3)
            P.wait("tensor", "dve", self.evac_done)
            vmh = 0
            for h in range(32):
                P.wait("tensor", "dve", vmh)
                vpm = P.run("tensor", lambda e, h=h, qt=qt: e.matmul(
                    self.ps[4][:, 0:1], self.sb_ones_row_f[:1, :], self.qn[0:1, qt * 32 + h:qt * 32 + h + 1], start=True, stop=True), inc="pe")
                P.wait("vector", "pe", vpm)
                P.fence("vector", "dve")
                vmh = P.run("vector", lambda e, h=h: e.tensor_copy(self.mhat32[:, h:h + 1], self.ps[4][:, 0:1]), inc="dve")
            mhat_done = vmh

            for kvg in range(nkv):
                tiles = self.span_tiles(qt)
                if self.kv_gate is not None:
                    P.wait("sync", *self.kv_gate)
                    self.kv_gate = None
                for kt, (c, w, own) in enumerate(tiles):
                    if own:
                        ksrc = self.sd("k_own", L)[kvg * 128 * (L):kvg * 128 * (L) + 128, c * TT + w * 128:c * TT + w * 128 + 128] if L == 1 else                                self.sd("k_own", L)[:, c * TT + w * 128:c * TT + w * 128 + 128]
                        vsrc = self.sd("v_own", L)[c * TT + w * 128:c * TT + w * 128 + 128, kvg * 128:(kvg + 1) * 128] if L == 1 else                                self.sd("v_own", L)[c * TT + w * 128:c * TT + w * 128 + 128, :]
                    else:
                        rk = c if c < 4 else 7 - c
                        sl = 0 if c < 4 else 1
                        rstride = 128 * nkv
                        ksrc = self.sd("k_all", L)[rk * rstride + kvg * 128:rk * rstride + kvg * 128 + 128,
                                                  sl * TT + w * 128:sl * TT + w * 128 + 128]
                        vsrc = self.sd("v_all", L)[rk * TOK + sl * TT + w * 128:rk * TOK + sl * TT + w * 128 + 128,
                                                  kvg * 128:(kvg + 1) * 128] if L == 1 else                                self.sd("v_all", L)[rk * TOK + sl * TT + w * 128:rk * TOK + sl * TT + w * 128 + 128, :]
                    P.run("sync", lambda e, kt=kt, ksrc=ksrc: e.dma_start(
                        self.kv_k[:, kt * 128:(kt + 1) * 128], ksrc), inc="dkv", by=16)
                    P.run("sync", lambda e, kt=kt, vsrc=vsrc: e.dma_start(
                        self.kv_v[:, kt * 132:kt * 132 + 128], vsrc), inc="dkv", by=16)
                vkv = P.now("dkv")
                P.wait("vector", "dkv", vkv)
                P.fence("vector", "dve")
                vo = None
                for kt in range(nkt):
                    P.run("vector", lambda e, kt=kt, qt=qt: e.tensor_scalar(
                        self.kv_v[:, kt * 132:kt * 132 + 128], self.kv_v[:, kt * 132:kt * 132 + 128],
                        self.vones[:, qt * 32 + kt:qt * 32 + kt + 1], None, ALU.mult), inc="dve")
                    vo = P.run("vector", lambda e, kt=kt, qt=qt: e.tensor_copy(
                        self.kv_v[:, kt * 132 + 128:kt * 132 + 129], self.vones[:, qt * 32 + kt:qt * 32 + kt + 1]), inc="dve")
                vones_done = vo

                h0 = kvg * nheads_per_kv
                for hh in range(nheads_per_kv):
                    h = h0 + hh
                    self._flash_head(L, qt, h, nkt, vkv, vql, vones_done, mhat_done)
                self.kv_gate = ("pe", P.now("pe"))

            self._out_proj(L, qt, wname)
        self.mask_gate = ("dve", P.now("dve"))
        self.resid_ready = ("dve", P.now("dve"))

    def _flash_head(self, L, qt, h, nkt, vkv, vql, vones_done, mhat_done):
        P = self.P
        nb = nkt // 2
        ctx_banks = [self.ps[4], self.ps[5], self.ps_x,
                     self.ps_tp[:, :].bitcast(F32)[:, 0:512]]
        qsl = self.fl_q[:, h * 512:(h + 1) * 512]
        P.wait("tensor", "dq", vql)
        P.wait("tensor", "dkv", vkv)
        P.wait("tensor", "dve", max(vones_done, mhat_done, self.evac_done))
        expd = {}
        qkd = {}
        pvd = {}
        maskd = {}
        for b in range(nb):
            s0 = self.ps[(2 * b) % 4]
            s1 = self.ps[(2 * b + 1) % 4]
            if b >= 2:
                P.wait("tensor", "act", expd[b - 2])
            P.run("tensor", lambda e, s0=s0, kt=2 * b, qsl=qsl: e.matmul(
                s0[:, :], self.kv_k[:, kt * 128:(kt + 1) * 128], qsl, start=True, stop=True))
            qkd[b] = P.run("tensor", lambda e, s1=s1, kt=2 * b + 1, qsl=qsl: e.matmul(
                s1[:, :], self.kv_k[:, kt * 128:(kt + 1) * 128], qsl, start=True, stop=True), inc="pe")
            for kk in (2 * b, 2 * b + 1):
                if kk >= nkt - 4:
                    w = kk - (nkt - 4)
                    sb = self.ps[kk % 4]
                    P.wait("vector", "pe", qkd[b])
                    P.fence("vector", "dve")
                    maskd[kk] = P.run("vector", lambda e, sb=sb, w=w: e.tensor_tensor(
                        sb[:, :], sb[:, :], self.sb_mask[:, 384 - 128 * w:896 - 128 * w], ALU.add), inc="dve")
            P.wait("scalar", "pe", qkd[b])
            if (2 * b) >= nkt - 4 or (2 * b + 1) >= nkt - 4:
                P.wait("scalar", "dve", max(v for k, v in maskd.items() if k in (2 * b, 2 * b + 1)))
            if b >= 2:
                P.wait("scalar", "pe", pvd.get(b - 2, 0))
            P.run("scalar", lambda e, s0=s0, b=b, h=h: e.activation(
                self.fl_pr[:, ((2 * b) % 4) * 512:((2 * b) % 4) * 512 + 512], s0[:, :], AF.Exp,
                bias=self.mhat32[:, h:h + 1], scale=1.0))
            expd[b] = P.run("scalar", lambda e, s1=s1, b=b, h=h: e.activation(
                self.fl_pr[:, ((2 * b + 1) % 4) * 512:((2 * b + 1) % 4) * 512 + 512], s1[:, :], AF.Exp,
                bias=self.mhat32[:, h:h + 1], scale=1.0), inc="act")
            P.wait("tensor", "act", expd[b])
            vp = None
            for kk in (2 * b, 2 * b + 1):
                pr = self.fl_pr[:, (kk % 4) * 512:(kk % 4) * 512 + 512]
                for qs in range(4):
                    vp = P.run("tensor", lambda e, pr=pr, qs=qs, kk=kk, b=b: e.matmul(
                        ctx_banks[qs][:, 0:129], pr[:, qs * 128:(qs + 1) * 128],
                        self.kv_v[:, kk * 132:kk * 132 + 129],
                        start=(kk == 0), stop=(kk == nkt - 1)),
                        inc="pe" if (kk == 2 * b + 1 and qs == 3) else None)
            pvd[b] = vp
        P.wait("vector", "pe", pvd[nb - 1])
        P.fence("vector", "dve")
        tpb = self.ps[0].bitcast(BF16) if False else self.ps[0]
        for qs in (3, 2, 1, 0):
            cb = ctx_banks[qs]
            tpd = self.ps_tp[:, 384 + qs * 128:384 + qs * 128 + 128]
            vr = P.run("vector", lambda e, cb=cb: e.reciprocal(self.small[:, 3:4], cb[:, 128:129]), inc="dve")
            P.wait("vector", "dve", vr)
            vs = P.run("vector", lambda e, cb=cb, qs=qs: e.tensor_scalar(
                self.fl_ct[:, qs * 128:(qs + 1) * 128], cb[:, 0:128], self.small[:, 3:4], None, ALU.mult), inc="dve")
            P.wait("tensor", "dve", vs)
            vt = P.run("tensor", lambda e, tpd=tpd, qs=qs: e.transpose(
                tpd, self.fl_ct[:, qs * 128:(qs + 1) * 128], self.sb_ident[:, :]), inc="pe")
            P.wait("vector", "pe", vt)
            ve = P.run("vector", lambda e, tpd=tpd, qs=qs, h=h: e.tensor_copy(
                self.fl_q[:, h * 512 + qs * 128:h * 512 + qs * 128 + 128], tpd), inc="dve")
        self.evac_done = P.now("dve")

    def _out_proj(self, L, qt, wname):
        P = self.P
        P.wait("tensor", "dve", self.evac_done)
        nmm = 0
        for o in range(NKC):
            vw = self.load_wslice(wname, o)
            bank = self.ps[nmm % 2]
            P.wait("tensor", "dwt", vw)
            if self.bank_gate[nmm % 2] is not None:
                nm, vv = self.bank_gate[nmm % 2]
                P.wait("tensor", nm, vv)
                self.bank_gate[nmm % 2] = None
            vmm = None
            for kc in range(NKC):
                w = self.wchunk(kc)
                cx = self.fl_q[:, kc * 512:(kc + 1) * 512]
                vmm = P.run("tensor", lambda e, bank=bank, w=w, cx=cx, kc=kc: e.matmul(
                    bank[:, :], w, cx, start=(kc == 0), stop=(kc == NKC - 1)),
                    inc="pe" if kc == NKC - 1 else None)
            self.wbf_gate[0] = ("pe", vmm)
            self.wbf_gate[1] = ("pe", vmm)
            P.wait("vector", "pe", vmm)
            P.fence("vector", "dve")
            xa = self.x_res[:, o * TOK + qt * TT:o * TOK + qt * TT + TT]
            va = P.run("vector", lambda e, bank=bank, xa=xa: e.tensor_tensor(xa, bank[:, :], xa, ALU.add), inc="dve")
            self.bank_gate[nmm % 2] = ("dve", va)
            nmm += 1

    def ffn1(self):
        P = self.P
        self.rot_gate = Gate(4)
        self._rot_next = 0
        self._rot_pool = (3,)
        self.routeT_gate = 0
        self.stats_cast(0, want_rcol=True)
        self.stats_cast(1, want_rcol=True)
        castdone = self.cast_done
        P.wait("tensor", "dve", castdone)
        if self.wg_gate is not None:
            P.wait("sync", *self.wg_gate)
            self.wg_gate = None
        vwg = P.run("sync", lambda e: e.dma_start(
            self.wg_sb[:, :].rearrange("p (kc o) -> p kc o", o=8),
            self.dram["f1_wg"][:, :].rearrange("(kc p) o -> p kc o", p=128)), inc="dwg", by=16)
        for ti in range(2):
            for ts_ in range(4):
                col = ti * 4 + ts_
                bank = self.ps[0]
                P.wait("tensor", "dwg", vwg)
                if self.bank_gate[0] is not None:
                    P.wait("tensor", *self.bank_gate[0]); self.bank_gate[0] = None
                vmm = None
                for kc in range(NKC):
                    xsl = self.xbf[:, (ti * NKC + kc) * TT + ts_ * 128:(ti * NKC + kc) * TT + ts_ * 128 + 128]
                    wg = self.wg_sb[:, kc * 8:(kc + 1) * 8]
                    vmm = P.run("tensor", lambda e, bank=bank, xsl=xsl, wg=wg, kc=kc: e.matmul(
                        bank[:, 0:8], xsl, wg, start=(kc == 0), stop=(kc == NKC - 1)),
                        inc="pe" if kc == NKC - 1 else None)
                P.wait("vector", "pe", vmm)
                P.fence("vector", "dve")
                v1 = P.run("vector", lambda e, bank=bank, col=col: e.tensor_scalar(
                    self.lg_sb[:, :], bank[:, 0:8], self.rcol[:, col:col + 1], None, ALU.mult), inc="dve")
                self.bank_gate[0] = ("dve", v1)
                P.wait("vector", "dve", v1)
                v2 = P.run("vector", lambda e: e.reduce_max(self.small[:, 3:4], self.lg_sb[:, :], axis=mybir.AxisListType.X, negate=True), inc="dve")
                P.wait("scalar", "dve", v2)
                v3 = P.run("scalar", lambda e: e.activation(self.scr[:, 0:8], self.lg_sb[:, :], AF.Exp, bias=self.small[:, 3:4], scale=1.0), inc="act")
                P.wait("vector", "dve", v2)
                v4 = P.run("vector", lambda e: e.tensor_scalar(self.scr2[:, 0:8], self.lg_sb[:, :], self.small[:, 3:4], None, ALU.add), inc="dve")
                P.wait("vector", "dve", v4)
                v5 = P.run("vector", lambda e: e.tensor_single_scalar(self.scr2[:, 8:16], self.scr2[:, 0:8], -1e-30, ALU.is_ge), inc="dve")
                P.wait("vector", "dve", v5)
                v6 = P.run("vector", lambda e: e.scalar_tensor_tensor(self.scr2[:, 0:8], self.scr2[:, 8:16], NEG, self.lg_sb[:, :], ALU.mult, ALU.add), inc="dve")
                P.wait("vector", "dve", v6)
                v7 = P.run("vector", lambda e: e.reduce_max(self.small[:, 4:5], self.scr2[:, 0:8], axis=mybir.AxisListType.X), inc="dve")
                P.wait("vector", "dve", v7)
                v8 = P.run("vector", lambda e: e.tensor_single_scalar(self.scr2[:, 0:8], self.lg_sb[:, :], self.small[:, 4:5], ALU.is_ge), inc="dve")
                P.wait("vector", "dve", v8)
                P.wait("vector", "act", v3)
                v9 = P.run("vector", lambda e: e.tensor_tensor(self.scr[:, 8:16], self.scr[:, 0:8], self.scr2[:, 0:8], ALU.mult), inc="dve")
                P.wait("vector", "dve", v9)
                va = P.run("vector", lambda e: e.reduce_sum(self.small[:, 5:6], self.scr[:, 8:16], axis=mybir.AxisListType.X), inc="dve")
                P.wait("vector", "dve", va)
                vb = P.run("vector", lambda e: e.reciprocal(self.small[:, 6:7], self.small[:, 5:6]), inc="dve")
                P.wait("vector", "dve", vb)
                ri = self.rot_gate_acquire_next()
                rt = self.rot[:, ri * 512:ri * 512 + 512]
                vc = P.run("vector", lambda e, rt=rt: e.tensor_scalar(rt[:, 0:8], self.scr[:, 8:16], self.small[:, 6:7], None, ALU.mult), inc="dve")
                P.wait("tensor", "dve", vc)
                P.wait("tensor", "pe", self.routeT_gate)
                vt = P.run("tensor", lambda e, rt=rt: e.transpose(self.ps_tp[0:8, 0:128], rt[:, 0:8], self.sb_ident[:, :]), inc="pe")
                P.wait("vector", "pe", vt)
                P.fence("vector", "dve")
                vu = P.run("vector", lambda e, ts_=ts_: e.tensor_copy(self.route_T[0:8, ts_ * 128:(ts_ + 1) * 128], self.ps_tp[0:8, 0:128]), inc="dve")
                self.rot_gate.release(ri, "pe", vt)
            vwa = self.load_wslice("f1_at", 0)
            bank = self.ps[1]
            P.wait("tensor", "dwt", vwa)
            if self.bank_gate[1] is not None:
                P.wait("tensor", *self.bank_gate[1]); self.bank_gate[1] = None
            vmm = None
            for kc in range(NKC):
                xb = self.xbf[:, (ti * NKC + kc) * TT:(ti * NKC + kc) * TT + TT]
                w = self.wchunk(kc)
                vmm = P.run("tensor", lambda e, bank=bank, w=w, xb=xb, kc=kc: e.matmul(
                    bank[:, :], w, xb, start=(kc == 0), stop=(kc == NKC - 1)),
                    inc="pe" if kc == NKC - 1 else None)
            self.wbf_gate[0] = ("pe", vmm)
            self.wbf_gate[1] = ("pe", vmm)
            vre = P.now("dve")
            P.wait("tensor", "dve", vre)
            vx = P.run("tensor", lambda e: e.matmul(self.ps[2][:, :], self.sb_expand[0:8, :], self.route_T[0:8, :], start=True, stop=True), inc="pe")
            self.routeT_gate = vx
            bcr = (self.bc0, self.bc1)[ti]
            P.wait("vector", "pe", vx)
            P.fence("vector", "dve")
            v1 = P.run("vector", lambda e, bcr=bcr: e.tensor_tensor(self.scr[:, :], self.ps[2][:, :], bcr[:, :], ALU.mult), inc="dve")
            P.wait("vector", "pe", vmm)
            P.wait("vector", "dve", v1)
            v2 = P.run("vector", lambda e, bank=bank, ti=ti: e.tensor_tensor(
                self.hw_sb[:, ti * 512:(ti + 1) * 512], bank[:, :], self.scr[:, :], ALU.mult), inc="dve")
            self.bank_gate[1] = ("dve", v2)
            self.hw_done = v2
        self.bsl_gate = [("pe", P.now("pe")), ("pe", P.now("pe"))]
        self.dbank_gate = [("dve", self.hw_done), ("dve", self.hw_done)]
        nmm = 0
        for o in range(NKC):
            vw = self.load_wslice("f1_wb", o)
            bhalf = self.wg_sb[:, (o % 2) * 128:(o % 2) * 128 + 128]
            if self.bsl_gate[o % 2] is not None:
                P.wait("sync", *self.bsl_gate[o % 2])
                self.bsl_gate[o % 2] = None
            vbs = P.run("sync", lambda e, bhalf=bhalf, o=o: e.dma_start(
                bhalf, self.dram["f1_bf"][:, o * 128:(o + 1) * 128]), inc="dbs", by=16)
            for ti in range(2):
                bank = self.ps[nmm % 2]
                dbank = self.ps[2 + nmm % 2]
                P.wait("tensor", "dwt", vw)
                if self.bank_gate[nmm % 2] is not None:
                    nm, vv = self.bank_gate[nmm % 2]
                    P.wait("tensor", nm, vv)
                    self.bank_gate[nmm % 2] = None
                if self.dbank_gate[nmm % 2] is not None:
                    nm, vv = self.dbank_gate[nmm % 2]
                    P.wait("tensor", nm, vv)
                    self.dbank_gate[nmm % 2] = None
                vmm = None
                for kc in range(NKC):
                    w = self.wchunk(kc)
                    xb = self.xbf[:, (ti * NKC + kc) * TT:(ti * NKC + kc) * TT + TT]
                    vmm = P.run("tensor", lambda e, bank=bank, w=w, xb=xb, kc=kc: e.matmul(
                        bank[:, :], w, xb, start=(kc == 0), stop=(kc == NKC - 1)),
                        inc="pe" if kc == NKC - 1 else None)
                self.wbf_gate[0] = ("pe", vmm)
                self.wbf_gate[1] = ("pe", vmm)
                P.wait("tensor", "dbs", vbs)
                vdm = P.run("tensor", lambda e, dbank=dbank, bhalf=bhalf, ti=ti: e.matmul(
                    dbank[:, :], bhalf, self.hw_sb[:, ti * 512:(ti + 1) * 512], start=True, stop=True), inc="pe")
                self.bsl_gate[o % 2] = ("pe", vdm)
                bcr = (self.bc0, self.bc1)[ti]
                P.wait("vector", "pe", vdm)
                P.fence("vector", "dve")
                v1 = P.run("vector", lambda e, bank=bank, bcr=bcr: e.tensor_tensor(self.scr[:, :], bank[:, :], bcr[:, :], ALU.mult), inc="dve")
                self.bank_gate[nmm % 2] = ("dve", v1)
                P.wait("vector", "dve", v1)
                v2 = P.run("vector", lambda e, dbank=dbank, o=o: e.scalar_tensor_tensor(
                    self.scr2[:, :], dbank[:, :], self.vec[:, 6 * 32 + o:6 * 32 + o + 1], self.scr[:, :], ALU.add, ALU.add), inc="dve")
                self.dbank_gate[nmm % 2] = ("dve", v2)
                P.wait("vector", "dve", v2)
                xa = self.x_res[:, o * TOK + ti * TT:o * TOK + ti * TT + TT]
                v3 = P.run("vector", lambda e, xa=xa: e.scalar_tensor_tensor(xa, self.scr2[:, :], 0.5, xa, ALU.mult, ALU.add), inc="dve")
                nmm += 1
        self.resid_ready = ("dve", P.now("dve"))
        self._rot_pool = (0, 1, 2, 3)


    def prep(self):
        P = self.P
        groups = [[0, 1, 2, 3], [4, 5, 6, 7]]
        order = [sp for sp in PREP_SPECS if sp[0] in PREP_SHARDED] + \
                [sp for sp in PREP_SPECS if sp[0] not in PREP_SHARDED]
        for name, rows, cols, tern in order:
            shard = name in PREP_SHARDED
            win = self.dram[name + "_f"]
            wout = self.dram[name + "_part"] if shard else self.dram[name]
            lrows = rows // 4 if shard else rows
            if cols % 1024 == 0 and not tern:
                width = 1024
            elif cols % 512 == 0:
                width = 512
            else:
                width = cols
            ntile = max(1, cols // width)
            for rb in range(max(1, lrows // 128)):
                pr = min(128, lrows)
                for tb in range(ntile):
                    if self.pp_gate[0] is not None:
                        P.wait("sync", *self.pp_gate[0])
                        self.pp_gate[0] = None
                    stw = self.w_bf0[:, :].bitcast(F32) if width == 1024 else self.w_st
                    vd = P.run("sync", lambda e, win=win, rb=rb, tb=tb, width=width, pr=pr, stw=stw: e.dma_start(
                        stw[0:pr, :width], win[rb * 128:rb * 128 + pr, tb * width:(tb + 1) * width]), inc="dpp", by=16)
                    rt = self.rot[:, 0:1024] if width == 1024 else self.rot[:, 0:512]
                    if self.pp_gate[1] is not None:
                        P.wait("scalar", *self.pp_gate[1])
                        P.wait("vector", *self.pp_gate[1])
                        self.pp_gate[1] = None
                    if not tern:
                        P.wait("scalar", "dpp", vd)
                        vc = P.run("scalar", lambda e, rt=rt, width=width, pr=pr, stw=stw: e.copy(rt[0:pr, :width], stw[0:pr, :width]), inc="act")
                        self.pp_gate[0] = ("act", vc)
                        gate = ("act", vc)
                    else:
                        P.wait("scalar", "dpp", vd)
                        v1 = P.run("scalar", lambda e, width=width, pr=pr: e.activation(self.scr[0:pr, :width], self.w_st[0:pr, :width], AF.Sign), inc="act")
                        v1b = P.run("scalar", lambda e, width=width, pr=pr: e.activation(self.scr2[0:pr, :width], self.w_st[0:pr, :width], AF.Abs), inc="act")
                        self.pp_gate[0] = ("act", v1b)
                        P.wait("vector", "act", v1b)
                        P.fence("vector", "dve")
                        v2 = P.run("vector", lambda e, width=width, pr=pr: e.tensor_single_scalar(self.scr2[0:pr, :width], self.scr2[0:pr, :width], 0.1, ALU.is_ge), inc="dve")
                        P.wait("vector", "dve", v2)
                        vc = P.run("vector", lambda e, rt=rt, width=width, pr=pr: e.tensor_tensor(rt[0:pr, :width], self.scr[0:pr, :width], self.scr2[0:pr, :width], ALU.mult), inc="dve")
                        gate = ("dve", vc)
                    P.wait("scalar", gate[0], gate[1])
                    vo = P.run("scalar", lambda e, wout=wout, rb=rb, tb=tb, rt=rt, width=width, pr=pr: e.dma_start(
                        wout[rb * 128:rb * 128 + pr, tb * width:(tb + 1) * width], rt[0:pr, :width]), inc="dpo", by=16)
                    self.pp_gate[1] = ("dpo", vo)
            if shard:
                a = self.dram[name + "_part"]
                b = self.dram[name]
                P.wait("gpsimd", "dpo", P.now("dpo"))
                P.run("gpsimd", lambda e, a=a, b=b, groups=groups: e.collective_compute(
                    "AllGather", ALU.bypass, replica_groups=groups,
                    ins=[a.ap().opt()], outs=[b.ap().opt()]), inc="cc", by=1)
        vtot = P.now("dpo")
        P.wait("sync", "dpo", vtot)

    def collect_kv(self, L):
        P = self.P
        for rr in range(4):
            P.wait("gpsimd", f"dr{rr}", P.now(f"dr{rr}"))
        P.wait("gpsimd", "dnrm", P.now("dnrm"))
        groups = [[0, 1, 2, 3], [4, 5, 6, 7]]
        for src, dst in (("k_own", "k_all"), ("v_own", "v_all"), ("n_own", "n_all")):
            a = self.sd(src, L)
            b = self.sd(dst, L)
            P.run("gpsimd", lambda e, a=a, b=b, groups=groups: e.collective_compute(
                "AllGather", ALU.bypass, replica_groups=groups,
                ins=[a.ap().opt()], outs=[b.ap().opt()]), inc="cc", by=1)
        v = P.now("cc")
        P.wait("sync", "cc", v)
        P.wait("vector", "cc", v)
        P.wait("tensor", "cc", v)
        P.wait("scalar", "cc", v)

    def cc0(self):
        self.collect_kv(0)

    def cc1(self):
        self.collect_kv(1)
    def att_pre0(self):
        self.att_pre(0)

    def att_pre1(self):
        self.att_pre(1)

    def att_flash0(self):
        self.att_flash(0)

    def att_flash1(self):
        self.att_flash(1)

    def build(self, phases):
        self.pp_gate = [None, None]
        self.resid_ready = None
        self.mask_gate = None
        self.kv_gate = None
        self.wg_gate = None
        self.bbsb_gate = None
        self.bsl_gate = [None, None]
        self.evac_done = 0
        self.qn_gate = 0
        self._rot_next = 0
        self.wbf_gate = [None, None]
        self.wst_gate = [None, None]
        self.bank_gate = [None, None]
        self.p2_gate = [None, None]
        self.rot_gate = Gate(4)
        self.load_consts()
        self.load_x()
        self.preload_vectors(phases)
        for ph in phases:
            getattr(self, ph)()
            if self.fused:
                self.P.barrier()
        self.store_x()
        self.P.finish()
        return self.nc



PREP_SPECS = [
    ("l0_wq", D, D, False), ("l0_wk", D, 128, False), ("l0_wv", D, 128, False),
    ("l0_wo", D, D, False), ("tern", D, D, True),
    ("l1_qkvw", D, 4608, False), ("l1_dense", D, D, False),
    ("f1_wb", D, D, False), ("f1_at", D, 128, False), ("f1_wg", D, 8, False),
    ("f1_bf", 128, D, False),
]

PREP_SHARDED = {"l0_wq", "l0_wo", "tern", "l1_qkvw", "l1_dense", "f1_wb"}


def _prep_program():
    nc = bass.Bass("TRN2", target_bir_lowering=False)
    P = Prog(nc)
    st = nc.alloc_sbuf_tensor("st", [128, 1024], F32)
    sg = nc.alloc_sbuf_tensor("sg", [128, 1024], F32)
    ot = nc.alloc_sbuf_tensor("ot", [128, 1024], BF16)
    g_st = [None]
    g_ot = [None]
    for name, rows, cols, tern in PREP_SPECS:
        win = nc.dram_tensor(name + "_f", [rows, cols], F32, kind="ExternalInput")
        wout = nc.dram_tensor(name, [rows, cols], BF16, kind="ExternalOutput")
        if cols % 1024 == 0:
            width = 1024
        elif cols % 512 == 0:
            width = 512
        else:
            width = cols
        ntile = max(1, cols // width)
        for rb in range(rows // 128):
            for tb in range(ntile):
                if g_st[0] is not None:
                    P.wait("sync", *g_st[0])
                vd = P.run("sync", lambda e, win=win, rb=rb, tb=tb, width=width: e.dma_start(
                    st[:, :width], win[rb * 128:(rb + 1) * 128, tb * width:(tb + 1) * width]), inc="d", by=16)
                if g_ot[0] is not None:
                    P.wait("scalar" if not tern else "vector", *g_ot[0])
                if not tern:
                    P.wait("scalar", "d", vd)
                    vc = P.run("scalar", lambda e, width=width: e.copy(ot[:, :width], st[:, :width]), inc="a")
                    g_st[0] = ("a", vc)
                    gate = ("a", vc)
                else:
                    P.wait("scalar", "d", vd)
                    v1 = P.run("scalar", lambda e, width=width: e.activation(sg[:, :width], st[:, :width], AF.Sign), inc="a")
                    v1b = P.run("scalar", lambda e, width=width: e.activation(st[:, :width], st[:, :width], AF.Abs), inc="a")
                    P.wait("vector", "a", v1b)
                    v2 = P.run("vector", lambda e, width=width: e.tensor_single_scalar(st[:, :width], st[:, :width], 0.1, ALU.is_ge), inc="v")
                    P.wait("vector", "v", v2)
                    vc = P.run("vector", lambda e, width=width: e.tensor_tensor(ot[:, :width], sg[:, :width], st[:, :width], ALU.mult), inc="v")
                    g_st[0] = ("v", vc)
                    gate = ("v", vc)
                P.wait("scalar", gate[0], gate[1])
                vo = P.run("scalar", lambda e, wout=wout, rb=rb, tb=tb, width=width: e.dma_start(
                    wout[rb * 128:(rb + 1) * 128, tb * width:(tb + 1) * width], ot[:, :width]), inc="do", by=16)
                g_ot[0] = ("do", vo)
    P.wait("scalar", "do", P.now("do"))
    P.finish()
    return nc


def _mode_for(phases, fused):
    IN, OUT, INT = "ExternalInput", "ExternalOutput", "Internal"
    stg = INT if fused else None
    m = {"x_own": ((D, TOK), F32, IN), "x_out": ((D, TOK), F32, OUT)}
    ph = set(phases)
    if "ffn0" in ph:
        m.update({"tern": ((D, D), BF16, IN), "y2": ((D, TOK), BF16, INT),
                  "f0_bias": ((128, 32), F32, IN), "f0_scale": ((128, 32), F32, IN),
                  "f0_gamma": ((128, 32), F32, IN), "f0_beta": ((128, 32), F32, IN),
                  "f0_qs": ((1, 1), F32, IN)})
    if "ffn1" in ph:
        m.update({"f1_wb": ((D, D), BF16, IN), "f1_at": ((D, 128), BF16, IN),
                  "f1_wg": ((D, 8), BF16, IN), "f1_bf": ((128, D), BF16, IN),
                  "f1_bbase": ((128, 32), F32, IN)})
    if "att_pre0" in ph:
        m.update({"l0_wq": ((D, D), BF16, IN), "l0_wk": ((D, 128), BF16, IN),
                  "l0_wv": ((D, 128), BF16, IN),
                  "qd": ((D, TOK), BF16, stg or OUT),
                  "k_own": ((128, TOK), BF16, stg or OUT),
                  "v_own": ((TOK, 128), BF16, stg or OUT),
                  "n_own": ((1, 2), F32, stg or OUT),
                  "qn_st": ((1, 64), F32, stg or OUT)})
    if "att_flash0" in ph:
        m.update({"l0_wo": ((D, D), BF16, IN), "vones": ((128, 64), F32, IN)})
        if not fused:
            m.update({"qd": ((D, TOK), BF16, IN), "qn_st": ((1, 64), F32, IN),
                      "k_own": ((128, TOK), BF16, IN), "v_own": ((TOK, 128), BF16, IN),
                      "k_all": ((512, TOK), BF16, IN), "v_all": ((4 * TOK, 128), BF16, IN),
                      "n_all": ((4, 2), F32, IN)})
        else:
            m.update({"k_all": ((512, TOK), BF16, INT), "v_all": ((4 * TOK, 128), BF16, INT),
                      "n_all": ((4, 2), F32, INT)})
    if "att_pre1" in ph:
        m.update({"l1_qkvw": ((D, 4608), BF16, IN), "l1_qkvb": ((1, 4608), F32, IN),
                  "rope_cos": ((TOK, 32), F32, IN), "rope_sin": ((TOK, 32), F32, IN),
                  "qd": ((D, TOK), BF16, stg or OUT),
                  "k_own": ((256, TOK), BF16, stg or OUT),
                  "v_own": ((TOK, 256), BF16, stg or OUT),
                  "n_own": ((1, 2), F32, stg or OUT),
                  "qn_st": ((1, 64), F32, stg or OUT)})
    if "att_flash1" in ph:
        m.update({"l1_dense": ((D, D), BF16, IN), "vones": ((128, 64), F32, IN)})
        if not fused:
            m.update({"qd": ((D, TOK), BF16, IN), "qn_st": ((1, 64), F32, IN),
                      "k_own": ((256, TOK), BF16, IN), "v_own": ((TOK, 256), BF16, IN),
                      "k_all": ((1024, TOK), BF16, IN), "v_all": ((4 * TOK, 256), BF16, IN),
                      "n_all": ((4, 2), F32, IN)})
        else:
            m.update({"k_all": ((1024, TOK), BF16, INT), "v_all": ((4 * TOK, 256), BF16, INT),
                      "n_all": ((4, 2), F32, INT)})
    if fused:
        for nm in ("qd", "k_own", "v_own", "n_own", "qn_st", "k_all", "v_all", "n_all"):
            m.pop(nm, None)
        m.update({
            "qd0": ((D, TOK), BF16, INT), "k_own0": ((128, TOK), BF16, INT),
            "v_own0": ((TOK, 128), BF16, INT), "n_own0": ((1, 2), F32, INT),
            "qn_st0": ((1, 64), F32, INT), "k_all0": ((512, TOK), BF16, INT),
            "v_all0": ((4 * TOK, 128), BF16, INT), "n_all0": ((4, 2), F32, INT),
            "qd1": ((D, TOK), BF16, INT), "k_own1": ((256, TOK), BF16, INT),
            "v_own1": ((TOK, 256), BF16, INT), "n_own1": ((1, 2), F32, INT),
            "qn_st1": ((1, 64), F32, INT), "k_all1": ((1024, TOK), BF16, INT),
            "v_all1": ((4 * TOK, 256), BF16, INT), "n_all1": ((4, 2), F32, INT)})
        for name, rows, cols, tern in PREP_SPECS:
            if name in m:
                m[name] = ((rows, cols), BF16, INT)
                if name in PREP_SHARDED:
                    m[name + "_f"] = ((rows // 4, cols), F32, IN)
                    m[name + "_part"] = ((rows // 4, cols), BF16, INT)
                else:
                    m[name + "_f"] = ((rows, cols), F32, IN)
    return m


def _host_shard_x(xfull):
    xf = np.ascontiguousarray(xfull.reshape(BATCH * SEQ, D).T)
    return [np.ascontiguousarray(xf[:, _rows_of_core(c)]) for c in range(NC)]


def _host_unshard_x(shards):
    xf = np.empty((D, BATCH * SEQ), dtype=np.float32)
    for c in range(NC):
        xf[:, _rows_of_core(c)] = shards[c]
    return np.ascontiguousarray(xf.T).reshape(BATCH, SEQ, D)


def _vones_for_core(c):
    v = np.zeros((128, 64), dtype=np.float32)
    j = c % 4
    for qt, g in ((0, j), (1, 7 - j)):
        nch = 4 if qt == 0 else 8
        kt = 0
        for cc in range(nch - 1):
            for w in range(4):
                v[:, qt * 32 + kt] = 1.0 if cc < g else 0.0
                kt += 1
        for w in range(4):
            v[:, qt * 32 + kt] = 1.0
            kt += 1
    return v


def vec128(v):
    return np.ascontiguousarray(np.asarray(v, dtype=np.float32).reshape(32, 128).T)


_PROG_CACHE = {}


def _get_prog(key, phases, fused):
    if key not in _PROG_CACHE:
        mk = MK(_mode_for(phases, fused), fused=fused)
        _PROG_CACHE[key] = (mk, mk.build(list(phases)))
    return _PROG_CACHE[key]


FUSED = True
_FUSED_PHASES = ("prep", "ffn0", "att_pre0", "cc0", "att_flash0", "ffn0",
                 "ffn1", "att_pre1", "cc1", "att_flash1", "ffn1")


def _weights_f(inputs):
    return {
        "l0_wq_f": np.ascontiguousarray(np.asarray(inputs["L0_attn_wq"], np.float32)),
        "l0_wk_f": np.ascontiguousarray(np.asarray(inputs["L0_attn_wk"], np.float32)),
        "l0_wv_f": np.ascontiguousarray(np.asarray(inputs["L0_attn_wv"], np.float32)),
        "l0_wo_f": np.ascontiguousarray(np.asarray(inputs["L0_attn_wo"], np.float32)),
        "tern_f": np.ascontiguousarray(np.asarray(inputs["L0_ffn_weight"], np.float32).T),
        "l1_qkvw_f": np.ascontiguousarray(np.asarray(inputs["L1_attn_qkv_w"], np.float32).T),
        "l1_dense_f": np.ascontiguousarray(np.asarray(inputs["L1_attn_dense_w"], np.float32).T),
        "f1_wb_f": np.ascontiguousarray(np.asarray(inputs["L1_ffn_W_base"], np.float32).T),
        "f1_at_f": np.ascontiguousarray(np.asarray(inputs["L1_ffn_A"], np.float32).reshape(128, D).T),
        "f1_wg_f": np.ascontiguousarray(np.asarray(inputs["L1_ffn_W_gate"], np.float32).T),
        "f1_bf_f": np.ascontiguousarray(np.asarray(inputs["L1_ffn_B"], np.float32).transpose(0, 2, 1).reshape(128, D)),
    }


def _vec_inputs(inputs):
    return {
        "f0_bias": vec128(inputs["L0_ffn_bias"]),
        "f0_scale": vec128(inputs["L0_ffn_scale"]),
        "f0_gamma": vec128(inputs["L0_ffn_ln_gamma"]),
        "f0_beta": vec128(inputs["L0_ffn_ln_beta"]),
        "f0_qs": np.asarray(inputs["L0_ffn_quant_scale"], np.float32).reshape(1, 1),
        "f1_bbase": vec128(inputs["L1_ffn_b_base"]),
        "l1_qkvb": np.ascontiguousarray(np.asarray(inputs["L1_attn_qkv_b"], np.float32).reshape(1, 4608)),
    }


def kernel(**inputs):
    cos, sin = _rope_tables()
    xs = _host_shard_x(np.asarray(inputs["L0_attn_x"], dtype=np.float32))

    if FUSED:
        mk, nc = _get_prog("fused", _FUSED_PHASES, fused=True)
        wf = _weights_f(inputs)
        base = _vec_inputs(inputs)
        for nm, arr in wf.items():
            if nm[:-2] not in PREP_SHARDED:
                base[nm] = arr
        vones = [_vones_for_core(c) for c in range(NC)]
        in_maps = []
        for c in range(NC):
            gr = c % 4
            im = dict(base, x_own=xs[c], vones=vones[c],
                      rope_cos=np.ascontiguousarray(cos[_positions_of_core(c)]),
                      rope_sin=np.ascontiguousarray(sin[_positions_of_core(c)]))
            for nm in PREP_SHARDED:
                arr = wf[nm + "_f"]
                q = arr.shape[0] // 4
                im[nm + "_f"] = arr[gr * q:(gr + 1) * q]
            in_maps.append(im)
        res = run_bass_kernel_spmd(nc, in_maps, core_ids=list(range(NC)))
        return _host_unshard_x([res.results[c]["x_out"] for c in range(NC)])

    if "prep" not in _PROG_CACHE:
        _PROG_CACHE["prep"] = _prep_program()
    ncp = _PROG_CACHE["prep"]
    prep_in = {
        "l0_wq_f": np.ascontiguousarray(np.asarray(inputs["L0_attn_wq"], np.float32)),
        "l0_wk_f": np.ascontiguousarray(np.asarray(inputs["L0_attn_wk"], np.float32)),
        "l0_wv_f": np.ascontiguousarray(np.asarray(inputs["L0_attn_wv"], np.float32)),
        "l0_wo_f": np.ascontiguousarray(np.asarray(inputs["L0_attn_wo"], np.float32)),
        "tern_f": np.ascontiguousarray(np.asarray(inputs["L0_ffn_weight"], np.float32).T),
        "l1_qkvw_f": np.ascontiguousarray(np.asarray(inputs["L1_attn_qkv_w"], np.float32).T),
        "l1_dense_f": np.ascontiguousarray(np.asarray(inputs["L1_attn_dense_w"], np.float32).T),
        "f1_wb_f": np.ascontiguousarray(np.asarray(inputs["L1_ffn_W_base"], np.float32).T),
        "f1_at_f": np.ascontiguousarray(np.asarray(inputs["L1_ffn_A"], np.float32).reshape(128, D).T),
        "f1_wg_f": np.ascontiguousarray(np.asarray(inputs["L1_ffn_W_gate"], np.float32).T),
        "f1_bf_f": np.ascontiguousarray(np.asarray(inputs["L1_ffn_B"], np.float32).transpose(0, 2, 1).reshape(128, D)),
    }
    pres = run_bass_kernel_spmd(ncp, [prep_in], core_ids=[0])
    W = pres.results[0]

    base_all = {
        "f0_bias": vec128(inputs["L0_ffn_bias"]),
        "f0_scale": vec128(inputs["L0_ffn_scale"]),
        "f0_gamma": vec128(inputs["L0_ffn_ln_gamma"]),
        "f0_beta": vec128(inputs["L0_ffn_ln_beta"]),
        "f0_qs": np.asarray(inputs["L0_ffn_quant_scale"], np.float32).reshape(1, 1),
        "f1_bbase": vec128(inputs["L1_ffn_b_base"]),
        "l1_qkvb": np.ascontiguousarray(np.asarray(inputs["L1_attn_qkv_b"], np.float32).reshape(1, 4608)),
    }
    base_all.update({k: W[k] for k in W})
    vones = [_vones_for_core(c) for c in range(NC)]
    ropes = [(np.ascontiguousarray(cos[_positions_of_core(c)]),
              np.ascontiguousarray(sin[_positions_of_core(c)])) for c in range(NC)]

    groups = [("g_ffn0", ("ffn0",)), ("g_pre0", ("att_pre0",)), ("g_fl0", ("att_flash0",)),
              ("g_ffn0", ("ffn0",)), ("g_ffn1", ("ffn1",)), ("g_pre1", ("att_pre1",)),
              ("g_fl1", ("att_flash1",)), ("g_ffn1", ("ffn1",))]

    cur = xs
    kv = {}
    for key, phases in groups:
        mk, nc = _get_prog(key, phases, fused=False)
        names = set(mk.t.keys())
        in_maps = []
        for c in range(NC):
            im = {"x_own": cur[c]}
            for nm in names:
                kind = mk.t[nm][2]
                if kind != "ExternalInput" or nm == "x_own":
                    continue
                if nm in ("qd", "k_own", "v_own", "n_own", "qn_st"):
                    im[nm] = kv[nm][c]
                elif nm in ("k_all", "v_all", "n_all"):
                    grp = range(4 * (c // 4), 4 * (c // 4) + 4)
                    src = {"k_all": "k_own", "v_all": "v_own", "n_all": "n_own"}[nm]
                    im[nm] = np.ascontiguousarray(np.concatenate([kv[src][r] for r in grp], axis=0))
                elif nm == "vones":
                    im[nm] = vones[c]
                elif nm == "rope_cos":
                    im[nm] = ropes[c][0]
                elif nm == "rope_sin":
                    im[nm] = ropes[c][1]
                else:
                    im[nm] = base_all[nm]
            in_maps.append(im)
        res = run_bass_kernel_spmd(nc, in_maps, core_ids=list(range(NC)))
        cur = [res.results[c]["x_out"] for c in range(NC)]
        for nm in ("qd", "k_own", "v_own", "n_own", "qn_st"):
            if nm in names and mk.t[nm][2] == "ExternalOutput":
                kv[nm] = [res.results[c][nm] for c in range(NC)]
    return _host_unshard_x(cur)
